# version 01
# baseline (speedup 1.0000x reference)
import math
import numpy as np
import concourse.bass as bass
import concourse.mybir as mybir
from concourse.bass_utils import run_bass_kernel_spmd

F32 = mybir.dt.float32
BF16 = mybir.dt.bfloat16
I32 = mybir.dt.int32
AF = mybir.ActivationFunctionType
ALU = mybir.AluOpType
AX = mybir.AxisListType

NCORES = 8
NSEQ = 4
T = 2048
D = 2048
NT = 8
TOK = NT * 128
NTA = 16
EPS = 1e-5

ENGINES = ("tensor", "vector", "scalar", "gpsimd", "sync")
DMA_QUEUES = ("sync", "gpsimd", "scalar")


class Op:
    __slots__ = ("engine", "fn", "ins", "outs", "is_dma", "sem", "seq",
                 "marked", "target", "waits", "tag", "deps")

    def __init__(self, engine, fn, ins, outs, is_dma, sem, seq, tag):
        self.engine = engine
        self.fn = fn
        self.ins = ins
        self.outs = outs
        self.is_dma = is_dma
        self.sem = sem
        self.seq = seq
        self.marked = False
        self.target = None
        self.waits = []
        self.tag = tag


class Prog:
    def __init__(self, nc, n_dma_sems=10):
        self.nc = nc
        self.ops = []
        self.last_writer = {}
        self.readers = {}
        self.key_sem = {}
        self.dma_sem_names = {}
        self.sem_handles = {}
        self._sem_ctxs = []
        self.engine_sem = {e: f"es_{e}" for e in ENGINES}
        self._dma_pools = {q: [f"ds_{q}_{i}" for i in range(n_dma_sems)]
                           for q in DMA_QUEUES}
        self._dma_rr = {q: 0 for q in DMA_QUEUES}
        self._outstanding = {}
        self._cc_idx = 0

    def op(self, engine, fn, ins=(), outs=(), tag=""):
        o = Op(engine, fn, tuple(ins), tuple(outs), False, None,
               len(self.ops), tag)
        self._link(o)
        self.ops.append(o)
        return o

    def dma(self, queue, fn, ins=(), outs=(), tag=""):
        assert queue in DMA_QUEUES
        key = outs[0] if outs else ("dmaout", len(self.ops))
        sem = self.key_sem.get(key)
        if sem is None:
            pool = self._dma_pools[queue]
            sem = pool[self._dma_rr[queue] % len(pool)]
            self._dma_rr[queue] += 1
            self.key_sem[key] = sem
        o = Op(queue, fn, tuple(ins), tuple(outs), True, sem,
               len(self.ops), tag)
        prev = self._outstanding.get(sem)
        if prev is not None and prev.outs and prev.outs[0] != key:
            o.ins = o.ins + prev.outs
        self._link(o)
        self._outstanding[sem] = o
        self.ops.append(o)
        return o

    def collective(self, fn, ins=(), outs=(), tag="cc"):
        sem = f"cc_{self._cc_idx}"
        self._cc_idx += 1
        o = Op("gpsimd", fn, tuple(ins), tuple(outs), False, sem,
               len(self.ops), tag)
        self._link(o)
        self.ops.append(o)
        return o

    def _link(self, o):
        deps = set()
        for k in o.ins:
            w = self.last_writer.get(k)
            if w is not None:
                deps.add(w)
            self.readers.setdefault(k, [])
        for k in o.outs:
            w = self.last_writer.get(k)
            if w is not None:
                deps.add(w)
            for r in self.readers.get(k, ()):
                deps.add(r)
        deps.discard(o)
        o.deps = deps
        for k in o.ins:
            self.readers.setdefault(k, []).append(o)
        for k in o.outs:
            self.last_writer[k] = o
            self.readers[k] = []

    def finalize(self):
        for o in self.ops:
            for d in o.deps:
                d.marked = True
        counts = {}
        for o in self.ops:
            if o.is_dma or o.sem is not None:
                o.marked = True
            if o.marked:
                sem = o.sem if o.sem is not None else self.engine_sem[o.engine]
                inc = 16 if o.is_dma else 1
                counts[sem] = counts.get(sem, 0) + inc
                o.target = (sem, counts[sem], inc)
        watermark = {}
        engine_pos = {e: 0 for e in ENGINES}
        for o in self.ops:
            ws = {}
            for d in o.deps:
                if d.target is None:
                    continue
                sem, val, _ = d.target
                if ws.get(sem, -1) < val:
                    ws[sem] = val
            for sem, val in ws.items():
                wm = watermark.get((o.engine, sem), -1)
                if val > wm:
                    o.waits.append((sem, val))
                    watermark[(o.engine, sem)] = val
        self.final_waits = []

    def sems_needed(self):
        s = set()
        for o in self.ops:
            if o.target is not None:
                s.add(o.target[0])
            for sem, _ in o.waits:
                s.add(sem)
        for sem, _ in self.final_waits:
            s.add(sem)
        return sorted(s)

    def replay(self, block):
        streams = {e: [] for e in ENGINES}
        for o in self.ops:
            streams[o.engine].append(o)

        def mk(engine, ops):
            def run(eng):
                for o in ops:
                    for sem, val in o.waits:
                        eng.wait_ge(self.sem_handles[sem], val)
                    inst = o.fn(eng)
                    if o.target is not None:
                        sem, val, inc = o.target
                        if inst is None:
                            raise RuntimeError(f"op {o.tag} returned no inst")
                        inst.then_inc(self.sem_handles[sem], inc)
                if engine == "sync":
                    for sem, val in self.final_waits:
                        eng.wait_ge(self.sem_handles[sem], val)
            return run

        for e in ENGINES:
            getattr(block, e)(mk(e, streams[e]))


class Mem:

    def __init__(self, nc, stack):
        self.nc = nc
        self.stack = stack
        self.t = {}

    def sbuf(self, name, free_elems, dtype):
        h = self.stack.enter_context(self.nc.sbuf_tensor("sb_" + name, [128, free_elems], dtype))
        self.t[name] = h
        return h

    def psum(self, name, free_elems, dtype=F32):
        h = self.stack.enter_context(self.nc.psum_tensor(name, [128, free_elems], dtype))
        self.t[name] = h
        return h

    def __getitem__(self, name):
        return self.t[name]


def _interleave_index(core):
    b = core // 2
    par = core % 2
    tiles = [b * T + (2 * j + par) * 128 for j in range(NT)]
    idx = np.concatenate([np.arange(t0, t0 + 128) for t0 in tiles])
    return idx


def host_prepare(inputs):
    xs_full = np.ascontiguousarray(np.asarray(inputs["L0_attn_x"], np.float32).reshape(NSEQ * T, D))

    w = {}
    w["wq"] = np.ascontiguousarray(np.asarray(inputs["L0_attn_Wq"], np.float32))
    w["wk"] = np.ascontiguousarray(np.asarray(inputs["L0_attn_Wk"], np.float32))
    w["wv"] = np.ascontiguousarray(np.asarray(inputs["L0_attn_Wv"], np.float32))
    w["wo"] = np.ascontiguousarray(np.asarray(inputs["L0_attn_Wo"], np.float32))
    w["sim"] = np.ascontiguousarray(np.asarray(inputs["L0_ffn_sim_matrix"], np.float32))
    w["gates"] = np.ascontiguousarray(np.asarray(inputs["L0_ffn_gates"], np.float32).reshape(1, 16))
    for nm, src in (("qcat", "L0_ffn_q_proj"), ("kcat", "L0_ffn_k_proj"), ("vcat", "L0_ffn_v_proj")):
        a = np.asarray(inputs[src], np.float32)
        w[nm] = np.ascontiguousarray(a.transpose(1, 0, 2).reshape(D, 16 * 128))
    w["ocat"] = np.ascontiguousarray(np.asarray(inputs["L0_ffn_o_proj"], np.float32).reshape(16 * 128, D))
    w["wgT"] = np.ascontiguousarray(np.asarray(inputs["L1_ffn_w_gate"], np.float32).T)
    w["wuT"] = np.ascontiguousarray(np.asarray(inputs["L1_ffn_w_up"], np.float32).T)
    w["wdT"] = np.ascontiguousarray(np.asarray(inputs["L1_ffn_w_down"], np.float32).T)
    win = np.asarray(inputs["L1_attn_W_in"], np.float32)
    winp = np.zeros((D, 8320), np.float32)
    winp[:, 0:8192] = win[:, 0:8192]
    winp[:, 8192:8208] = win[:, 8192:8208]
    winp[:, 8224:8240] = win[:, 8208:8224]
    winp[:, 8256:8320] = win[:, 8224:8288]
    w["winp"] = np.ascontiguousarray(winp)
    w["wout"] = np.ascontiguousarray(np.asarray(inputs["L1_attn_W_out"], np.float32))
    convw = np.asarray(inputs["L1_attn_conv_w"], np.float32)
    convb = np.asarray(inputs["L1_attn_conv_b"], np.float32)
    mcwx = np.zeros((128, 128), np.float32)
    mcbx = np.zeros((128, 32), np.float32)
    for mc in range(32):
        mcwx[:, mc * 4:(mc + 1) * 4] = convw[mc * 128:(mc + 1) * 128, :]
        mcbx[:, mc] = convb[mc * 128:(mc + 1) * 128]
    w["mcwx"] = np.ascontiguousarray(mcwx)
    w["mcbx"] = np.ascontiguousarray(mcbx)
    mcwbc = np.zeros((128, 4), np.float32)
    mcbbc = np.zeros((128, 1), np.float32)
    mcwbc[0:16] = convw[4096:4112]
    mcwbc[32:48] = convw[4112:4128]
    mcbbc[0:16, 0] = convb[4096:4112]
    mcbbc[32:48, 0] = convb[4112:4128]
    w["mcwbc"] = np.ascontiguousarray(mcwbc)
    w["mcbbc"] = np.ascontiguousarray(mcbbc)
    w["mnw"] = np.ascontiguousarray(
        np.asarray(inputs["L1_attn_norm_w"], np.float32).reshape(32, 128).T)
    w["alog"] = np.ascontiguousarray(np.asarray(inputs["L1_attn_A_log"], np.float32).reshape(64, 1))
    w["dtbias"] = np.ascontiguousarray(np.asarray(inputs["L1_attn_dt_bias"], np.float32).reshape(64, 1))
    w["mrow"] = np.ascontiguousarray(np.asarray(inputs["L1_attn_D_skip"], np.float32).reshape(1, 64))

    consts = {}
    consts["ident"] = np.eye(128, dtype=np.float32)
    consts["ones_col"] = np.ones((128, 1), np.float32)
    consts["ones_row"] = np.ones((1, 128), np.float32)
    consts["cums"] = np.triu(np.ones((128, 128), np.float32))
    consts["trineg"] = np.where(np.triu(np.ones((128, 128), bool)), 0.0, -1e9).astype(np.float32)

    oh = np.zeros((128, 16 * 16), np.float32)
    for sg in range(16):
        oh[:, sg * 16 + sg] = 1.0
    consts["onehot"] = oh
    ss = np.zeros((16, 8), np.float32)
    for sg in range(16):
        for j in range(8):
            if sg >= 2 * j + 2:
                ss[sg, j] = 1.0
    consts["suffsel"] = ss

    tri_mult = np.triu(np.ones((128, 128), np.float32))
    ones_m = np.ones((128, 128), np.float32)
    zeros_m = np.zeros((128, 128), np.float32)
    tri_add = np.where(tri_mult > 0, 0.0, -1e9).astype(np.float32)
    zeros_a = np.zeros((128, 128), np.float32)
    full_a = np.full((128, 128), -1e9, np.float32)

    in_maps = []
    for c in range(NCORES):
        par = c % 2
        m = {"xs": np.ascontiguousarray(xs_full[_interleave_index(c)])}
        m.update(w)
        m.update(consts)
        m["maskA_mult"] = tri_mult if par == 0 else ones_m
        m["maskB_mult"] = zeros_m if par == 0 else tri_mult
        m["maskA_add"] = tri_add if par == 0 else zeros_a
        m["maskB_add"] = full_a if par == 0 else tri_add
        m["coreinfo"] = np.array([[2 * (c // 2), 2 * (c // 2) + 1]], np.int32)
        tw = np.zeros((1, 16), np.float32)
        for j in range(8):
            tw[0, 2 * j] = 1.0 if par == 1 else 0.0
            tw[0, 2 * j + 1] = 1.0 if (par == 0 and j > 0) else 0.0
        m["tailw_bc"] = np.ascontiguousarray(np.broadcast_to(tw, (128, 16)))
        pw = np.array([[1.0 - par, float(par)]], np.float32)
        m["parw_bc"] = np.ascontiguousarray(np.broadcast_to(pw, (128, 2)))
        in_maps.append(m)
    return in_maps


SQ_T = 1.0 / math.sqrt(2048.0)
SQ_H = 1.0 / math.sqrt(128.0)

INPUT_SPECS = [
    ("xs", [TOK, D]), ("wq", [D, D]), ("wk", [D, D]), ("wv", [D, D]),
    ("wo", [D, D]), ("sim", [D, 16]), ("gates", [1, 16]),
    ("qcat", [D, D]), ("kcat", [D, D]), ("vcat", [D, D]), ("ocat", [D, D]),
    ("wgT", [D, 8192]), ("wuT", [D, 8192]), ("wdT", [8192, D]),
    ("winp", [D, 8320]), ("wout", [4096, D]),
    ("mcwx", [128, 128]), ("mcwbc", [128, 4]), ("mcbx", [128, 32]),
    ("mcbbc", [128, 1]), ("mnw", [128, 32]),
    ("alog", [64, 1]), ("dtbias", [64, 1]), ("mrow", [1, 64]),
    ("tailw_bc", [128, 16]), ("parw_bc", [128, 2]),
    ("ident", [128, 128]), ("ones_col", [128, 1]), ("ones_row", [1, 128]),
    ("cums", [128, 128]), ("trineg", [128, 128]),
    ("maskA_mult", [128, 128]), ("maskB_mult", [128, 128]),
    ("maskA_add", [128, 128]), ("maskB_add", [128, 128]),
    ("onehot", [128, 256]), ("suffsel", [16, 8]),
]


class Ker:

    def __init__(self, chain):
        self.chain = chain
        self.nc = bass.Bass("TRN2", target_bir_lowering=False, debug=False)
        self.dram = {}
        for name, shape in INPUT_SPECS:
            self.dram[name] = self.nc.dram_tensor(name, shape, F32, kind="ExternalInput")
        self.dram["coreinfo"] = self.nc.dram_tensor("coreinfo", [1, 2], I32, kind="ExternalInput")
        self.dram["out"] = self.nc.dram_tensor("out", [TOK, D], F32, kind="ExternalOutput")
        self.dram["dbg"] = self.nc.dram_tensor("dbg", [128, 16384], F32, kind="ExternalOutput")
        self.dram["dbg2"] = self.nc.dram_tensor("dbg2", [128, 512], F32, kind="ExternalOutput")
        self.dram["dbgb"] = self.nc.dram_tensor("dbgb", [128, 16384], BF16, kind="ExternalOutput")
        self.dram["agin"] = self.nc.dram_tensor("agin", [2, 2097152], BF16, kind="Internal")
        self.dram["agout"] = self.nc.dram_tensor("agout", [2, 2 * 2097152], BF16, kind="Internal")
        self.dram["suffdram"] = self.nc.dram_tensor("suffdram", [8, 2048], BF16, kind="Internal")
        self.dram["agin_s"] = self.nc.dram_tensor("agin_s", [262144], BF16, kind="Internal")
        self.dram["agout_s"] = self.nc.dram_tensor("agout_s", [2, 262144], BF16, kind="Internal")
        self.dram["agin_t"] = self.nc.dram_tensor("agin_t", [67584], BF16, kind="Internal")
        self.dram["agout_tA"] = self.nc.dram_tensor("agout_tA", [135168], BF16, kind="Internal")
        self.dram["agout_tB"] = self.nc.dram_tensor("agout_tB", [135168], BF16, kind="Internal")
        self.dram["agin_m"] = self.nc.dram_tensor("agin_m", [4, 65600], F32, kind="Internal")
        self.dram["agout_mA"] = self.nc.dram_tensor("agout_mA", [524800], F32, kind="Internal")
        self.dram["agout_mB"] = self.nc.dram_tensor("agout_mB", [524800], F32, kind="Internal")
        self.dram["msdram"] = self.nc.dram_tensor("msdram", [16, 4096], F32, kind="Internal")
        self.dram["gdram"] = self.nc.dram_tensor("gdram", [8, 128, 4096], BF16, kind="Internal")
        self.prog = Prog(self.nc)
        self.uid = 0

    def fresh(self, base):
        self.uid += 1
        return f"{base}{self.uid}"


class PsumGroup:

    def __init__(self):
        self.started = set()
        self.last = {}

    def next_flags(self, bank=0):
        f = {"start": bank not in self.started, "stop": False}
        self.started.add(bank)
        self.last[bank] = f
        return f

    def close(self):
        assert self.last, "empty psum group"
        for f in self.last.values():
            f["stop"] = True


def build(chain=("l0ffn", "l0attn", "l0ffn2", "l1ffn", "l1attn", "l1ffn2")):
    import contextlib
    K = Ker(chain)
    nc, prog = K.nc, K.prog
    with contextlib.ExitStack() as stack:
        mem = Mem(nc, stack)
        K.mem = mem

        mem.sbuf("x", NT * D, F32)
        mem.sbuf("p1", 16384, BF16)
        mem.sbuf("p2", 32768, BF16)
        mem.sbuf("p3", 8192, BF16)
        mem.sbuf("wb", 8192, BF16)
        mem.sbuf("small", 2304, F32)
        mem.sbuf("stat", 512, F32)
        for cn, fe, dt in (("ident", 128, F32), ("identb", 128, BF16),
                           ("ones_col", 1, F32),
                           ("trineg", 128, F32),
                           ("mAm", 128, BF16), ("mBm", 128, BF16),
                           ("mAa", 128, BF16), ("mBa", 128, BF16)):
            mem.sbuf(cn, fe, dt)
        mem.sbuf("ones_row", 128, F32)
        mem.sbuf("ones_rowb", 128, BF16)
        mem.sbuf("ones_colb", 1, BF16)
        mem.sbuf("suffsel", 8, BF16)
        mem.sbuf("cinfo", 2, I32)

        mem.sbuf("neg16", 16, F32)
        mem.sbuf("sgbc", 16, F32)
        mem.sbuf("wrtb", 128, BF16)
        mem.sbuf("tailw_bc", 16, F32)
        mem.sbuf("parw_bc", 2, F32)
        mem.sbuf("mabd", 4, F32)
        mem.sbuf("mcwx", 128, F32)
        mem.sbuf("mcwbc", 4, F32)
        mem.sbuf("mcbx", 32, F32)
        mem.sbuf("mcbbc", 1, F32)
        mem.sbuf("mnw", 32, F32)
        mem.sbuf("mrow", 64, F32)

        mem.psum("pbig", 2048)
        mem.psum("pa", 512)
        mem.psum("pb", 512)
        mem.psum("pc", 512)
        mem.psum("pd", 512)

        build_body(K)

        prog.finalize()
        prog.final_waits = [(o.target[0], o.target[1]) for o in K.final_wait_ops]
        sem_names = prog.sems_needed()
        with contextlib.ExitStack() as sstack:
            for sn in sem_names:
                prog.sem_handles[sn] = sstack.enter_context(nc.semaphore(sn))
            with nc.Block() as block:
                prog.replay(block)
    return nc


def load_consts(K):
    p, m, d = K.prog, K.mem, K.dram
    pairs = [("ident", "ident", F32), ("identb", "ident", BF16),
             ("ones_col", "ones_col", F32),
             ("trineg", "trineg", F32),
             ("mAm", "maskA_mult", BF16), ("mBm", "maskB_mult", BF16),
             ("mAa", "maskA_add", BF16), ("mBa", "maskB_add", BF16)]
    for dst, src, dt in pairs:
        ap = d[src].ap()
        queue = "gpsimd" if dt == BF16 else "sync"
        p.dma(queue, lambda e, dst=dst, ap=ap: e.dma_start(out=K.mem[dst][:, :], in_=ap),
              outs=[("c", dst)], tag=f"ld_{dst}")
    p.dma("sync", lambda e: e.dma_start(out=K.mem["ones_row"][0:1, :], in_=d["ones_row"].ap()),
          outs=[("c", "ones_row")], tag="ld_ones_row")
    for j in range(NT):
        p.dma("sync",
              lambda e, j=j: e.dma_start(out=K.mem["x"][:, j * D:(j + 1) * D],
                                         in_=d["xs"].ap()[j * 128:(j + 1) * 128, :]),
              outs=[("x", j)], tag=f"ld_x{j}")


def xtile(K, j):
    return K.mem["x"][:, j * D:(j + 1) * D]


def emit_ln(K, dst_htm=True, tag="ln", tiles=None, colbase=0):
    p, m = K.prog, K.mem
    if tiles is None:
        tiles = range(NT)
    for j in tiles:
        x = xtile(K, j)
        st = K.fresh("lnst")
        def f_stats(e, x=x, j=j):
            r = None
            for k in range(4):
                r = e.bn_stats(m["small"][:, j * 32 + k * 6: j * 32 + (k + 1) * 6],
                               x[:, k * 512:(k + 1) * 512])
            return r
        p.op("vector", f_stats, ins=[("x", j)], outs=[("small", 0)], tag=f"{tag}_bn{j}")
        p.op("vector",
             lambda e, j=j: e.bn_aggr(m["stat"][:, j * 2:j * 2 + 2],
                                      m["small"][:, j * 32:j * 32 + 24]),
             ins=[("small", 0)], outs=[("mv", j)], tag=f"{tag}_ag{j}")
        p.op("vector",
             lambda e, j=j: e.tensor_scalar(out=m["stat"][:, 256 + j:256 + j + 1],
                                            in0=m["stat"][:, j * 2 + 1:j * 2 + 2],
                                            scalar1=float(EPS), scalar2=None,
                                            op0=ALU.add),
             ins=[("mv", j)], outs=[("vpe", j)], tag=f"{tag}_ve{j}")
        p.op("scalar",
             lambda e, j=j: e.activation(m["stat"][:, 272 + j:272 + j + 1],
                                         m["stat"][:, 256 + j:256 + j + 1],
                                         AF.Ln),
             ins=[("vpe", j)], outs=[("lnv", j)], tag=f"{tag}_ln{j}")
        p.op("scalar",
             lambda e, j=j: e.activation(m["stat"][:, 288 + j:288 + j + 1],
                                         m["stat"][:, 272 + j:272 + j + 1],
                                         AF.Exp, scale=-0.5),
             ins=[("lnv", j)], outs=[("rstd", j)], tag=f"{tag}_rs{j}")
        p.op("vector",
             lambda e, j=j: e.tensor_tensor(m["stat"][:, 304 + j:304 + j + 1],
                                            m["stat"][:, j * 2:j * 2 + 1],
                                            m["stat"][:, 288 + j:288 + j + 1],
                                            ALU.mult),
             ins=[("rstd", j), ("mv", j)], outs=[("mrs", j)], tag=f"{tag}_mr{j}")
        p.op("vector",
             lambda e, j=j: e.tensor_scalar(out=m["stat"][:, 304 + j:304 + j + 1],
                                            in0=m["stat"][:, 304 + j:304 + j + 1],
                                            scalar1=-1.0, scalar2=None, op0=ALU.mult),
             ins=[("mrs", j)], outs=[("nmrs", j)], tag=f"{tag}_nm{j}")
        p.op("scalar",
             lambda e, j=j, colbase=colbase: e.activation(
                 m["p1"][:, colbase + j * D: colbase + (j + 1) * D], xtile(K, j),
                 AF.Identity,
                 bias=m["stat"][:, 304 + j:304 + j + 1],
                 scale=m["stat"][:, 288 + j:288 + j + 1]),
             ins=[("x", j), ("rstd", j), ("nmrs", j)],
             outs=[("p1", (colbase + j * D) // D)],
             tag=f"{tag}_h{j}")


def emit_hfm(K, groups=(0, 1), tag="hfm", src_colbase=0):
    p, m = K.prog, K.mem
    for g in groups:
        for jj in range(4):
            j = 4 * g + jj
            def ft(e, j=j, jj=jj, src_colbase=src_colbase):
                src = m["p1"][:, src_colbase + j * D: src_colbase + (j + 1) * D]
                dst = bass.AP(m.t["p3"], jj * 128,
                              [[8192, 128], [512, 16], [1, 128]])
                return e.dma_start_transpose(dst, src)
            p.dma("sync", ft, ins=[("p1", (src_colbase + j * D) // D)],
                  outs=[("p3", ci) for ci in range(16)], tag=f"{tag}{j}")


def emit_rank_regs(K):
    pass


def _stream_lhsT(K, queue, wname, row_stride, col0, nk, dst_ap, key, tag):
    p = K.prog

    def fw(e):
        src = bass.AP(K.dram[wname], col0,
                      [[row_stride, 128], [128 * row_stride, nk], [1, 128]])
        return e.dma_start(out=dst_ap, in_=src)
    return p.dma(queue, fw, outs=[key], tag=tag)


DBG = False


def store_out(K):
    p, m, d = K.prog, K.mem, K.dram
    waits = []
    if DBG:
        pass
    for j in range(NT):
        o = p.dma("sync",
                  lambda e, j=j: e.dma_start(out=d["out"].ap()[j * 128:(j + 1) * 128, :],
                                             in_=xtile(K, j)),
                  ins=[("x", j)], outs=[("outj", j)], tag=f"st_out{j}")
        waits.append(o)
    K.final_wait_ops = waits


def emit_l1ffn(K, site):
    p, m = K.prog, K.mem
    S = f"f{site}_"
    emit_ln(K, tag=S + "ln")
    for g in range(2):
        emit_hfm(K, groups=(g,), tag=S + f"hfm{g}_")
        for hc in range(64):
            slot = hc % 2
            gb = m["wb"][:, slot * 4096: slot * 4096 + 2048]
            ub = m["wb"][:, slot * 4096 + 2048: slot * 4096 + 4096]
            for nm, dst, wname in (("wg", gb, "wgT"), ("wu", ub, "wuT")):
                def fw(e, dst=dst, wname=wname, hc=hc):
                    src = bass.AP(K.dram[wname], hc * 128,
                                  [[8192, 128], [128 * 8192, 16], [1, 128]])
                    return e.dma_start(out=dst, in_=src)
                wkey = ("wb", slot * 2 + (0 if nm == "wg" else 1))
                p.dma("gpsimd", fw, outs=[wkey], ins=(), tag=S + f"ldw_{nm}{hc}")
            ps_g = "pa" if slot == 0 else "pc"
            ps_u = "pb" if slot == 0 else "pd"
            grp_g, grp_u = PsumGroup(), PsumGroup()
            for kc in range(16):
                fg = grp_g.next_flags()
                p.op("tensor",
                     lambda e, kc=kc, gb=gb, ps=ps_g, fg=fg:
                     e.matmul(m[ps][:, :], gb[:, kc * 128:(kc + 1) * 128],
                              m["p3"][:, kc * 512:(kc + 1) * 512],
                              start=fg["start"], stop=fg["stop"]),
                     ins=[("wb", slot * 2), ("p3", kc)],
                     outs=[(ps_g,)], tag=S + f"mm_g{hc}_{kc}")
            grp_g.close()
            for kc in range(16):
                fu = grp_u.next_flags()
                p.op("tensor",
                     lambda e, kc=kc, ub=ub, ps=ps_u, fu=fu:
                     e.matmul(m[ps][:, :], ub[:, kc * 128:(kc + 1) * 128],
                              m["p3"][:, kc * 512:(kc + 1) * 512],
                              start=fu["start"], stop=fu["stop"]),
                     ins=[("wb", slot * 2 + 1), ("p3", kc)],
                     outs=[(ps_u,)], tag=S + f"mm_u{hc}_{kc}")
            grp_u.close()
            sc = 512 + slot * 512
            skeys = [("small", 2 + 2 * slot), ("small", 3 + 2 * slot)]
            p.op("scalar",
                 lambda e, ps=ps_g, sc=sc: e.activation(m["small"][:, sc:sc + 512],
                                                        m[ps][:, :], AF.Silu),
                 ins=[(ps_g,)], outs=skeys, tag=S + f"silu{hc}")
            p.op("vector",
                 lambda e, ps=ps_u, hc=hc, sc=sc: e.tensor_tensor(
                     m["p2"][:, hc * 512:(hc + 1) * 512],
                     m["small"][:, sc:sc + 512], m[ps][:, :], ALU.mult),
                 ins=skeys + [(ps_u,)], outs=[("p2", hc)],
                 tag=S + f"gated{hc}")
        for cc in range(16):
            ps_o = "pa" if cc % 2 == 0 else "pb"
            grp = PsumGroup()
            for half in range(2):
                slot = half
                wdb = m["wb"][:, slot * 4096: (slot + 1) * 4096]
                def fw(e, wdb=wdb, cc=cc, half=half):
                    src = bass.AP(K.dram["wdT"], half * 32 * 128 * 2048 + cc * 128,
                                  [[2048, 128], [128 * 2048, 32], [1, 128]])
                    return e.dma_start(out=wdb, in_=src)
                p.dma("gpsimd", fw,
                      outs=[("wb", slot * 2), ("wb", slot * 2 + 1)],
                      tag=S + f"ldwd{cc}_{half}")
                for hh in range(32):
                    hc = half * 32 + hh
                    fl = grp.next_flags()
                    p.op("tensor",
                         lambda e, hh=hh, hc=hc, wdb=wdb, ps=ps_o, fl=fl:
                         e.matmul(m[ps][:, :], wdb[:, hh * 128:(hh + 1) * 128],
                                  m["p2"][:, hc * 512:(hc + 1) * 512],
                                  start=fl["start"], stop=fl["stop"]),
                         ins=[("wb", slot * 2), ("wb", slot * 2 + 1), ("p2", hc)],
                         outs=[(ps_o,)], tag=S + f"mm_d{cc}_{hc}")
            grp.close()
            p.op("scalar",
                 lambda e, ps=ps_o, cc=cc: e.activation(
                     m["p3"][:, cc * 512:(cc + 1) * 512], m[ps][:, :], AF.Identity),
                 ins=[(ps_o,)], outs=[("p3", cc)], tag=S + f"evac_d{cc}")
        for jj in range(4):
            j = 4 * g + jj
            for cb in range(4):
                for ci4 in range(4):
                    ci = cb * 4 + ci4
                    def ftr(e, ci=ci, ci4=ci4, jj=jj):
                        pdb = m["pd"][:, :].bitcast(BF16)
                        return e.transpose(pdb[:, ci4 * 128:(ci4 + 1) * 128],
                                           m["p3"][:, ci * 512 + jj * 128: ci * 512 + (jj + 1) * 128],
                                           m["identb"][:, :])
                    p.op("tensor", ftr, ins=[("p3", ci), ("c", "identb")],
                         outs=[("pd",)], tag=S + f"tr{j}_{cb}_{ci4}")
                p.op("vector",
                     lambda e, j=j, cb=cb: e.scalar_tensor_tensor(
                         out=xtile(K, j)[:, cb * 512:(cb + 1) * 512],
                         in0=m["pd"][:, :].bitcast(BF16)[:, 0:512], scalar=0.5,
                         in1=xtile(K, j)[:, cb * 512:(cb + 1) * 512],
                         op0=ALU.mult, op1=ALU.add),
                     ins=[("pd",), ("x", j)], outs=[("x", j)],
                     tag=S + f"res{j}_{cb}")


L0A_PHASES = ("proj", "ag", "vsum", "attn", "wo")


def emit_l0attn(K, site):
    p, m, d = K.prog, K.mem, K.dram
    S = f"a{site}_"
    emit_ln(K, tag=S + "ln")

    p.op("gpsimd", lambda e: e.memset(m["ones_rowb"][:, :], 1.0),
         outs=[("ones_rowb",)], tag=S + "orb")
    p.op("gpsimd", lambda e: e.memset(m["ones_colb"][:, :], 1.0),
         outs=[("ones_colb",)], tag=S + "ocb")

    for g in range(2) if "proj" in L0A_PHASES else ():
        emit_hfm(K, groups=(g,), tag=S + f"hfm{g}_")
        for nm, wname in (("q", "wq"), ("k", "wk")):
            for dhc in range(16):
                slot = dhc % 2
                wk_key = ("wb", slot)
                _stream_lhsT(K, "gpsimd", wname, 2048, dhc * 128, 16,
                             m["wb"][:, slot * 2048:(slot + 1) * 2048],
                             wk_key, S + f"ldw{nm}{g}_{dhc}")
                ps = "pa" if slot == 0 else "pb"
                grp = PsumGroup()
                for kc in range(16):
                    fl = grp.next_flags()
                    p.op("tensor",
                         lambda e, kc=kc, slot=slot, ps=ps, fl=fl:
                         e.matmul(m[ps][:, :],
                                  m["wb"][:, slot * 2048 + kc * 128: slot * 2048 + (kc + 1) * 128],
                                  m["p3"][:, kc * 512:(kc + 1) * 512],
                                  start=fl["start"], stop=fl["stop"]),
                         ins=[wk_key, ("p3", kc)], outs=[(ps,)],
                         tag=S + f"mm{nm}{g}_{dhc}_{kc}")
                grp.close()
                off = (0 if nm == "q" else 16384) + dhc * 1024 + g * 512
                dstk = ("p2", (0 if nm == "q" else 32) + 2 * dhc + g)
                p.op("scalar",
                     lambda e, ps=ps, off=off: e.activation(
                         m["p2"][:, off:off + 512], m[ps][:, :], AF.Identity),
                     ins=[(ps,)], outs=[dstk], tag=S + f"ev{nm}{g}_{dhc}")
        def fka(e, g=g):
            src = bass.AP(m.t["p2"], 16384 + g * 512,
                          [[32768, 128], [1024, 16], [1, 512]])
            dst = d["agin"].ap()[0:1, :].squeeze(0).rearrange(
                "(a b c) -> a b c", b=128, c=1024)[:, :, g * 512:(g + 1) * 512] \
                .transpose([1, 0, 2])
            return e.dma_start(out=dst, in_=src)
        p.dma("sync", fka, ins=[("p2", 32 + i) for i in range(32)],
              outs=[("aginK",)], tag=S + f"kag{g}")
        for tt in range(4):
            slot = tt % 2
            grp = PsumGroup()
            for kc in range(16):
                def fwv(e, kc=kc, slot=slot):
                    src = bass.AP(d["wv"], kc * 128 * 2048, [[2048, 128], [1, 2048]])
                    return e.dma_start(out=m["wb"][:, (2 + slot) * 2048:(3 + slot) * 2048], in_=src)
                p.dma("gpsimd", fwv, outs=[("wb", 2 + slot)], tag=S + f"lwv{g}_{tt}_{kc}")
                for nn in range(4):
                    fl = grp.next_flags(nn)
                    p.op("tensor",
                         lambda e, kc=kc, tt=tt, nn=nn, slot=slot, fl=fl:
                         e.matmul(m["pbig"][:, nn * 512:(nn + 1) * 512],
                                  m["p3"][:, kc * 512 + tt * 128: kc * 512 + tt * 128 + 128],
                                  m["wb"][:, (2 + slot) * 2048 + nn * 512:(2 + slot) * 2048 + (nn + 1) * 512],
                                  start=fl["start"], stop=fl["stop"]),
                         ins=[("wb", 2 + slot), ("p3", kc)], outs=[("pbig",)],
                         tag=S + f"mmv{g}_{tt}_{kc}_{nn}")
            grp.close()
            j = 4 * g + tt
            p.op("scalar",
                 lambda e, j=j: e.activation(m["p1"][:, j * D:(j + 1) * D],
                                             m["pbig"][:, :], AF.Identity),
                 ins=[("pbig",)], outs=[("p1", j)], tag=S + f"evv{g}_{tt}")
            def fva(e, j=j):
                dst = d["agin"].ap()[1:2, :].squeeze(0).rearrange(
                    "(a b c) -> a b c", b=128, c=2048)[j:j + 1].squeeze(0)
                return e.dma_start(out=dst, in_=m["p1"][:, j * D:(j + 1) * D])
            p.dma("sync", fva, ins=[("p1", j)], outs=[("aginV", j)], tag=S + f"vag{g}_{tt}")

    if "ag" not in L0A_PHASES:
        return
    p.collective(
        lambda e: e.collective_compute(
            "AllGather", ALU.bypass,
            replica_groups=[[0, 1], [2, 3], [4, 5], [6, 7]],
            ins=[d["agin"].ap()],
            outs=[d["agout"].ap()]),
        ins=[("aginK",)] + [("aginV", j) for j in range(NT)],
        outs=[("agoutK",), ("agoutV",)], tag=S + "ag")

    if "vsum" not in L0A_PHASES:
        return
    p.dma("gpsimd", lambda e: e.dma_start(out=m["p3"][:, 6656:6912], in_=d["onehot"].ap()),
          outs=[("p3", 13)], tag=S + "ld_oh")
    p.dma("gpsimd", lambda e: e.dma_start(out=m["suffsel"][0:16, :], in_=d["suffsel"].ap()),
          outs=[("suffsel",)], tag=S + "ld_ss")
    grp = PsumGroup()
    for sg in range(16):
        par, jloc = sg % 2, sg // 2
        slot = sg % 2
        def fvt(e, par=par, jloc=jloc, slot=slot):
            src = d["agout"].ap()[par:par + 1].squeeze(0)[2097152:2 * 2097152].rearrange(
                "(a b c) -> a b c", b=128, c=2048)[jloc:jloc + 1].squeeze(0)
            return e.dma_start(out=m["wb"][:, slot * 2048:(slot + 1) * 2048], in_=src)
        p.dma("gpsimd", fvt, ins=[("agoutV",)], outs=[("wb", slot)], tag=S + f"vls{sg}")
        for nn in range(4):
            fl = grp.next_flags(nn)
            p.op("tensor",
                 lambda e, sg=sg, nn=nn, slot=slot, fl=fl:
                 e.matmul(m["pbig"][0:16, nn * 512:(nn + 1) * 512],
                          m["p3"][:, 6656 + sg * 16: 6656 + (sg + 1) * 16],
                          m["wb"][:, slot * 2048 + nn * 512: slot * 2048 + (nn + 1) * 512],
                          start=fl["start"], stop=fl["stop"]),
                 ins=[("p3", 13), ("wb", slot)], outs=[("pbig",)],
                 tag=S + f"vsum{sg}_{nn}")
    grp.close()
    p.op("scalar", lambda e: e.activation(m["p3"][0:16, 0:2048], m["pbig"][0:16, :],
                                          AF.Identity),
         ins=[("pbig",)], outs=[("p3", ci) for ci in range(4)], tag=S + "ev_vsum")
    for nn in range(4):
        ps = "pc" if nn % 2 == 0 else "pd"
        p.op("tensor",
             lambda e, nn=nn, ps=ps:
             e.matmul(m[ps][0:8, :], m["suffsel"][0:16, 0:8],
                      m["p3"][0:16, nn * 512:(nn + 1) * 512],
                      start=True, stop=True),
             ins=[("suffsel",)] + [("p3", nn)], outs=[(ps,)],
             tag=S + f"suff{nn}")
        p.op("scalar",
             lambda e, nn=nn, ps=ps: e.activation(
                 m["p3"][0:8, 2048 + nn * 512: 2048 + (nn + 1) * 512],
                 m[ps][0:8, :], AF.Identity),
             ins=[(ps,)], outs=[("p3", 4 + nn)], tag=S + f"evs{nn}")
    def fsd(e):
        src = bass.AP(m.t["p3"], 2048, [[8192, 8], [1, 2048]])
        return e.dma_start(out=d["suffdram"].ap(), in_=src)
    p.dma("sync", fsd, ins=[("p3", 4 + nn) for nn in range(4)],
          outs=[("suffdram",)], tag=S + "st_suff")

    if "attn" not in L0A_PHASES:
        return
    for j in range(NT):
        nsig = 2 * j + 2
        grp_out = PsumGroup()
        grp_z = PsumGroup()
        def fsl(e, j=j):
            return e.dma_start(out=m["p3"][0:1, 6144:8192],
                               in_=d["suffdram"].ap()[j:j + 1, :])
        p.dma("scalar", fsl, ins=[("suffdram",)],
              outs=[("p3", 12), ("p3", 13), ("p3", 14), ("p3", 15)], tag=S + f"sufl{j}")
        for sg in range(nsig):
            par, jloc = sg % 2, sg // 2
            kslot = sg % 2
            def fkt(e, par=par, jloc=jloc, kslot=kslot):
                src = d["agout"].ap()[par:par + 1].squeeze(0)[0:2097152].rearrange(
                    "(a b c) -> a b c", b=128,
                    c=1024)[:, :, jloc * 128:(jloc + 1) * 128]
                return e.dma_start(out=bass.AP(m.t["wb"], kslot * 2048,
                                               [[8192, 128], [128, 16], [1, 128]]),
                                   in_=src.transpose([1, 0, 2]))
            p.dma("sync", fkt, ins=[("agoutK",)], outs=[("wb", kslot)], tag=S + f"kld{j}_{sg}")
            def fvt2(e, par=par, jloc=jloc, kslot=kslot):
                src = d["agout"].ap()[par:par + 1].squeeze(0)[2097152:2 * 2097152].rearrange(
                    "(a b c) -> a b c", b=128, c=2048)[jloc:jloc + 1].squeeze(0)
                return e.dma_start(out=m["wb"][:, (2 + kslot) * 2048:(3 + kslot) * 2048], in_=src)
            p.dma("gpsimd", fvt2, ins=[("agoutV",)], outs=[("wb", 2 + kslot)], tag=S + f"vld{j}_{sg}")
            for h in range(16):
                pslot = (sg * 16 + h) % 2
                ps = "pb" if pslot == 0 else "pc"
                ptk = ("p3", 8 + pslot)
                ptcol = 4096 + pslot * 128
                p.op("tensor",
                     lambda e, h=h, j=j, kslot=kslot, ps=ps:
                     e.matmul(m[ps][:, 0:128],
                              m["wb"][:, kslot * 2048 + h * 128: kslot * 2048 + (h + 1) * 128],
                              m["p2"][:, h * 1024 + j * 128: h * 1024 + (j + 1) * 128],
                              start=True, stop=True),
                     ins=[("wb", kslot), ("p2", 2 * h + j // 4)], outs=[(ps,)],
                     tag=S + f"st{j}_{sg}_{h}")
                if sg == 2 * j or sg == 2 * j + 1:
                    mk = "mAm" if sg == 2 * j else "mBm"
                    p.op("vector",
                         lambda e, ps=ps, mk=mk: e.tensor_tensor(
                             m[ps][:, 0:128], m[ps][:, 0:128], m[mk][:, :], ALU.mult),
                         ins=[(ps,), ("c", mk)], outs=[(ps,)], tag=S + f"msk{j}_{sg}_{h}")
                p.op("scalar",
                     lambda e, ps=ps, ptcol=ptcol: e.activation(
                         m["p3"][:, ptcol:ptcol + 128], m[ps][:, 0:128], AF.Exp,
                         scale=SQ_T),
                     ins=[(ps,)], outs=[ptk], tag=S + f"exp{j}_{sg}_{h}")
                fo = grp_out.next_flags(h // 4)
                p.op("tensor",
                     lambda e, h=h, kslot=kslot, ptcol=ptcol, fo=fo:
                     e.matmul(m["pbig"][:, h * 128:(h + 1) * 128],
                              m["p3"][:, ptcol:ptcol + 128],
                              m["wb"][:, (2 + kslot) * 2048 + h * 128:(2 + kslot) * 2048 + (h + 1) * 128],
                              start=fo["start"], stop=fo["stop"]),
                     ins=[ptk, ("wb", 2 + kslot)], outs=[("pbig",)],
                     tag=S + f"av{j}_{sg}_{h}")
                fz = grp_z.next_flags()
                p.op("tensor",
                     lambda e, h=h, ptcol=ptcol, fz=fz:
                     e.matmul(m["pa"][:, h:h + 1], m["p3"][:, ptcol:ptcol + 128],
                              m["ones_colb"][:, :],
                              start=fz["start"], stop=fz["stop"]),
                     ins=[ptk, ("ones_colb",)], outs=[("pa",)],
                     tag=S + f"z{j}_{sg}_{h}")
        for h in range(16):
            fo = grp_out.next_flags(h // 4)
            p.op("tensor",
                 lambda e, h=h, fo=fo:
                 e.matmul(m["pbig"][:, h * 128:(h + 1) * 128],
                          m["ones_rowb"][0:1, :],
                          m["p3"][0:1, 6144 + h * 128: 6144 + (h + 1) * 128],
                          start=fo["start"], stop=fo["stop"]),
                 ins=[("ones_rowb",), ("p3", 12), ("p3", 13)], outs=[("pbig",)],
                 tag=S + f"sufv{j}_{h}")
        grp_out.close()
        grp_z.close()
        cnt = float(128 * (14 - 2 * j))
        p.op("vector",
             lambda e, cnt=cnt: e.tensor_scalar(
                 out=m["small"][:, 256:272], in0=m["pa"][:, 0:16],
                 scalar1=cnt, scalar2=None, op0=ALU.add),
             ins=[("pa",)], outs=[("small", 1)], tag=S + f"zc{j}")
        p.op("vector",
             lambda e: e.reciprocal(m["small"][:, 256:272], m["small"][:, 256:272]),
             ins=[("small", 1)], outs=[("small", 1)], tag=S + f"zr{j}")
        for h in range(16):
            p.op("scalar",
                 lambda e, h=h, j=j: e.activation(
                     m["p1"][:, j * D + h * 128: j * D + (h + 1) * 128],
                     m["pbig"][:, h * 128:(h + 1) * 128], AF.Identity,
                     scale=m["small"][:, 256 + h:257 + h]),
                 ins=[("pbig",), ("small", 1)], outs=[("p1", j)],
                 tag=S + f"ao{j}_{h}")

    if "wo" not in L0A_PHASES:
        return
    for kc in range(16):
        def fwo(e, kc=kc):
            src = bass.AP(d["wo"], kc * 128 * 2048, [[2048, 128], [1, 2048]])
            return e.dma_start(out=m["p2"][:, kc * 2048:(kc + 1) * 2048], in_=src)
        p.dma("gpsimd", fwo, outs=[("p2", 4 * kc + i) for i in range(4)],
              tag=S + f"ldwo{kc}")
    for j in range(NT):
        def fat(e, j=j):
            src = m["p1"][:, j * D:(j + 1) * D]
            dst = bass.AP(m.t["p3"], 4096, [[8192, 128], [128, 16], [1, 128]])
            return e.dma_start_transpose(dst, src)
        p.dma("sync", fat, ins=[("p1", j)], outs=[("p3", 8), ("p3", 9), ("p3", 10), ("p3", 11)],
              tag=S + f"aot{j}")
        grp = PsumGroup()
        for kc in range(16):
            for nn in range(4):
                fl = grp.next_flags(nn)
                p.op("tensor",
                     lambda e, kc=kc, nn=nn, fl=fl:
                     e.matmul(m["pbig"][:, nn * 512:(nn + 1) * 512],
                              m["p3"][:, 4096 + kc * 128: 4096 + (kc + 1) * 128],
                              m["p2"][:, kc * 2048 + nn * 512: kc * 2048 + (nn + 1) * 512],
                              start=fl["start"], stop=fl["stop"]),
                     ins=[("p3", 8 + kc // 4), ("p2", 4 * kc + nn)],
                     outs=[("pbig",)], tag=S + f"wo{j}_{kc}_{nn}")
        grp.close()
        p.op("vector",
             lambda e, j=j: e.tensor_tensor(xtile(K, j), xtile(K, j),
                                            m["pbig"][:, :], ALU.add),
             ins=[("pbig",), ("x", j)], outs=[("x", j)], tag=S + f"res{j}")


def emit_l0ffn(K, site):
    p, m, d = K.prog, K.mem, K.dram
    S = f"m{site}_"
    emit_ln(K, tag=S + "ln")
    p.op("gpsimd", lambda e: e.memset(m["ones_rowb"][:, :], 1.0),
         outs=[("ones_rowb",)], tag=S + "orb")
    p.op("gpsimd", lambda e: e.memset(m["ones_colb"][:, :], 1.0),
         outs=[("ones_colb",)], tag=S + "ocb")
    p.op("gpsimd", lambda e: e.memset(m["neg16"][:, :], -1.0e9),
         outs=[("neg16",)], tag=S + "n16")

    p2f = m.t["p2"].ap().bitcast(F32)

    p.dma("sync", lambda e: e.dma_start(out=m["small"][0:1, 288:304], in_=d["gates"].ap()),
          outs=[("small", 1)], tag=S + "ldg")
    p.op("scalar", lambda e: e.activation(m["small"][0:1, 304:320], m["small"][0:1, 288:304],
                                          AF.Exp, scale=-1.0),
         ins=[("small", 1)], outs=[("small", 1)], tag=S + "sg1")
    p.op("vector", lambda e: e.tensor_scalar(out=m["small"][0:1, 304:320],
                                             in0=m["small"][0:1, 304:320],
                                             scalar1=1.0, scalar2=None, op0=ALU.add),
         ins=[("small", 1)], outs=[("small", 1)], tag=S + "sg2")
    p.op("vector", lambda e: e.reciprocal(m["small"][0:1, 304:320], m["small"][0:1, 304:320]),
         ins=[("small", 1)], outs=[("small", 1)], tag=S + "sg3")
    p.op("tensor",
         lambda e: e.matmul(m["pc"][:, 0:16], m["ones_row"][0:1, :],
                            m["small"][0:1, 304:320], start=True, stop=True),
         ins=[("small", 1), ("c", "ones_row")], outs=[("pc",)], tag=S + "sgbcm")
    p.op("vector", lambda e: e.tensor_copy(m["sgbc"][:, :], m["pc"][:, 0:16]),
         ins=[("pc",)], outs=[("sgbc",)], tag=S + "sgbce")

    def fsim(e):
        src = bass.AP(d["sim"], 0, [[16, 128], [128 * 16, 16], [1, 16]])
        return e.dma_start(out=p2f[:, 8448:8704], in_=src)
    p.dma("sync", fsim, outs=[("p2", 33)], tag=S + "ldsim")
    p.op("scalar", lambda e: e.activation(p2f[:, 8192:8448], p2f[:, 8448:8704], AF.Square),
         ins=[("p2", 33)], outs=[("p2", 32)], tag=S + "simsq")
    p.op("tensor",
         lambda e: e.matmul(m["pc"][0:1, 0:256], m["ones_col"][:, 0:1],
                            p2f[:, 8192:8448], start=True, stop=True),
         ins=[("p2", 32), ("c", "ones_col")], outs=[("pc",)], tag=S + "simcs")
    p.op("vector", lambda e: e.tensor_copy(m["small"][0:1, 0:256], m["pc"][0:1, 0:256]),
         ins=[("pc",)], outs=[("small", 0)], tag=S + "simev")
    for step in (128, 64, 32, 16):
        p.op("vector",
             lambda e, step=step: e.tensor_add(m["small"][0:1, 0:step],
                                               m["small"][0:1, 0:step],
                                               m["small"][0:1, step:2 * step]),
             ins=[("small", 0)], outs=[("small", 0)], tag=S + f"simtr{step}")
    p.op("scalar", lambda e: e.activation(m["small"][0:1, 16:32], m["small"][0:1, 0:16], AF.Ln),
         ins=[("small", 0)], outs=[("small", 0)], tag=S + "simln")
    p.op("scalar", lambda e: e.activation(m["small"][0:1, 0:16], m["small"][0:1, 16:32],
                                          AF.Exp, scale=-0.5),
         ins=[("small", 0)], outs=[("small", 0)], tag=S + "simexp")
    p.op("tensor",
         lambda e: e.matmul(m["pc"][:, 0:16], m["ones_row"][0:1, :],
                            m["small"][0:1, 0:16], start=True, stop=True),
         ins=[("small", 0), ("c", "ones_row")], outs=[("pc",)], tag=S + "rsnbc")
    p.op("vector",
         lambda e: e.tensor_tensor(
             p2f[:, 8448:8704], p2f[:, 8448:8704],
             bass.AP(m.t["pc"], 0, [[512, 128], [0, 16], [1, 16]]), ALU.mult),
         ins=[("pc",), ("p2", 33)], outs=[("p2", 33)], tag=S + "simu")
    p.op("tensor",
         lambda e: e.matmul(m["pc"][0:1, 0:256], m["ones_col"][:, 0:1],
                            p2f[:, 8448:8704], start=True, stop=True),
         ins=[("p2", 33), ("c", "ones_col")], outs=[("pc",)], tag=S + "simcs2")
    p.op("vector", lambda e: e.tensor_copy(m["small"][0:1, 0:256], m["pc"][0:1, 0:256]),
         ins=[("pc",)], outs=[("small", 0)], tag=S + "csev")
    for step in (128, 64, 32, 16):
        p.op("vector",
             lambda e, step=step: e.tensor_add(m["small"][0:1, 0:step],
                                               m["small"][0:1, 0:step],
                                               m["small"][0:1, step:2 * step]),
             ins=[("small", 0)], outs=[("small", 0)], tag=S + f"cstr{step}")
    p.op("tensor",
         lambda e: e.matmul(m["pc"][:, 0:16], m["ones_row"][0:1, :],
                            m["small"][0:1, 0:16], start=True, stop=True),
         ins=[("small", 0), ("c", "ones_row")], outs=[("pc",)], tag=S + "csbc")
    p.op("vector", lambda e: e.tensor_copy(m["small"][:, 192:208], m["pc"][:, 0:16]),
         ins=[("pc",)], outs=[("small", 0)], tag=S + "csbc2")

    for j in range(NT):
        for cb in range(4):
            grp = PsumGroup()
            for c4 in range(4):
                ci = cb * 4 + c4
                fl = grp.next_flags(0)
                def ftr(e, ci=ci, c4=c4, j=j):
                    return e.transpose(m["pd"][:, c4 * 128:(c4 + 1) * 128],
                                       xtile(K, j)[:, ci * 128:(ci + 1) * 128],
                                       m["ident"][:, :])
                p.op("tensor", ftr, ins=[("x", j), ("c", "ident")], outs=[("pd",)],
                     tag=S + f"xt{j}_{ci}")
            p.op("scalar",
                 lambda e, cb=cb: e.activation(
                     p2f[:, cb * 512:(cb + 1) * 512],
                     m["pd"][:, :], AF.Identity),
                 ins=[("pd",)], outs=[("p2", cb)], tag=S + f"xte{j}_{cb}")
        grp = PsumGroup()
        for kc in range(16):
            fl = grp.next_flags(0)
            p.op("tensor",
                 lambda e, kc=kc, fl=fl:
                 e.matmul(m["pa"][0:16, 0:128],
                          p2f[:, 8448 + kc * 16: 8448 + (kc + 1) * 16],
                          p2f[:, kc * 128:(kc + 1) * 128],
                          start=fl["start"], stop=fl["stop"]),
                 ins=[("p2", 33)] + [("p2", cb) for cb in range(4)],
                 outs=[("pa",)], tag=S + f"lg{j}_{kc}")
        grp.close()
        p.op("vector", lambda e: e.tensor_copy(m["small"][0:16, 64:192], m["pa"][0:16, 0:128]),
             ins=[("pa",)], outs=[("small", 0)], tag=S + f"lgev{j}")
        p.op("tensor",
             lambda e: e.transpose(m["pd"][:, 0:16], m["small"][0:16, 64:192],
                                   m["ident"][0:16, 0:16]),
             ins=[("small", 0), ("c", "ident")], outs=[("pd",)], tag=S + f"lgtr{j}")
        L = 512
        p.op("scalar",
             lambda e, j=j: e.activation(m["stat"][:, 448 + j:449 + j],
                                         m["stat"][:, j * 2 + 1:j * 2 + 2], AF.Ln),
             ins=[("mv", j)], outs=[("lnvr", j)], tag=S + f"lnv{j}")
        p.op("scalar",
             lambda e, j=j: e.activation(m["stat"][:, 456 + j:457 + j],
                                         m["stat"][:, 448 + j:449 + j], AF.Exp,
                                         scale=-0.5),
             ins=[("lnvr", j)], outs=[("rln", j)], tag=S + f"rln{j}")
        p.op("vector",
             lambda e, j=j: e.tensor_scalar(out=m["stat"][:, 464 + j:465 + j],
                                            in0=m["stat"][:, j * 2:j * 2 + 1],
                                            scalar1=-1.0, scalar2=None, op0=ALU.mult),
             ins=[("mv", j)], outs=[("negm", j)], tag=S + f"negm{j}")
        p.op("vector",
             lambda e, j=j, L=L: e.scalar_tensor_tensor(
                 out=m["small"][:, L:L + 16], in0=m["small"][:, 192:208],
                 scalar=m["stat"][:, 464 + j:465 + j], in1=m["pd"][:, 0:16],
                 op0=ALU.mult, op1=ALU.add),
             ins=[("small", 0), ("negm", j), ("pd",)], outs=[("small", 2)],
             tag=S + f"lgc{j}")
        p.op("vector",
             lambda e, j=j, L=L: e.tensor_scalar(
                 out=m["small"][:, L:L + 16], in0=m["small"][:, L:L + 16],
                 scalar1=m["stat"][:, 456 + j:457 + j],
                 scalar2=float(2048.0 ** -0.5), op0=ALU.mult, op1=ALU.mult),
             ins=[("small", 2), ("rln", j)], outs=[("small", 2)], tag=S + f"lgs{j}")
        p.op("vector",
             lambda e, L=L: e.tensor_sub(m["small"][:, L:L + 16],
                                         m["small"][:, L:L + 16], m["sgbc"][:, :]),
             ins=[("small", 2), ("sgbc",)], outs=[("small", 2)], tag=S + f"lgsg{j}")
        p.op("vector",
             lambda e, L=L: e.tensor_relu(m["small"][:, L + 16:L + 32], m["small"][:, L:L + 16]),
             ins=[("small", 2)], outs=[("small", 2)], tag=S + f"rl{j}")
        p.op("vector",
             lambda e, L=L: e.tensor_scalar(out=m["small"][:, L + 32:L + 48],
                                            in0=m["small"][:, L + 16:L + 32],
                                            scalar1=0.0, scalar2=None, op0=ALU.is_gt),
             ins=[("small", 2)], outs=[("small", 2)], tag=S + f"mk{j}")
        p.op("vector",
             lambda e, L=L: e.reduce_sum(m["small"][:, L + 48:L + 49],
                                         m["small"][:, L + 32:L + 48], axis=AX.X),
             ins=[("small", 2)], outs=[("small", 2)], tag=S + f"rs{j}")
        p.op("vector",
             lambda e, L=L: e.tensor_scalar(out=m["small"][:, L + 48:L + 49],
                                            in0=m["small"][:, L + 48:L + 49],
                                            scalar1=0.5, scalar2=None, op0=ALU.is_lt),
             ins=[("small", 2)], outs=[("small", 2)], tag=S + f"ia{j}")
        p.op("vector",
             lambda e, L=L: e.reduce_max(m["small"][:, L + 49:L + 50],
                                         m["small"][:, L:L + 16], axis=AX.X),
             ins=[("small", 2)], outs=[("small", 2)], tag=S + f"m1{j}")
        p.op("vector",
             lambda e, L=L: e.tensor_scalar(out=m["small"][:, L + 64:L + 80],
                                            in0=m["small"][:, L:L + 16],
                                            scalar1=m["small"][:, L + 49:L + 50],
                                            scalar2=None, op0=ALU.is_ge),
             ins=[("small", 2)], outs=[("small", 2)], tag=S + f"ge1{j}")
        p.op("vector",
             lambda e, L=L: e.scalar_tensor_tensor(
                 out=m["small"][:, L + 80:L + 96], in0=m["small"][:, L + 64:L + 80],
                 scalar=-2.0e9, in1=m["small"][:, L:L + 16],
                 op0=ALU.mult, op1=ALU.add),
             ins=[("small", 2)], outs=[("small", 2)], tag=S + f"msk2{j}")
        p.op("vector",
             lambda e, L=L: e.reduce_max(m["small"][:, L + 50:L + 51],
                                         m["small"][:, L + 80:L + 96], axis=AX.X),
             ins=[("small", 2)], outs=[("small", 2)], tag=S + f"m2{j}")
        p.op("vector",
             lambda e, L=L: e.tensor_scalar(out=m["small"][:, L + 96:L + 112],
                                            in0=m["small"][:, L:L + 16],
                                            scalar1=m["small"][:, L + 50:L + 51],
                                            scalar2=None, op0=ALU.is_ge),
             ins=[("small", 2)], outs=[("small", 2)], tag=S + f"fb{j}")
        p.op("vector",
             lambda e, L=L: e.tensor_sub(m["small"][:, L + 96:L + 112],
                                         m["small"][:, L + 96:L + 112],
                                         m["small"][:, L + 32:L + 48]),
             ins=[("small", 2)], outs=[("small", 2)], tag=S + f"fmd{j}")
        p.op("vector",
             lambda e, L=L: e.scalar_tensor_tensor(
                 out=m["small"][:, L + 32:L + 48],
                 in0=m["small"][:, L + 96:L + 112],
                 scalar=m["small"][:, L + 48:L + 49],
                 in1=m["small"][:, L + 32:L + 48],
                 op0=ALU.mult, op1=ALU.add),
             ins=[("small", 2)], outs=[("small", 2)], tag=S + f"fm{j}")
        p.op("vector",
             lambda e, L=L: e.tensor_scalar(
                 out=m["small"][:, L + 96:L + 112], in0=m["small"][:, L + 32:L + 48],
                 scalar1=1.0e9, scalar2=-1.0e9, op0=ALU.mult, op1=ALU.add),
             ins=[("small", 2)], outs=[("small", 2)], tag=S + f"mlt{j}")
        p.op("vector",
             lambda e, L=L: e.tensor_mul(m["small"][:, L + 112:L + 128],
                                         m["small"][:, L + 16:L + 32],
                                         m["small"][:, L + 32:L + 48]),
             ins=[("small", 2)], outs=[("small", 2)], tag=S + f"mlm{j}")
        p.op("vector",
             lambda e, L=L: e.tensor_add(m["small"][:, L + 112:L + 128],
                                         m["small"][:, L + 112:L + 128],
                                         m["small"][:, L + 96:L + 112]),
             ins=[("small", 2)], outs=[("small", 2)], tag=S + f"ml{j}")
        p.op("scalar",
             lambda e, L=L: e.activation(m["small"][:, L + 112:L + 128],
                                         m["small"][:, L + 112:L + 128], AF.Exp),
             ins=[("small", 2)], outs=[("small", 2)], tag=S + f"sme{j}")
        p.op("vector",
             lambda e, L=L: e.reduce_sum(m["small"][:, L + 51:L + 52],
                                         m["small"][:, L + 112:L + 128], axis=AX.X),
             ins=[("small", 2)], outs=[("small", 2)], tag=S + f"smz{j}")
        p.op("vector",
             lambda e, L=L: e.reciprocal(m["small"][:, L + 51:L + 52],
                                         m["small"][:, L + 51:L + 52]),
             ins=[("small", 2)], outs=[("small", 2)], tag=S + f"smr{j}")
        p.op("vector",
             lambda e, L=L, j=j: e.scalar_tensor_tensor(
                 out=m["stat"][:, 320 + j * 16: 336 + j * 16],
                 in0=m["small"][:, L + 112:L + 128],
                 scalar=m["small"][:, L + 51:L + 52],
                 in1=m["small"][:, L + 32:L + 48],
                 op0=ALU.mult, op1=ALU.mult),
             ins=[("small", 2)], outs=[("wrt", j)], tag=S + f"w{j}")
        p.op("scalar",
             lambda e, j=j: e.activation(m["wrtb"][:, j * 16:(j + 1) * 16],
                                         m["stat"][:, 320 + j * 16: 336 + j * 16],
                                         AF.Identity),
             ins=[("wrt", j)], outs=[("wrtb",)], tag=S + f"wb{j}")

    for g in range(2):
        emit_hfm(K, groups=(g,), tag=S + f"hfm{g}_")
        for tt in range(4):
            j = 4 * g + tt
            for pi, (pnm, wname) in enumerate((("q", "qcat"), ("k", "kcat"), ("v", "vcat"))):
                grp = PsumGroup()
                for kc in range(16):
                    slot = kc % 2
                    def fw(e, kc=kc, slot=slot, wname=wname):
                        src = bass.AP(d[wname], kc * 128 * 2048, [[2048, 128], [1, 2048]])
                        return e.dma_start(out=m["wb"][:, slot * 2048:(slot + 1) * 2048], in_=src)
                    p.dma("gpsimd", fw, outs=[("wb", slot)], tag=S + f"lw{pnm}{j}_{kc}")
                    for nn in range(4):
                        fl = grp.next_flags(nn)
                        p.op("tensor",
                             lambda e, kc=kc, tt=tt, nn=nn, slot=slot, fl=fl:
                             e.matmul(m["pbig"][:, nn * 512:(nn + 1) * 512],
                                      m["p3"][:, kc * 512 + tt * 128: kc * 512 + tt * 128 + 128],
                                      m["wb"][:, slot * 2048 + nn * 512: slot * 2048 + (nn + 1) * 512],
                                      start=fl["start"], stop=fl["stop"]),
                             ins=[("wb", slot), ("p3", kc)], outs=[("pbig",)],
                             tag=S + f"pe{pnm}{j}_{kc}_{nn}")
                grp.close()
                acol = 1536 + pi * 256
                for ei in range(16):
                    if ei == 0:
                        p.op("vector",
                             lambda e, j=j, acol=acol: e.tensor_scalar(
                                 out=m["small"][:, acol:acol + 128],
                                 in0=m["pbig"][:, 0:128],
                                 scalar1=m["stat"][:, 320 + j * 16:321 + j * 16],
                                 scalar2=None, op0=ALU.mult),
                             ins=[("pbig",), ("wrt", j)], outs=[("small", 6 + pi)],
                             tag=S + f"wr{pnm}{j}_0")
                    else:
                        p.op("vector",
                             lambda e, ei=ei, j=j, acol=acol: e.scalar_tensor_tensor(
                                 out=m["small"][:, acol:acol + 128],
                                 in0=m["pbig"][:, ei * 128:(ei + 1) * 128],
                                 scalar=m["stat"][:, 320 + j * 16 + ei:321 + j * 16 + ei],
                                 in1=m["small"][:, acol:acol + 128],
                                 op0=ALU.mult, op1=ALU.add),
                             ins=[("pbig",), ("wrt", j), ("small", 6 + pi)],
                             outs=[("small", 6 + pi)], tag=S + f"wr{pnm}{j}_{ei}")
                p.op("scalar",
                     lambda e, j=j, pi=pi, acol=acol: e.activation(
                         m["p1"][:, pi * 1024 + j * 128: pi * 1024 + (j + 1) * 128],
                         m["small"][:, acol:acol + 128], AF.Identity),
                     ins=[("small", 6 + pi)], outs=[("p1", pi // 2)],
                     tag=S + f"cv{pnm}{j}")

    for pi, pnm in ((0, "q"), (1, "k")):
        for j in range(NT):
            def ftr2(e, pi=pi, j=j):
                pdb = m["pd"][:, :].bitcast(BF16)
                return e.transpose(pdb[:, 0:128],
                                   m["p1"][:, pi * 1024 + j * 128: pi * 1024 + (j + 1) * 128],
                                   m["identb"][:, :])
            p.op("tensor", ftr2, ins=[("p1", pi // 2), ("c", "identb")],
                 outs=[("pd",)], tag=S + f"tf{pnm}{j}")
            p.op("scalar",
                 lambda e, pi=pi, j=j: e.activation(
                     m["p1"][:, 4096 + pi * 1024 + j * 128: 4096 + pi * 1024 + (j + 1) * 128],
                     m["pd"][:, :].bitcast(BF16)[:, 0:128], AF.Identity),
                 ins=[("pd",)], outs=[("p1", 2)], tag=S + f"tfe{pnm}{j}")

    def fag1(e):
        dst = d["agin_s"].ap()[0:131072].rearrange("(a b) -> a b", b=1024)
        return e.dma_start(out=dst, in_=m["p1"][:, 5120:6144])
    p.dma("sync", fag1, ins=[("p1", 2)], outs=[("agins", 0)], tag=S + "agk")
    def fag2(e):
        dst = bass.AP(d["agin_s"], 131072, [[128, 128], [16384, 8], [1, 128]])
        src = bass.AP(m.t["p1"], 2048, [[16384, 128], [128, 8], [1, 128]])
        return e.dma_start(out=dst, in_=src)
    p.dma("sync", fag2, ins=[("p1", 1)], outs=[("agins", 1)], tag=S + "agv")
    p.collective(
        lambda e: e.collective_compute(
            "AllGather", ALU.bypass,
            replica_groups=[[0, 1], [2, 3], [4, 5], [6, 7]],
            ins=[d["agin_s"].ap()], outs=[d["agout_s"].ap()]),
        ins=[("agins", 0), ("agins", 1)], outs=[("agouts",)], tag=S + "ag")

    for j in range(NT):
        grp_o = PsumGroup()
        grp_z = PsumGroup()
        for sg in range(2 * j + 2):
            par, jloc = sg % 2, sg // 2
            kslot = sg % 2
            def fk2(e, par=par, jloc=jloc, kslot=kslot):
                src = d["agout_s"].ap()[par:par + 1].squeeze(0)[0:131072] \
                    .rearrange("(a b) -> a b", b=1024)[:, jloc * 128:(jloc + 1) * 128]
                return e.dma_start(out=m["wb"][:, kslot * 2048: kslot * 2048 + 128], in_=src)
            p.dma("sync", fk2, ins=[("agouts",)], outs=[("wb", kslot)], tag=S + f"kl{j}_{sg}")
            def fv2(e, par=par, jloc=jloc, kslot=kslot):
                src = d["agout_s"].ap()[par:par + 1].squeeze(0)[131072:262144] \
                    .rearrange("(a b c) -> a b c", a=8, b=128)[jloc:jloc + 1].squeeze(0)
                return e.dma_start(out=m["wb"][:, (2 + kslot) * 2048: (2 + kslot) * 2048 + 128], in_=src)
            p.dma("gpsimd", fv2, ins=[("agouts",)], outs=[("wb", 2 + kslot)], tag=S + f"vl{j}_{sg}")
            pslot = sg % 2
            ps = "pb" if pslot == 0 else "pc"
            ptk = ("p3", 12 + pslot)
            ptcol = 6144 + pslot * 512
            p.op("tensor",
                 lambda e, j=j, kslot=kslot, ps=ps:
                 e.matmul(m[ps][:, 0:128],
                          m["wb"][:, kslot * 2048: kslot * 2048 + 128],
                          m["p1"][:, 4096 + j * 128: 4096 + (j + 1) * 128],
                          start=True, stop=True),
                 ins=[("wb", kslot), ("p1", 2)], outs=[(ps,)], tag=S + f"s{j}_{sg}")
            if sg == 2 * j or sg == 2 * j + 1:
                mk = "mAa" if sg == 2 * j else "mBa"
                p.op("vector",
                     lambda e, ps=ps, mk=mk: e.tensor_tensor(
                         m[ps][:, 0:128], m[ps][:, 0:128], m[mk][:, :], ALU.add),
                     ins=[(ps,), ("c", mk)], outs=[(ps,)], tag=S + f"mka{j}_{sg}")
            p.op("scalar",
                 lambda e, ps=ps, ptcol=ptcol: e.activation(
                     m["p3"][:, ptcol:ptcol + 128], m[ps][:, 0:128], AF.Exp, scale=SQ_H),
                 ins=[(ps,)], outs=[ptk], tag=S + f"ex{j}_{sg}")
            fo = grp_o.next_flags(0)
            p.op("tensor",
                 lambda e, ptcol=ptcol, kslot=kslot, fo=fo:
                 e.matmul(m["pa"][:, 0:128],
                          m["p3"][:, ptcol:ptcol + 128],
                          m["wb"][:, (2 + kslot) * 2048: (2 + kslot) * 2048 + 128],
                          start=fo["start"], stop=fo["stop"]),
                 ins=[ptk, ("wb", 2 + kslot)], outs=[("pa",)], tag=S + f"o{j}_{sg}")
            fz = grp_z.next_flags(0)
            p.op("tensor",
                 lambda e, ptcol=ptcol, fz=fz:
                 e.matmul(m["pd"][:, 0:1],
                          m["p3"][:, ptcol:ptcol + 128],
                          m["ones_colb"][:, :],
                          start=fz["start"], stop=fz["stop"]),
                 ins=[ptk, ("ones_colb",)], outs=[("pd",)], tag=S + f"zz{j}_{sg}")
        grp_o.close()
        grp_z.close()
        p.op("vector",
             lambda e: e.tensor_copy(m["small"][:, 272:273], m["pd"][:, 0:1]),
             ins=[("pd",)], outs=[("small", 1)], tag=S + f"zc{j}")
        p.op("vector",
             lambda e: e.reciprocal(m["small"][:, 272:273], m["small"][:, 272:273]),
             ins=[("small", 1)], outs=[("small", 1)], tag=S + f"zr{j}")
        p.op("scalar",
             lambda e, j=j: e.activation(
                 m["p1"][:, 3072 + j * 128: 3072 + (j + 1) * 128],
                 m["pa"][:, 0:128], AF.Identity,
                 scale=m["small"][:, 272:273]),
             ins=[("pa",), ("small", 1)], outs=[("p1", 1)], tag=S + f"aoev{j}")

    for j in range(NT):
        def fz(e, j=j):
            in0 = bass.AP(m.t["p1"], 3072 + j * 128, [[16384, 128], [0, 16], [1, 128]])
            in1 = bass.AP(m.t["wrtb"], j * 16, [[128, 128], [1, 16], [0, 128]])
            return e.tensor_tensor(m["p2"][:, j * 2048:(j + 1) * 2048], in0, in1, ALU.mult)
        p.op("vector", fz, ins=[("p1", 1), ("wrtb",)],
             outs=[("p2", 4 * j + i) for i in range(4)], tag=S + f"zt{j}")
        def fzt(e, j=j):
            src = m["p2"][:, j * 2048:(j + 1) * 2048]
            dst = bass.AP(m.t["p2"], 16384 + j * 128, [[32768, 128], [1024, 16], [1, 128]])
            return e.dma_start_transpose(dst, src)
        p.dma("sync", fzt, ins=[("p2", 4 * j + i) for i in range(4)],
              outs=[("p2", 32 + i) for i in range(32)], tag=S + f"ztt{j}")
    for j in range(NT):
        grp = PsumGroup()
        for kc in range(16):
            slot = kc % 2
            def fwo(e, kc=kc, slot=slot):
                src = bass.AP(d["ocat"], kc * 128 * 2048, [[2048, 128], [1, 2048]])
                return e.dma_start(out=m["wb"][:, slot * 2048:(slot + 1) * 2048], in_=src)
            p.dma("gpsimd", fwo, outs=[("wb", slot)], tag=S + f"lwo{j}_{kc}")
            for nn in range(4):
                fl = grp.next_flags(nn)
                p.op("tensor",
                     lambda e, kc=kc, nn=nn, slot=slot, j=j, fl=fl:
                     e.matmul(m["pbig"][:, nn * 512:(nn + 1) * 512],
                              bass.AP(m.t["p2"], 16384 + kc * 1024 + j * 128,
                                      [[32768, 128], [1, 128]]),
                              m["wb"][:, slot * 2048 + nn * 512: slot * 2048 + (nn + 1) * 512],
                              start=fl["start"], stop=fl["stop"]),
                     ins=[("wb", slot), ("p2", 32 + 2 * kc), ("p2", 33 + 2 * kc)],
                     outs=[("pbig",)], tag=S + f"op{j}_{kc}_{nn}")
        grp.close()
        p.op("vector",
             lambda e, j=j: e.scalar_tensor_tensor(
                 out=xtile(K, j), in0=m["pbig"][:, :], scalar=0.5,
                 in1=xtile(K, j), op0=ALU.mult, op1=ALU.add),
             ins=[("pbig",), ("x", j)], outs=[("x", j)], tag=S + f"res{j}")


MB = {
    "mbcd": 4096, "mdtr": 4728, "mdts": 5240, "mbcs": 5752, "msta": 7064,
    "tail0": 7576, "tail1": 7972,
}
MGT, MPT, MBTMB = 8192, 8320, 8448


def p2b(K, col, n):
    return K.mem["p2"][:, col:col + n]


def p1f(K, col, n):
    return K.mem.t["p1"].ap().bitcast(F32)[:, col:col + n]


def p1b(K, col, n):
    return K.mem["p1"][:, col:col + n]


def emit_l1attn(K, site):
    p, m, d = K.prog, K.mem, K.dram
    S = "s_"
    p.dma("sync", lambda e: e.dma_start(out=m["mabd"][64:128, 0:1], in_=d["alog"].ap()),
          outs=[("mabd",)], tag=S + "lda")
    p.dma("sync", lambda e: e.dma_start(out=m["mabd"][64:128, 1:2], in_=d["dtbias"].ap()),
          outs=[("mabd",)], tag=S + "ldb")
    p.op("scalar", lambda e: e.activation(m["mabd"][64:128, 2:3], m["mabd"][64:128, 0:1], AF.Exp),
         ins=[("mabd",)], outs=[("mabd",)], tag=S + "expA")
    p.op("vector", lambda e: e.tensor_scalar(out=m["mabd"][64:128, 2:3],
                                             in0=m["mabd"][64:128, 2:3],
                                             scalar1=-1.0, scalar2=None, op0=ALU.mult),
         ins=[("mabd",)], outs=[("mabd",)], tag=S + "negA")
    for nm in ("mcwx", "mcwbc", "mcbx", "mcbbc", "mnw", "tailw_bc", "parw_bc"):
        p.dma("sync", lambda e, nm=nm: e.dma_start(out=m[nm][:, :], in_=d[nm].ap()),
              outs=[(nm,)], tag=S + "ld" + nm)
    p.dma("sync", lambda e: e.dma_start(out=m["mrow"][0:1, :], in_=d["mrow"].ap()),
          outs=[("mrow",)], tag=S + "ldmrow")
    p.op("tensor",
         lambda e: e.matmul(m["pd"][:, 0:64], m["ones_row"][0:1, :],
                            m["mrow"][0:1, 0:64], start=True, stop=True),
         ins=[("mrow",), ("c", "ones_row")], outs=[("pd",)], tag=S + "dskbc")
    p.op("vector", lambda e: e.tensor_copy(m["small"][:, 2048:2112], m["pd"][:, 0:64]),
         ins=[("pd",)], outs=[("small", 8)], tag=S + "dskev")
    p.op("vector", lambda e: e.memset(
        m.t["p2"].ap().bitcast(F32)[0:16, 0:4096], 0.0),
        outs=[("p2", i) for i in range(16)], tag=S + "zeroT")
    p.dma("sync", lambda e: e.dma_start(out=d["msdram"].ap(),
                                        in_=m.t["p2"].ap().bitcast(F32)[0:16, 0:4096]),
          ins=[("p2", i) for i in range(16)], outs=[("msdram",)], tag=S + "savT0")
    for g in range(2):
        _mamba_group(K, site, g)
        if DBG and g == 0:
            p.dma("sync", lambda e: e.dma_start(out=d["dbg"].ap()[:, 0:8192],
                                                in_=m["x"][:, 0:8192]),
                  ins=[("x", j) for j in range(4)], outs=[("dbgt3",)],
                  tag=S + "tap3")
            p.dma("sync",
                  lambda e: e.dma_start(out=d["dbg"].ap()[0:16, 8192:12288],
                                        in_=d["msdram"].ap()),
                  ins=[("msdram",)], outs=[("dbgt4",)], tag=S + "tap4")


def _mamba_group(K, site, g):
    p, m, d = K.prog, K.mem, K.dram
    S = f"s{g}_"
    p2f = m.t["p2"].ap().bitcast(F32)
    emit_ln(K, tiles=range(4 * g, 4 * g + 4), colbase=-8192 * g, tag=S + "ln")
    emit_hfm(K, groups=(g,), src_colbase=-8192 * g, tag=S + "hfm")

    for tt in range(4):
        for half in range(2):
            grp = PsumGroup()
            for kc in range(16):
                slot = kc % 2
                def fw(e, kc=kc, slot=slot, half=half):
                    src = bass.AP(d["winp"], kc * 128 * 8320 + half * 2048,
                                  [[8320, 128], [1, 2048]])
                    return e.dma_start(out=m["wb"][:, slot * 2048:(slot + 1) * 2048], in_=src)
                p.dma("gpsimd", fw, outs=[("wb", slot)], tag=S + f"lwg{tt}_{half}_{kc}")
                for nn in range(4):
                    fl = grp.next_flags(nn)
                    p.op("tensor",
                         lambda e, kc=kc, tt=tt, nn=nn, slot=slot, fl=fl:
                         e.matmul(m["pbig"][:, nn * 512:(nn + 1) * 512],
                                  m["p3"][:, kc * 512 + tt * 128: kc * 512 + tt * 128 + 128],
                                  m["wb"][:, slot * 2048 + nn * 512: slot * 2048 + (nn + 1) * 512],
                                  start=fl["start"], stop=fl["stop"]),
                         ins=[("wb", slot), ("p3", kc)], outs=[("pbig",)],
                         tag=S + f"gmm{tt}_{half}_{kc}_{nn}")
            grp.close()
            p.op("scalar",
                 lambda e, half=half: e.activation(
                     m["wb"][:, 4096 + half * 2048: 4096 + (half + 1) * 2048],
                     m["pbig"][:, :], AF.Silu),
                 ins=[("pbig",)], outs=[("wb", 2 + half)], tag=S + f"gsilu{tt}_{half}")
        def fgd(e, tt=tt, g=g):
            return e.dma_start(out=d["gdram"].ap()[4 * g + tt: 4 * g + tt + 1].squeeze(0),
                               in_=m["wb"][:, 4096:8192])
        p.dma("sync", fgd, ins=[("wb", 2), ("wb", 3)], outs=[("gdram", 4 * g + tt)],
              tag=S + f"gst{tt}")
    for mc in range(32):
        slot = mc % 2
        def fw2(e, mc=mc, slot=slot):
            src = bass.AP(d["winp"], 4096 + mc * 128,
                          [[8320, 128], [128 * 8320, 16], [1, 128]])
            return e.dma_start(out=m["wb"][:, slot * 2048:(slot + 1) * 2048], in_=src)
        p.dma("gpsimd", fw2, outs=[("wb", slot)], tag=S + f"lwx{mc}")
        ps = "pa" if slot == 0 else "pb"
        grp = PsumGroup()
        for kc in range(16):
            fl = grp.next_flags(0)
            p.op("tensor",
                 lambda e, kc=kc, slot=slot, ps=ps, fl=fl:
                 e.matmul(m[ps][:, :],
                          m["wb"][:, slot * 2048 + kc * 128: slot * 2048 + (kc + 1) * 128],
                          m["p3"][:, kc * 512:(kc + 1) * 512],
                          start=fl["start"], stop=fl["stop"]),
                 ins=[("wb", slot), ("p3", kc)], outs=[(ps,)], tag=S + f"xmm{mc}_{kc}")
        grp.close()
        p.op("scalar",
             lambda e, ps=ps, mc=mc: e.activation(m["p2"][:, mc * 512:(mc + 1) * 512],
                                                  m[ps][:, :], AF.Identity),
             ins=[(ps,)], outs=[("p2", mc)], tag=S + f"xev{mc}")
    grp = PsumGroup()
    for kc in range(16):
        slot = kc % 2
        def fw3(e, kc=kc, slot=slot):
            src = bass.AP(d["winp"], 8192 + kc * 128 * 8320, [[8320, 128], [1, 128]])
            return e.dma_start(out=m["wb"][:, slot * 2048: slot * 2048 + 128], in_=src)
        p.dma("gpsimd", fw3, outs=[("wb", slot)], tag=S + f"lwb{kc}")
        for nn in range(4):
            fl = grp.next_flags(0)
            p.op("tensor",
                 lambda e, kc=kc, nn=nn, slot=slot, fl=fl:
                 e.matmul(m["pc"][:, nn * 128:(nn + 1) * 128],
                          m["wb"][:, slot * 2048: slot * 2048 + 128],
                          m["p3"][:, kc * 512 + nn * 128: kc * 512 + (nn + 1) * 128],
                          start=fl["start"], stop=fl["stop"]),
                 ins=[("wb", slot), ("p3", kc)], outs=[("pc",)], tag=S + f"bmm{kc}_{nn}")
    grp.close()
    for tc in range(4):
        p.op("scalar",
             lambda e, tc=tc: e.activation(
                 p1f(K, MB["mbcd"] + tc * 133 + 3, 128)[0:48, :],
                 m["pc"][0:48, tc * 128:(tc + 1) * 128], AF.Identity),
             ins=[("pc",)], outs=[("mbcd",)], tag=S + f"bcev{tc}")
    p.op("scalar",
         lambda e: e.activation(p1f(K, MB["mdtr"], 512)[64:128, :],
                                m["pc"][64:128, :], AF.Identity),
         ins=[("pc",)], outs=[("mdtr",)], tag=S + "dtev")

    for tc in range(4):
        def ft1(e, tc=tc):
            src = bass.AP(m.t["p2"], tc * 128 + 125, [[32768, 128], [512, 32], [1, 3]])
            dst = bass.AP(d["agin_t"], tc * 33 * 128 * 4, [[4, 128], [128 * 4, 32], [1, 3]])
            return e.dma_start(out=dst, in_=src)
        p.dma("sync", ft1, ins=[("p2", i) for i in range(32)], outs=[("agint", tc, 0)],
              tag=S + f"tl{tc}")
        def ft2(e, tc=tc):
            src = p1f(K, MB["mbcd"] + tc * 133 + 128, 3)
            dst = bass.AP(d["agin_t"], (tc * 33 + 32) * 128 * 4, [[4, 128], [1, 3]])
            return e.dma_start(out=dst, in_=src)
        p.dma("gpsimd", ft2, ins=[("mbcd",)], outs=[("agint", tc, 1)], tag=S + f"tlb{tc}")
    agout_t = "agout_tA" if g == 0 else "agout_tB"
    p.collective(
        lambda e, agout_t=agout_t: e.collective_compute(
            "AllGather", ALU.bypass,
            replica_groups=[[0, 1], [2, 3], [4, 5], [6, 7]],
            ins=[d["agin_t"].ap()], outs=[d[agout_t].ap()]),
        ins=[("agint", tc, i) for tc in range(4) for i in range(2)],
        outs=[(agout_t,)], tag=S + "agt")

    for tc in range(4):
        j = 4 * g + tc
        agA, idxA = agout_t, tc
        if tc > 0:
            agB, idxB = agout_t, tc - 1
        elif g == 1:
            agB, idxB = "agout_tA", 3
        else:
            agB, idxB = agout_t, 0
        def flA2(e, agA=agA, idxA=idxA):
            src = bass.AP(d[agA], idxA * 33 * 128 * 4, [[4, 128], [128 * 4, 33], [1, 3]])
            dst = p1f(K, MB["tail1"], 99).rearrange("p (a b) -> p a b", b=3)
            return e.dma_start(out=dst, in_=src)
        p.dma("gpsimd", flA2, ins=[(agA,)], outs=[("tailc",)], tag=S + f"tbA{tc}")
        p.op("vector",
             lambda e, j=j, tc=tc: e.tensor_scalar(
                 out=p1f(K, MB["tail0"] + tc * 99, 99), in0=p1f(K, MB["tail1"], 99),
                 scalar1=m["tailw_bc"][:, 2 * j:2 * j + 1], scalar2=None, op0=ALU.mult),
             ins=[("tailc",), ("tailw_bc",)], outs=[("tails",)], tag=S + f"tbs{tc}")
        def flB(e, agB=agB, idxB=idxB):
            src = bass.AP(d[agB], 4 * 33 * 128 * 4 + idxB * 33 * 128 * 4,
                          [[4, 128], [128 * 4, 33], [1, 3]])
            dst = p1f(K, MB["tail1"], 99).rearrange("p (a b) -> p a b", b=3)
            return e.dma_start(out=dst, in_=src)
        p.dma("gpsimd", flB, ins=[(agB,), ("tails",)], outs=[("tailc",)], tag=S + f"tbB{tc}")
        p.op("vector",
             lambda e, j=j, tc=tc: e.scalar_tensor_tensor(
                 out=p1f(K, MB["tail0"] + tc * 99, 99), in0=p1f(K, MB["tail1"], 99),
                 scalar=m["tailw_bc"][:, 2 * j + 1:2 * j + 2],
                 in1=p1f(K, MB["tail0"] + tc * 99, 99), op0=ALU.mult, op1=ALU.add),
             ins=[("tailc",), ("tails",), ("tailw_bc",)], outs=[("tails",)],
             tag=S + f"tbb{tc}")
        p.op("vector",
             lambda e, tc=tc: e.tensor_copy(
                 p1f(K, MB["mbcd"] + tc * 133, 3)[0:48, :],
                 p1f(K, MB["tail0"] + tc * 99 + 96, 3)[0:48, :]),
             ins=[("tails",)], outs=[("mbcd",)], tag=S + f"bcpre{tc}")

    for mc in range(32):
        ps = "pa" if mc % 2 == 0 else "pb"
        pkeys = [("p2", mc)]
        for k in range(4):
            nb = 3 - k
            if nb > 0:
                def fbnd(e, k=k, mc=mc, nb=nb, ps=ps, first=(k == 0)):
                    out = bass.AP(m.t[ps], 0, [[512, 128], [128, 4], [1, nb]])
                    in0 = bass.AP(m.t["p1"], 2 * (MB["tail0"] + mc * 3 + k),
                                  [[16384, 128], [99 * 2, 4], [2, nb]]).bitcast(F32) \
                        if False else None
                    in0 = p1f(K, MB["tail0"], 396).rearrange(
                        "p (t a b) -> p t a b", t=4, b=3)[:, :, mc, k:3]
                    if first:
                        in1 = m["mcbx"][:, mc:mc + 1].broadcast_to([128, 4, nb]) \
                            if False else bass.AP(m.t["mcbx"], mc, [[32, 128], [0, 4], [0, nb]])
                    else:
                        in1 = bass.AP(m.t[ps], 0, [[512, 128], [128, 4], [1, nb]])
                    return e.scalar_tensor_tensor(
                        out=out, in0=in0, scalar=m["mcwx"][:, mc * 4 + k: mc * 4 + k + 1],
                        in1=in1, op0=ALU.mult, op1=ALU.add)
                p.op("vector", fbnd, ins=[("tails",), ("mcwx",), ("mcbx",), (ps,)],
                     outs=[(ps,)], tag=S + f"cvb{mc}_{k}")
            def fmain(e, k=k, mc=mc, nb=nb, ps=ps, first=(k == 0)):
                out = bass.AP(m.t[ps], nb, [[512, 128], [128, 4], [1, 128 - nb]])
                in0 = bass.AP(m.t["p2"], mc * 512, [[32768, 128], [128, 4], [1, 128 - nb]])
                if first:
                    in1 = bass.AP(m.t["mcbx"], mc, [[32, 128], [0, 4], [0, 128 - nb]])
                else:
                    in1 = out
                return e.scalar_tensor_tensor(
                    out=out, in0=in0, scalar=m["mcwx"][:, mc * 4 + k: mc * 4 + k + 1],
                    in1=in1, op0=ALU.mult, op1=ALU.add)
            p.op("vector", fmain, ins=pkeys + [("mcwx",), ("mcbx",), (ps,)],
                 outs=[(ps,)], tag=S + f"cvm{mc}_{k}")
        p.op("scalar",
             lambda e, mc=mc, ps=ps: e.activation(
                 bass.AP(m.t["p1"], mc * 128, [[16384, 128], [4096, 2], [1, 128]]),
                 m[ps][:, 0:256], AF.Silu),
             ins=[(ps,)], outs=[("p1", 0), ("p1", 1), ("p1", 2), ("p1", 3)],
             tag=S + f"cse{mc}_a")
        p.op("scalar",
             lambda e, mc=mc, ps=ps: e.activation(
                 bass.AP(m.t["p3"], mc * 128, [[8192, 128], [4096, 2], [1, 128]]),
                 m[ps][:, 256:512], AF.Silu),
             ins=[(ps,)], outs=[("p3", i) for i in range(16)],
             tag=S + f"cse{mc}_b")

    for tc in range(4):
        j = 4 * g + tc
        for k in range(4):
            def fbc(e, k=k, tc=tc, first=(k == 0)):
                out = m["pd"][0:48, 0:128]
                in0 = p1f(K, MB["mbcd"] + tc * 133 + k, 128)[0:48, :]
                if first:
                    in1 = bass.AP(m.t["mcbbc"], 0, [[1, 128], [0, 128]])[0:48, :]
                else:
                    in1 = out
                return e.scalar_tensor_tensor(
                    out=out, in0=in0, scalar=m["mcwbc"][0:48, k:k + 1],
                    in1=in1, op0=ALU.mult, op1=ALU.add)
            p.op("vector", fbc, ins=[("mbcd",), ("mcwbc",), ("mcbbc",), ("pd",)],
                 outs=[("pd",)], tag=S + f"cvbc{tc}_{k}")
        p.op("scalar",
             lambda e, tc=tc: e.activation(p1f(K, MB["mbcs"] + tc * 128, 128)[0:48, :],
                                           m["pd"][0:48, 0:128], AF.Silu),
             ins=[("pd",)], outs=[("mbcs",)], tag=S + f"bcsilu{tc}")
        p.op("tensor",
             lambda e, tc=tc: e.transpose(m["pd"][:, 64:80],
                                          p1f(K, MB["mbcs"] + tc * 128, 128)[32:48, :],
                                          m["ident"][32:48, 32:48]),
             ins=[("mbcs",), ("c", "ident")], outs=[("pd",)], tag=S + f"ct1{tc}")
        p.op("scalar",
             lambda e: e.activation(p1f(K, MB["mbcs"] + 1024, 16),
                                    m["pd"][:, 64:80], AF.Identity),
             ins=[("pd",)], outs=[("mbcs",)], tag=S + f"ct1e{tc}")
        p.op("tensor",
             lambda e: e.transpose(m["pd"][0:16, 0:128],
                                   p1f(K, MB["mbcs"] + 1024, 16),
                                   m["ident"][:, :]),
             ins=[("mbcs",), ("c", "ident")], outs=[("pd",)], tag=S + f"ct2{tc}")
        p.op("scalar",
             lambda e, tc=tc: e.activation(p1f(K, MB["mbcs"] + 512 + tc * 128, 128)[0:16, :],
                                           m["pd"][0:16, 0:128], AF.Identity),
             ins=[("pd",)], outs=[("mbcs",)], tag=S + f"ct2e{tc}")

        dts = MB["mdts"]
        p.op("vector",
             lambda e, tc=tc: e.tensor_scalar(
                 out=p1f(K, dts, 128)[64:128, :],
                 in0=p1f(K, MB["mdtr"] + tc * 128, 128)[64:128, :],
                 scalar1=m["mabd"][64:128, 1:2], scalar2=20.0,
                 op0=ALU.add, op1=ALU.min),
             ins=[("mdtr",), ("mabd",)], outs=[("mdts",)], tag=S + f"dtz{tc}")
        p.op("scalar",
             lambda e: e.activation(p1f(K, dts + 128, 128)[64:128, :],
                                    p1f(K, dts, 128)[64:128, :], AF.Exp),
             ins=[("mdts",)], outs=[("mdts",)], tag=S + f"dte{tc}")
        p.op("vector",
             lambda e: e.tensor_scalar(
                 out=p1f(K, dts + 128, 128)[64:128, :],
                 in0=p1f(K, dts + 128, 128)[64:128, :],
                 scalar1=1.0, scalar2=None, op0=ALU.add),
             ins=[("mdts",)], outs=[("mdts",)], tag=S + f"dt1{tc}")
        p.op("scalar",
             lambda e: e.activation(p1f(K, dts + 128, 128)[64:128, :],
                                    p1f(K, dts + 128, 128)[64:128, :], AF.Ln),
             ins=[("mdts",)], outs=[("mdts",)], tag=S + f"dtv{tc}")
        p.op("scalar",
             lambda e: e.activation(p1f(K, dts + 256, 128)[64:128, :],
                                    p1f(K, dts + 128, 128)[64:128, :], AF.Ln),
             ins=[("mdts",)], outs=[("mdts",)], tag=S + f"lndt{tc}")
        p.op("vector",
             lambda e: e.tensor_scalar(
                 out=p1f(K, dts + 384, 128)[64:128, :],
                 in0=p1f(K, dts + 128, 128)[64:128, :],
                 scalar1=m["mabd"][64:128, 2:3], scalar2=None, op0=ALU.mult),
             ins=[("mdts",), ("mabd",)], outs=[("mdts",)], tag=S + f"dtA{tc}")
        p.op("vector",
             lambda e: e.tensor_tensor_scan(
                 out=p1f(K, dts, 128)[64:128, :],
                 data0=m["ones_col"][64:128, 0:1].broadcast_to([64, 128]),
                 data1=p1f(K, dts + 384, 128)[64:128, :],
                 initial=0.0, op0=ALU.mult, op1=ALU.add),
             ins=[("mdts",), ("c", "ones_col")], outs=[("mdts",)], tag=S + f"acum{tc}")

        p.op("tensor",
             lambda e, tc=tc: e.matmul(m["pc"][:, 0:128],
                                       p1f(K, MB["mbcs"] + tc * 128, 128)[0:16, :],
                                       p1f(K, MB["mbcs"] + 512 + tc * 128, 128)[0:16, :],
                                       start=True, stop=True),
             ins=[("mbcs",)], outs=[("pc",)], tag=S + f"gt{tc}")
        p.op("scalar",
             lambda e: e.activation(p2b(K, MGT, 128), m["pc"][:, 0:128], AF.Identity),
             ins=[("pc",)], outs=[("p2", 16)], tag=S + f"gte{tc}")
        p.op("tensor",
             lambda e, tc=tc: e.transpose(m["pd"][:, 0:16],
                                          p1f(K, MB["mbcs"] + tc * 128, 128)[0:16, :],
                                          m["ident"][0:16, 0:16]),
             ins=[("mbcs",), ("c", "ident")], outs=[("pd",)], tag=S + f"btm{tc}")
        p.op("scalar",
             lambda e: e.activation(p2b(K, MBTMB, 16), m["pd"][:, 0:16], AF.Identity),
             ins=[("pd",)], outs=[("p2", 16)], tag=S + f"btme{tc}")
        p.op("vector",
             lambda e: e.tensor_sub(p1f(K, dts + 384, 128)[64:128, :],
                                    p1f(K, dts, 128)[64:128, :],
                                    p1f(K, dts + 256, 128)[64:128, :]),
             ins=[("mdts",)], outs=[("mdts",)], tag=S + f"am{tc}")
        p.op("tensor",
             lambda e: e.transpose(m["pd"][:, 0:64], p1f(K, dts + 384, 128)[64:128, :],
                                   m["ident"][64:128, 64:128]),
             ins=[("mdts",), ("c", "ident")], outs=[("pd",)], tag=S + f"amt{tc}")
        p.op("scalar",
             lambda e: e.activation(m["small"][:, 1536:1600], m["pd"][:, 0:64],
                                    AF.Identity, scale=-1.0),
             ins=[("pd",)], outs=[("small", 6)], tag=S + f"amte{tc}")
        p.op("scalar",
             lambda e: e.activation(p1f(K, dts + 256, 128)[64:128, :],
                                    p1f(K, dts, 128)[64:128, :], AF.Exp),
             ins=[("mdts",)], outs=[("mdts",)], tag=S + f"expa{tc}")
        p.op("tensor",
             lambda e: e.transpose(m["pd"][:, 0:64], p1f(K, dts + 256, 128)[64:128, :],
                                   m["ident"][64:128, 64:128]),
             ins=[("mdts",), ("c", "ident")], outs=[("pd",)], tag=S + f"expat{tc}")
        p.op("scalar",
             lambda e, tc=tc: e.activation(m["small"][:, 1792 + tc * 64: 1856 + tc * 64],
                                           m["pd"][:, 0:64], AF.Identity),
             ins=[("pd",)], outs=[("small", 7)], tag=S + f"expate{tc}")
        p.op("scalar",
             lambda e: e.activation(m["small"][64:128, 1664:1665],
                                    p1f(K, dts, 128)[64:128, 127:128], AF.Exp),
             ins=[("mdts",)], outs=[("small", 6)], tag=S + f"dtot{tc}")
        def fdt(e, tc=tc):
            dst = bass.AP(d["agin_m"], tc * 65600 + 65536, [[1, 64], [1, 1]])
            return e.dma_start(out=dst, in_=m["small"][64:128, 1664:1665])
        p.dma("sync", fdt, ins=[("small", 6)], outs=[("aginm", tc, 8)], tag=S + f"dtst{tc}")
        p.op("scalar",
             lambda e: e.activation(p1f(K, dts + 384, 128)[64:128, :],
                                    p1f(K, dts, 128)[64:128, :], AF.Exp,
                                    scale=-1.0,
                                    bias=p1f(K, dts, 128)[64:128, 127:128]),
             ins=[("mdts",)], outs=[("mdts",)], tag=S + f"e2{tc}")
        p.op("vector",
             lambda e: e.tensor_mul(p1f(K, dts + 384, 128)[64:128, :],
                                    p1f(K, dts + 384, 128)[64:128, :],
                                    p1f(K, dts + 128, 128)[64:128, :]),
             ins=[("mdts",)], outs=[("mdts",)], tag=S + f"w2{tc}")
        p.op("tensor",
             lambda e: e.transpose(m["pd"][:, 0:64], p1f(K, dts + 384, 128)[64:128, :],
                                   m["ident"][64:128, 64:128]),
             ins=[("mdts",), ("c", "ident")], outs=[("pd",)], tag=S + f"w2t{tc}")
        p.op("scalar",
             lambda e: e.activation(m["small"][:, 2112:2176], m["pd"][:, 0:64], AF.Identity),
             ins=[("pd",)], outs=[("small", 8)], tag=S + f"w2te{tc}")
        def fxt1(e, tc=tc):
            if tc < 2:
                srcx = m["p1"][:, tc * 4096:(tc + 1) * 4096]
            else:
                srcx = m["p3"][:, (tc - 2) * 4096:(tc - 1) * 4096]
            dst = bass.AP(m.t["wb"], 0, [[8192, 128], [128, 32], [1, 128]])
            return e.dma_start_transpose(dst, srcx)
        p.dma("sync", fxt1,
              ins=([("p1", 2 * tc), ("p1", 2 * tc + 1)] if tc < 2 else
                   [("p3", i) for i in range((tc - 2) * 8, (tc - 1) * 8)]),
              outs=[("wb", 0), ("wb", 1)], tag=S + f"xt1{tc}")
        p.op("vector",
             lambda e: e.tensor_tensor(
                 m["wb"][:, 4096:8192], m["wb"][:, 0:4096],
                 bass.AP(m.t["small"], 2112, [[2304, 128], [1, 64], [0, 64]]),
                 ALU.mult),
             ins=[("wb", 0), ("wb", 1), ("small", 8)],
             outs=[("wb", 2), ("wb", 3)], tag=S + f"xe{tc}")
        grp_y = None
        grp_h = None
        for h in range(64):
            b8 = h % 8
            if b8 == 0:
                grp_y = PsumGroup()
                grp_h = PsumGroup()
            p.op("tensor",
                 lambda e, h=h: e.matmul(m["pb"][0:1, 0:128],
                                         m["ident"][64:128, 64 + h:65 + h],
                                         p1f(K, dts, 128)[64:128, :],
                                         start=True, stop=True),
                 ins=[("mdts",), ("c", "ident")], outs=[("pb",)], tag=S + f"ax{tc}_{h}")
            p.op("scalar",
                 lambda e: e.activation(m["small"][0:1, 2176:2304], m["pb"][0:1, 0:128],
                                        AF.Identity),
                 ins=[("pb",)], outs=[("small", 8)], tag=S + f"axe{tc}_{h}")
            p.op("tensor",
                 lambda e: e.matmul(m["pc"][:, 0:128], m["ones_row"][0:1, :],
                                    m["small"][0:1, 2176:2304], start=True, stop=False),
                 ins=[("small", 8), ("c", "ones_row")], outs=[("pc",)], tag=S + f"pb1{tc}_{h}")
            p.op("tensor",
                 lambda e: e.matmul(m["pc"][:, 0:128], m["ident"][:, :],
                                    m["trineg"][:, :], start=False, stop=True),
                 ins=[("c", "ident"), ("c", "trineg"), ("pc",)], outs=[("pc",)],
                 tag=S + f"pb2{tc}_{h}")
            p.op("scalar",
                 lambda e, h=h: e.activation(p2b(K, MPT, 128), m["pc"][:, 0:128],
                                             AF.Exp, bias=m["small"][:, 1536 + h:1537 + h]),
                 ins=[("pc",), ("small", 6)], outs=[("p2", 16)], tag=S + f"pex{tc}_{h}")
            p.op("vector",
                 lambda e: e.tensor_mul(p2b(K, MPT, 128), p2b(K, MPT, 128),
                                        p2b(K, MGT, 128)),
                 ins=[("p2", 16)], outs=[("p2", 16)], tag=S + f"pg{tc}_{h}")
            fy = grp_y.next_flags(0)
            p.op("tensor",
                 lambda e, h=h, b8=b8, fy=fy:
                 e.matmul(m["pa"][:, b8 * 64:(b8 + 1) * 64], p2b(K, MPT, 128),
                          m["wb"][:, h * 64:(h + 1) * 64],
                          start=fy["start"], stop=fy["stop"]),
                 ins=[("p2", 16), ("wb", 0), ("wb", 1)], outs=[("pa",)],
                 tag=S + f"yd{tc}_{h}")
            fh = grp_h.next_flags(0)
            p.op("tensor",
                 lambda e, h=h, b8=b8, fh=fh:
                 e.matmul(m["pd"][0:16, b8 * 64:(b8 + 1) * 64], p2b(K, MBTMB, 16),
                          m["wb"][:, 4096 + h * 64: 4096 + (h + 1) * 64],
                          start=fh["start"], stop=fh["stop"]),
                 ins=[("p2", 16), ("wb", 2), ("wb", 3)], outs=[("pd",)],
                 tag=S + f"hm{tc}_{h}")
            if b8 == 7:
                grp_y.close()
                grp_h.close()
                bi = h // 8
                p.op("scalar",
                     lambda e, bi=bi, tc=tc: e.activation(
                         m["p2"][:, 16384 + tc * 4096 + bi * 512: 16384 + tc * 4096 + (bi + 1) * 512],
                         m["pa"][:, :], AF.Identity),
                     ins=[("pa",)], outs=[("p2", 32 + tc * 8 + bi)],
                     tag=S + f"yev{tc}_{bi}")
                p.op("vector",
                     lambda e: e.tensor_copy(p1f(K, MB["msta"], 512)[0:16, :],
                                             m["pd"][0:16, :]),
                     ins=[("pd",)], outs=[("msta",)], tag=S + f"hev{tc}_{bi}")
                def fhst(e, tc=tc, bi=bi):
                    src = p1f(K, MB["msta"], 512)[0:16, :]
                    dst = bass.AP(d["agin_m"], tc * 65600 + bi * 512,
                                  [[4096, 16], [1, 512]])
                    return e.dma_start(out=dst, in_=src)
                p.dma("sync", fhst, ins=[("msta",)], outs=[("aginm", tc, bi)],
                      tag=S + f"hst{tc}_{bi}")

    if DBG and g == 0:
        p.dma("sync", lambda e: e.dma_start(out=d["dbgb"].ap(),
                                            in_=m["p2"][:, 16384:32768]),
              ins=[("p2", 32 + i) for i in range(32)], outs=[("dbgt1",)],
              tag=S + "tap1")
        p.dma("gpsimd", lambda e: e.dma_start(out=d["dbg2"].ap(),
                                              in_=m["small"][:, 1536:2048]),
              ins=[("small", 6), ("small", 7), ("small", 8)],
              outs=[("dbgt2",)], tag=S + "tap2")

    agout_m = "agout_mA" if g == 0 else "agout_mB"
    p.collective(
        lambda e, agout_m=agout_m: e.collective_compute(
            "AllGather", ALU.bypass,
            replica_groups=[[0, 1], [2, 3], [4, 5], [6, 7]],
            ins=[d["agin_m"].ap()], outs=[d[agout_m].ap()]),
        ins=[("aginm", tc, bi) for tc in range(4) for bi in list(range(8)) + [8]],
        outs=[(agout_m,)], tag=S + "agm")

    p2f = m.t["p2"].ap().bitcast(F32)
    p.dma("sync", lambda e: e.dma_start(out=p2f[0:16, 0:4096], in_=d["msdram"].ap()),
          ins=[("msdram",)], outs=[("p2", i) for i in range(16)], tag=S + "resT")
    for tc in range(4):
        j = 4 * g + tc
        _mamba_phase2(K, site, g, tc, j, agout_m)
    p.dma("sync", lambda e: e.dma_start(out=d["msdram"].ap(), in_=p2f[0:16, 0:4096]),
          ins=[("p2", i) for i in range(16)], outs=[("msdram",)], tag=S + "savT")

    _mamba_wout(K, site, g)


def _mamba_phase2(K, site, g, tc, j, agout_m):
    p, m, d = K.prog, K.mem, K.dram
    S = f"s{g}p{tc}_"
    p2f = m.t["p2"].ap().bitcast(F32)
    if j > 0:
        for a in (2 * j - 2, 2 * j - 1):
            slab, grp_i, idx = a % 2, a // 8, (a // 2) % 4
            agname = "agout_mA" if grp_i == 0 else "agout_mB"
            def fdk(e, agname=agname, slab=slab, idx=idx):
                src = bass.AP(d[agname], slab * 4 * 65600 + idx * 65600 + 65536,
                              [[64, 1], [1, 64]])
                return e.dma_start(out=m["small"][0:1, 2176:2240], in_=src)
            p.dma("scalar", fdk, ins=[(agname,)], outs=[("small", 8)], tag=S + f"dk{a}")
            p.op("tensor",
                 lambda e: e.matmul(m["pd"][:, 0:64], m["ones_row"][0:1, :],
                                    m["small"][0:1, 2176:2240], start=True, stop=True),
                 ins=[("small", 8), ("c", "ones_row")], outs=[("pd",)], tag=S + f"dkb{a}")
            p.op("vector",
                 lambda e: e.tensor_copy(m["small"][:, 1600:1664], m["pd"][:, 0:64]),
                 ins=[("pd",)], outs=[("small", 6)], tag=S + f"dke{a}")
            def fhl(e, agname=agname, slab=slab, idx=idx):
                src = bass.AP(d[agname], slab * 4 * 65600 + idx * 65600,
                              [[4096, 16], [1, 4096]])
                return e.dma_start(out=p2f[0:16, 4096:8192], in_=src)
            p.dma("scalar", fhl, ins=[(agname,)], outs=[("p2", 16 + i) for i in range(16)],
                  tag=S + f"hl{a}")
            p.op("vector",
                 lambda e: e.tensor_tensor(
                     p2f[0:16, 0:4096], p2f[0:16, 0:4096],
                     bass.AP(m.t["small"], 1600, [[2304, 128], [1, 64], [0, 64]])[0:16, :, :],
                     ALU.mult),
                 ins=[("small", 6)] + [("p2", i) for i in range(16)],
                 outs=[("p2", i) for i in range(16)], tag=S + f"tm{a}")
            p.op("vector",
                 lambda e: e.tensor_add(p2f[0:16, 0:4096], p2f[0:16, 0:4096],
                                        p2f[0:16, 4096:8192]),
                 ins=[("p2", 16 + i) for i in range(16)] + [("p2", i) for i in range(16)],
                 outs=[("p2", i) for i in range(16)], tag=S + f"ta{a}")
    a2 = 2 * j
    slab2, grp2, idx2 = a2 % 2, a2 // 8, (a2 // 2) % 4
    agn2 = "agout_mA" if grp2 == 0 else "agout_mB"
    def fdk2(e, agn2=agn2, slab2=slab2, idx2=idx2):
        src = bass.AP(d[agn2], slab2 * 4 * 65600 + idx2 * 65600 + 65536, [[64, 1], [1, 64]])
        return e.dma_start(out=m["small"][0:1, 2176:2240], in_=src)
    p.dma("scalar", fdk2, ins=[(agn2,)], outs=[("small", 8)], tag=S + "dk2")
    p.op("tensor",
         lambda e: e.matmul(m["pd"][:, 0:64], m["ones_row"][0:1, :],
                            m["small"][0:1, 2176:2240], start=True, stop=True),
         ins=[("small", 8), ("c", "ones_row")], outs=[("pd",)], tag=S + "dkb2")
    p.op("vector", lambda e: e.tensor_copy(m["small"][:, 1600:1664], m["pd"][:, 0:64]),
         ins=[("pd",)], outs=[("small", 6)], tag=S + "dke2")
    def fhl2(e, agn2=agn2, slab2=slab2, idx2=idx2):
        src = bass.AP(d[agn2], slab2 * 4 * 65600 + idx2 * 65600, [[4096, 16], [1, 4096]])
        return e.dma_start(out=p2f[0:16, 4096:8192], in_=src)
    p.dma("scalar", fhl2, ins=[(agn2,)], outs=[("p2", 16 + i) for i in range(16)],
          tag=S + "hl2")
    p.op("vector",
         lambda e, tc=tc: e.tensor_scalar(out=p1f(K, MB["mbcs"] + 1056, 128)[0:16, :],
                                          in0=p1f(K, MB["mbcs"] + 512 + tc * 128, 128)[0:16, :],
                                          scalar1=m["parw_bc"][0:16, 0:1], scalar2=None,
                                          op0=ALU.mult),
         ins=[("mbcs",), ("parw_bc",)], outs=[("mbcs",)], tag=S + "c0")
    p.op("vector",
         lambda e, tc=tc: e.tensor_scalar(out=p1f(K, MB["mbcs"] + 1184, 128)[0:16, :],
                                          in0=p1f(K, MB["mbcs"] + 512 + tc * 128, 128)[0:16, :],
                                          scalar1=m["parw_bc"][0:16, 1:2], scalar2=None,
                                          op0=ALU.mult),
         ins=[("mbcs",), ("parw_bc",)], outs=[("mbcs",)], tag=S + "c1")
    def fsg(e, j=j):
        return e.dma_start(out=m["wb"][:, 4096:8192],
                           in_=d["gdram"].ap()[j:j + 1].squeeze(0))
    p.dma("gpsimd", fsg, ins=[("gdram", j)], outs=[("wb", 2), ("wb", 3)], tag=S + "sg")
    def fxt1(e, tc=tc):
        if tc < 2:
            srcx = m["p1"][:, tc * 4096:(tc + 1) * 4096]
        else:
            srcx = m["p3"][:, (tc - 2) * 4096:(tc - 1) * 4096]
        dst = bass.AP(m.t["wb"], 0, [[8192, 128], [128, 32], [1, 128]])
        return e.dma_start_transpose(dst, srcx)
    p.dma("sync", fxt1,
          ins=([("p1", 2 * tc), ("p1", 2 * tc + 1)] if tc < 2 else
               [("p3", i) for i in range((tc - 2) * 8, (tc - 1) * 8)]),
          outs=[("wb", 0), ("wb", 1)], tag=S + "xt1")
    for sl in range(8):
        p.op("vector",
             lambda e, sl=sl: e.tensor_tensor(
                 p1f(K, MB["msta"], 512)[0:16, :],
                 p2f[0:16, sl * 512:(sl + 1) * 512],
                 bass.AP(m.t["small"], 1600 + sl * 8, [[2304, 128], [1, 8], [0, 64]])[0:16, :, :],
                 ALU.mult),
             ins=[("small", 6)] + [("p2", i) for i in range(16)], outs=[("msta",)],
             tag=S + f"td{sl}")
        grp = PsumGroup()
        f1 = grp.next_flags(0)
        p.op("tensor",
             lambda e, sl=sl, f1=f1: e.matmul(
                 m["pb"][:, :], p1f(K, MB["mbcs"] + 1056, 128)[0:16, :],
                 p2f[0:16, sl * 512:(sl + 1) * 512],
                 start=f1["start"], stop=f1["stop"]),
             ins=[("mbcs",)] + [("p2", i) for i in range(16)], outs=[("pb",)],
             tag=S + f"u0{sl}")
        f2 = grp.next_flags(0)
        p.op("tensor",
             lambda e, sl=sl, f2=f2: e.matmul(
                 m["pb"][:, :], p1f(K, MB["mbcs"] + 1184, 128)[0:16, :],
                 p1f(K, MB["msta"], 512)[0:16, :],
                 start=f2["start"], stop=f2["stop"]),
             ins=[("mbcs",), ("msta",)], outs=[("pb",)], tag=S + f"u1{sl}")
        f3 = grp.next_flags(0)
        p.op("tensor",
             lambda e, sl=sl, f3=f3: e.matmul(
                 m["pb"][:, :], p1f(K, MB["mbcs"] + 1184, 128)[0:16, :],
                 p2f[0:16, 4096 + sl * 512: 4096 + (sl + 1) * 512],
                 start=f3["start"], stop=f3["stop"]),
             ins=[("mbcs",)] + [("p2", 16 + i) for i in range(16)], outs=[("pb",)],
             tag=S + f"u2{sl}")
        grp.close()
        p.op("vector",
             lambda e, sl=sl, tc=tc: e.tensor_tensor(
                 p1f(K, MB["msta"], 512),
                 m["pb"][:, :],
                 bass.AP(m.t["small"], 1792 + tc * 64 + sl * 8,
                         [[2304, 128], [1, 8], [0, 64]]),
                 ALU.mult),
             ins=[("pb",), ("small", 7)], outs=[("msta",)], tag=S + f"ux{sl}")
        p.op("vector",
             lambda e, sl=sl, tc=tc: e.tensor_add(
                 p1f(K, MB["msta"], 512),
                 p1f(K, MB["msta"], 512),
                 m["p2"][:, 16384 + tc * 4096 + sl * 512: 16384 + tc * 4096 + (sl + 1) * 512]),
             ins=[("msta",), ("p2", 32 + tc * 8 + sl)],
             outs=[("msta",)], tag=S + f"ya{sl}")
        p.op("vector",
             lambda e, sl=sl: e.tensor_tensor(
                 p1f(K, MB["mdtr"], 512),
                 m["wb"][:, sl * 512:(sl + 1) * 512],
                 bass.AP(m.t["small"], 2048 + sl * 8, [[2304, 128], [1, 8], [0, 64]]),
                 ALU.mult),
             ins=[("wb", 0), ("wb", 1), ("small", 8)], outs=[("mdtr",)],
             tag=S + f"dx{sl}")
        p.op("vector",
             lambda e, sl=sl: e.tensor_add(p1f(K, MB["msta"], 512),
                                           p1f(K, MB["msta"], 512),
                                           p1f(K, MB["mdtr"], 512)),
             ins=[("msta",), ("mdtr",)], outs=[("msta",)], tag=S + f"yb{sl}")
        p.op("vector",
             lambda e, sl=sl, tc=tc: e.tensor_tensor(
                 m["p2"][:, 16384 + tc * 4096 + sl * 512: 16384 + tc * 4096 + (sl + 1) * 512],
                 p1f(K, MB["msta"], 512),
                 m["wb"][:, 4096 + sl * 512: 4096 + (sl + 1) * 512], ALU.mult),
             ins=[("msta",), ("wb", 2), ("wb", 3)],
             outs=[("p2", 32 + tc * 8 + sl)], tag=S + f"y2{sl}")
    for sl in range(8):
        p.op("vector",
             lambda e, sl=sl, tc=tc: e.bn_stats(
                 m["small"][:, sl * 6: (sl + 1) * 6],
                 m["p2"][:, 16384 + tc * 4096 + sl * 512: 16384 + tc * 4096 + (sl + 1) * 512]),
             ins=[("p2", 32 + tc * 8 + sl)], outs=[("small", 0)],
             tag=S + f"bn{sl}")
    p.op("vector",
         lambda e: e.bn_aggr(m["small"][:, 64:66], m["small"][:, 0:48]),
         ins=[("small", 0)], outs=[("small", 0)], tag=S + "ag")
    p.op("vector",
         lambda e: e.scalar_tensor_tensor(
             out=m["small"][:, 66:67], in0=m["small"][:, 64:65],
             scalar=m["small"][:, 64:65], in1=m["small"][:, 65:66],
             op0=ALU.mult, op1=ALU.add),
         ins=[("small", 0)], outs=[("small", 0)], tag=S + "msq")
    p.op("vector",
         lambda e: e.tensor_scalar(out=m["small"][:, 66:67], in0=m["small"][:, 66:67],
                                   scalar1=float(EPS), scalar2=None, op0=ALU.add),
         ins=[("small", 0)], outs=[("small", 0)], tag=S + "msqe")
    p.op("scalar",
         lambda e: e.activation(m["small"][:, 67:68], m["small"][:, 66:67], AF.Ln),
         ins=[("small", 0)], outs=[("small", 0)], tag=S + "lnm")
    p.op("scalar",
         lambda e: e.activation(m["small"][:, 66:67], m["small"][:, 67:68], AF.Exp,
                                scale=-0.5),
         ins=[("small", 0)], outs=[("small", 0)], tag=S + "rst")
    for sl in range(8):
        p.op("vector",
             lambda e, sl=sl, tc=tc: e.tensor_scalar(
                 out=m["p2"][:, 16384 + tc * 4096 + sl * 512: 16384 + tc * 4096 + (sl + 1) * 512],
                 in0=m["p2"][:, 16384 + tc * 4096 + sl * 512: 16384 + tc * 4096 + (sl + 1) * 512],
                 scalar1=m["small"][:, 66:67], scalar2=None, op0=ALU.mult),
             ins=[("small", 0), ("p2", 32 + tc * 8 + sl)],
             outs=[("p2", 32 + tc * 8 + sl)], tag=S + f"yn{sl}")
    def fyt1(e, tc=tc):
        src = m["p2"][:, 16384 + tc * 4096: 16384 + (tc + 1) * 4096]
        if tc < 2:
            dst = bass.AP(m.t["p1"], tc * 4096, [[16384, 128], [128, 32], [1, 128]])
        else:
            dst = bass.AP(m.t["p3"], (tc - 2) * 4096, [[8192, 128], [128, 32], [1, 128]])
        return e.dma_start_transpose(dst, src)
    p.dma("sync", fyt1, ins=[("p2", 32 + tc * 8 + i) for i in range(8)],
          outs=([("p1", 2 * tc), ("p1", 2 * tc + 1)] if tc < 2 else
                [("p3", i) for i in range((tc - 2) * 8, (tc - 1) * 8)]),
          tag=S + "yt1")


def _mamba_wout(K, site, g):
    p, m, d = K.prog, K.mem, K.dram
    S = f"s{g}w_"
    p2f = m.t["p2"].ap().bitcast(F32)
    for cc in range(16):
        slot = cc % 2
        def fwo(e, cc=cc, slot=slot):
            src = bass.AP(d["wout"], cc * 128, [[2048, 128], [128 * 2048, 32], [1, 128]])
            return e.dma_start(out=m["wb"][:, slot * 4096:(slot + 1) * 4096], in_=src)
        p.dma("gpsimd", fwo, outs=[("wb", 2 * slot), ("wb", 2 * slot + 1)],
              tag=S + f"lwo{cc}")
        p.op("vector",
             lambda e, slot=slot: e.tensor_tensor(
                 m["wb"][:, slot * 4096:(slot + 1) * 4096],
                 m["wb"][:, slot * 4096:(slot + 1) * 4096],
                 bass.AP(m.t["mnw"], 0, [[32, 128], [1, 32], [0, 128]]),
                 ALU.mult),
             ins=[("wb", 2 * slot), ("wb", 2 * slot + 1), ("mnw",)],
             outs=[("wb", 2 * slot), ("wb", 2 * slot + 1)], tag=S + f"nw{cc}")
        ps = "pa" if slot == 0 else "pb"
        grp = PsumGroup()
        for kc in range(32):
            for half in range(2):
                fl = grp.next_flags(0)
                def fmm(e, kc=kc, slot=slot, ps=ps, half=half, fl=fl):
                    if half == 0:
                        rhs = bass.AP(m.t["p1"], kc * 128,
                                      [[16384, 128], [4096, 2], [1, 128]])
                    else:
                        rhs = bass.AP(m.t["p3"], kc * 128,
                                      [[8192, 128], [4096, 2], [1, 128]])
                    return e.matmul(
                        m[ps][:, half * 256:(half + 1) * 256],
                        m["wb"][:, slot * 4096 + kc * 128: slot * 4096 + (kc + 1) * 128],
                        rhs, start=fl["start"], stop=fl["stop"])
                p.op("tensor", fmm,
                     ins=[("wb", 2 * slot), ("wb", 2 * slot + 1)] +
                         ([("p1", 0), ("p1", 1), ("p1", 2), ("p1", 3)] if half == 0 else
                          [("p3", i) for i in range(16)]),
                     outs=[(ps,)], tag=S + f"om{cc}_{kc}_{half}")
        grp.close()
        p.op("scalar",
             lambda e, ps=ps, cc=cc: e.activation(m["p2"][:, cc * 512:(cc + 1) * 512],
                                                  m[ps][:, :], AF.Identity),
             ins=[(ps,)], outs=[("p2", cc)], tag=S + f"oev{cc}")
    for jj in range(4):
        j = 4 * g + jj
        for cb in range(4):
            for ci4 in range(4):
                ci = cb * 4 + ci4
                def ftr(e, ci=ci, ci4=ci4, jj=jj):
                    pdb = m["pd"][:, :].bitcast(BF16)
                    return e.transpose(pdb[:, ci4 * 128:(ci4 + 1) * 128],
                                       m["p2"][:, ci * 512 + jj * 128: ci * 512 + (jj + 1) * 128],
                                       m["identb"][:, :])
                p.op("tensor", ftr, ins=[("p2", ci), ("c", "identb")],
                     outs=[("pd",)], tag=S + f"tr{j}_{cb}_{ci4}")
            p.op("vector",
                 lambda e, j=j, cb=cb: e.tensor_add(
                     xtile(K, j)[:, cb * 512:(cb + 1) * 512],
                     m["pd"][:, :].bitcast(BF16)[:, 0:512],
                     xtile(K, j)[:, cb * 512:(cb + 1) * 512]),
                 ins=[("pd",), ("x", j)], outs=[("x", j)], tag=S + f"res{j}_{cb}")


CHAIN = ('l0ffn', 'l0attn', 'l0ffn2', 'l1ffn', 'l1attn', 'l1ffn2')


EMIT_RANK_REGS = False


def build_body(K):
    load_consts(K)
    if EMIT_RANK_REGS:
        emit_rank_regs(K)
    if not K.chain:
        emit_ln(K)
        emit_hfm(K)
    for site, name in enumerate(K.chain):
        if name in ("l1ffn", "l1ffn2"):
            emit_l1ffn(K, site)
        elif name == "l0attn":
            emit_l0attn(K, site)
        elif name in ("l0ffn", "l0ffn2"):
            emit_l0ffn(K, site)
        elif name == "l1attn":
            emit_l1attn(K, site)
        else:
            raise ValueError(name)
    store_out(K)


def kernel(**inputs):
    in_maps = host_prepare(inputs)
    nc = build(CHAIN)
    res = run_bass_kernel_spmd(nc, in_maps, core_ids=list(range(NCORES)))
    out = np.zeros((NSEQ * T, D), np.float32)
    for c in range(NCORES):
        out[_interleave_index(c)] = res.results[c]["out"]
    return out.reshape(NSEQ, T, D)


if __name__ == "__main__":
    import reference as _r
    ins = _r.setup_inputs(0)
    o = kernel(**{k: np.asarray(v) for k, v in ins.items()})
    print("ran", o.shape, o.dtype)


# version 02
# speedup vs baseline: 1.0023x; 1.0023x over previous
import math
import numpy as np
import concourse.bass as bass
import concourse.mybir as mybir
from concourse.bass_utils import run_bass_kernel_spmd

F32 = mybir.dt.float32
BF16 = mybir.dt.bfloat16
I32 = mybir.dt.int32
AF = mybir.ActivationFunctionType
ALU = mybir.AluOpType
AX = mybir.AxisListType

NCORES = 8
NSEQ = 4
T = 2048
D = 2048
NT = 8
TOK = NT * 128
NTA = 16
EPS = 1e-5

ENGINES = ("tensor", "vector", "scalar", "gpsimd", "sync")
DMA_QUEUES = ("sync", "gpsimd", "scalar")


class Op:
    __slots__ = ("engine", "fn", "ins", "outs", "is_dma", "sem", "seq",
                 "marked", "target", "waits", "tag", "deps")

    def __init__(self, engine, fn, ins, outs, is_dma, sem, seq, tag):
        self.engine = engine
        self.fn = fn
        self.ins = ins
        self.outs = outs
        self.is_dma = is_dma
        self.sem = sem
        self.seq = seq
        self.marked = False
        self.target = None
        self.waits = []
        self.tag = tag


class Prog:
    def __init__(self, nc, n_dma_sems=10):
        self.nc = nc
        self.ops = []
        self.last_writer = {}
        self.readers = {}
        self.key_sem = {}
        self.dma_sem_names = {}
        self.sem_handles = {}
        self._sem_ctxs = []
        self.engine_sem = {e: f"es_{e}" for e in ENGINES}
        self._dma_pools = {q: [f"ds_{q}_{i}" for i in range(n_dma_sems)]
                           for q in DMA_QUEUES}
        self._dma_rr = {q: 0 for q in DMA_QUEUES}
        self._outstanding = {}
        self._cc_idx = 0

    def op(self, engine, fn, ins=(), outs=(), tag=""):
        o = Op(engine, fn, tuple(ins), tuple(outs), False, None,
               len(self.ops), tag)
        self._link(o)
        self.ops.append(o)
        return o

    def dma(self, queue, fn, ins=(), outs=(), tag=""):
        assert queue in DMA_QUEUES
        key = outs[0] if outs else ("dmaout", len(self.ops))
        sem = self.key_sem.get(key)
        if sem is None:
            pool = self._dma_pools[queue]
            sem = pool[self._dma_rr[queue] % len(pool)]
            self._dma_rr[queue] += 1
            self.key_sem[key] = sem
        o = Op(queue, fn, tuple(ins), tuple(outs), True, sem,
               len(self.ops), tag)
        prev = self._outstanding.get(sem)
        if prev is not None and prev.outs and prev.outs[0] != key:
            o.ins = o.ins + prev.outs
        self._link(o)
        self._outstanding[sem] = o
        self.ops.append(o)
        return o

    def collective(self, fn, ins=(), outs=(), tag="cc"):
        sem = f"cc_{self._cc_idx}"
        self._cc_idx += 1
        o = Op("gpsimd", fn, tuple(ins), tuple(outs), False, sem,
               len(self.ops), tag)
        self._link(o)
        self.ops.append(o)
        return o

    def _link(self, o):
        deps = set()
        for k in o.ins:
            w = self.last_writer.get(k)
            if w is not None:
                deps.add(w)
            self.readers.setdefault(k, [])
        for k in o.outs:
            w = self.last_writer.get(k)
            if w is not None:
                deps.add(w)
            for r in self.readers.get(k, ()):
                deps.add(r)
        deps.discard(o)
        o.deps = deps
        for k in o.ins:
            self.readers.setdefault(k, []).append(o)
        for k in o.outs:
            self.last_writer[k] = o
            self.readers[k] = []

    def finalize(self):
        for o in self.ops:
            for d in o.deps:
                d.marked = True
        counts = {}
        for o in self.ops:
            if o.is_dma or o.sem is not None:
                o.marked = True
            if o.marked:
                sem = o.sem if o.sem is not None else self.engine_sem[o.engine]
                inc = 16 if o.is_dma else 1
                counts[sem] = counts.get(sem, 0) + inc
                o.target = (sem, counts[sem], inc)
        watermark = {}
        engine_pos = {e: 0 for e in ENGINES}
        for o in self.ops:
            ws = {}
            for d in o.deps:
                if d.target is None:
                    continue
                sem, val, _ = d.target
                if ws.get(sem, -1) < val:
                    ws[sem] = val
            for sem, val in ws.items():
                wm = watermark.get((o.engine, sem), -1)
                if val > wm:
                    o.waits.append((sem, val))
                    watermark[(o.engine, sem)] = val
        self.final_waits = []

    def sems_needed(self):
        s = set()
        for o in self.ops:
            if o.target is not None:
                s.add(o.target[0])
            for sem, _ in o.waits:
                s.add(sem)
        for sem, _ in self.final_waits:
            s.add(sem)
        return sorted(s)

    def replay(self, block):
        streams = {e: [] for e in ENGINES}
        for o in self.ops:
            streams[o.engine].append(o)

        import contextlib
        import re

        def scope_of(tag):
            return re.sub(r'[0-9_]+$', '', re.sub(r'[0-9]+(_[0-9]+)*$', '', tag))

        def mk(engine, ops):
            def run(eng):
                for o in ops:
                    for sem, val in o.waits:
                        eng.wait_ge(self.sem_handles[sem], val)
                    ctx = (self.nc.named_scope(scope_of(o.tag)) if PROFILE
                           else contextlib.nullcontext())
                    with ctx:
                        inst = o.fn(eng)
                    if o.target is not None:
                        sem, val, inc = o.target
                        if inst is None:
                            raise RuntimeError(f"op {o.tag} returned no inst")
                        inst.then_inc(self.sem_handles[sem], inc)
                if engine == "sync":
                    for sem, val in self.final_waits:
                        eng.wait_ge(self.sem_handles[sem], val)
            return run

        for e in ENGINES:
            getattr(block, e)(mk(e, streams[e]))


class Mem:

    def __init__(self, nc, stack):
        self.nc = nc
        self.stack = stack
        self.t = {}

    def sbuf(self, name, free_elems, dtype):
        h = self.stack.enter_context(self.nc.sbuf_tensor("sb_" + name, [128, free_elems], dtype))
        self.t[name] = h
        return h

    def psum(self, name, free_elems, dtype=F32):
        h = self.stack.enter_context(self.nc.psum_tensor(name, [128, free_elems], dtype))
        self.t[name] = h
        return h

    def __getitem__(self, name):
        return self.t[name]


def _interleave_index(core):
    b = core // 2
    par = core % 2
    tiles = [b * T + (2 * j + par) * 128 for j in range(NT)]
    idx = np.concatenate([np.arange(t0, t0 + 128) for t0 in tiles])
    return idx


def host_prepare(inputs):
    xs_full = np.ascontiguousarray(np.asarray(inputs["L0_attn_x"], np.float32).reshape(NSEQ * T, D))

    w = {}
    w["wq"] = np.ascontiguousarray(np.asarray(inputs["L0_attn_Wq"], np.float32))
    w["wk"] = np.ascontiguousarray(np.asarray(inputs["L0_attn_Wk"], np.float32))
    w["wv"] = np.ascontiguousarray(np.asarray(inputs["L0_attn_Wv"], np.float32))
    w["wo"] = np.ascontiguousarray(np.asarray(inputs["L0_attn_Wo"], np.float32))
    w["sim"] = np.ascontiguousarray(np.asarray(inputs["L0_ffn_sim_matrix"], np.float32))
    w["gates"] = np.ascontiguousarray(np.asarray(inputs["L0_ffn_gates"], np.float32).reshape(1, 16))
    for nm, src in (("qcat", "L0_ffn_q_proj"), ("kcat", "L0_ffn_k_proj"), ("vcat", "L0_ffn_v_proj")):
        a = np.asarray(inputs[src], np.float32)
        w[nm] = np.ascontiguousarray(a.transpose(1, 0, 2).reshape(D, 16 * 128))
    w["ocat"] = np.ascontiguousarray(np.asarray(inputs["L0_ffn_o_proj"], np.float32).reshape(16 * 128, D))
    w["wgT"] = np.ascontiguousarray(np.asarray(inputs["L1_ffn_w_gate"], np.float32).T)
    w["wuT"] = np.ascontiguousarray(np.asarray(inputs["L1_ffn_w_up"], np.float32).T)
    w["wdT"] = np.ascontiguousarray(np.asarray(inputs["L1_ffn_w_down"], np.float32).T)
    win = np.asarray(inputs["L1_attn_W_in"], np.float32)
    winp = np.zeros((D, 8320), np.float32)
    winp[:, 0:8192] = win[:, 0:8192]
    winp[:, 8192:8208] = win[:, 8192:8208]
    winp[:, 8224:8240] = win[:, 8208:8224]
    winp[:, 8256:8320] = win[:, 8224:8288]
    w["winp"] = np.ascontiguousarray(winp)
    w["wout"] = np.ascontiguousarray(np.asarray(inputs["L1_attn_W_out"], np.float32))
    convw = np.asarray(inputs["L1_attn_conv_w"], np.float32)
    convb = np.asarray(inputs["L1_attn_conv_b"], np.float32)
    mcwx = np.zeros((128, 128), np.float32)
    mcbx = np.zeros((128, 32), np.float32)
    for mc in range(32):
        mcwx[:, mc * 4:(mc + 1) * 4] = convw[mc * 128:(mc + 1) * 128, :]
        mcbx[:, mc] = convb[mc * 128:(mc + 1) * 128]
    w["mcwx"] = np.ascontiguousarray(mcwx)
    w["mcbx"] = np.ascontiguousarray(mcbx)
    mcwbc = np.zeros((128, 4), np.float32)
    mcbbc = np.zeros((128, 1), np.float32)
    mcwbc[0:16] = convw[4096:4112]
    mcwbc[32:48] = convw[4112:4128]
    mcbbc[0:16, 0] = convb[4096:4112]
    mcbbc[32:48, 0] = convb[4112:4128]
    w["mcwbc"] = np.ascontiguousarray(mcwbc)
    w["mcbbc"] = np.ascontiguousarray(mcbbc)
    w["mnw"] = np.ascontiguousarray(
        np.asarray(inputs["L1_attn_norm_w"], np.float32).reshape(32, 128).T)
    w["alog"] = np.ascontiguousarray(np.asarray(inputs["L1_attn_A_log"], np.float32).reshape(64, 1))
    w["dtbias"] = np.ascontiguousarray(np.asarray(inputs["L1_attn_dt_bias"], np.float32).reshape(64, 1))
    w["mrow"] = np.ascontiguousarray(np.asarray(inputs["L1_attn_D_skip"], np.float32).reshape(1, 64))

    consts = {}
    consts["ident"] = np.eye(128, dtype=np.float32)
    consts["ones_col"] = np.ones((128, 1), np.float32)
    consts["ones_row"] = np.ones((1, 128), np.float32)
    consts["cums"] = np.triu(np.ones((128, 128), np.float32))
    consts["trineg"] = np.where(np.triu(np.ones((128, 128), bool)), 0.0, -1e9).astype(np.float32)

    oh = np.zeros((128, 16 * 16), np.float32)
    for sg in range(16):
        oh[:, sg * 16 + sg] = 1.0
    consts["onehot"] = oh
    ss = np.zeros((16, 8), np.float32)
    for sg in range(16):
        for j in range(8):
            if sg >= 2 * j + 2:
                ss[sg, j] = 1.0
    consts["suffsel"] = ss

    tri_mult = np.triu(np.ones((128, 128), np.float32))
    ones_m = np.ones((128, 128), np.float32)
    zeros_m = np.zeros((128, 128), np.float32)
    tri_add = np.where(tri_mult > 0, 0.0, -1e9).astype(np.float32)
    zeros_a = np.zeros((128, 128), np.float32)
    full_a = np.full((128, 128), -1e9, np.float32)

    in_maps = []
    for c in range(NCORES):
        par = c % 2
        m = {"xs": np.ascontiguousarray(xs_full[_interleave_index(c)])}
        m.update(w)
        m.update(consts)
        m["maskA_mult"] = tri_mult if par == 0 else ones_m
        m["maskB_mult"] = zeros_m if par == 0 else tri_mult
        m["maskA_add"] = tri_add if par == 0 else zeros_a
        m["maskB_add"] = full_a if par == 0 else tri_add
        m["coreinfo"] = np.array([[2 * (c // 2), 2 * (c // 2) + 1]], np.int32)
        tw = np.zeros((1, 16), np.float32)
        for j in range(8):
            tw[0, 2 * j] = 1.0 if par == 1 else 0.0
            tw[0, 2 * j + 1] = 1.0 if (par == 0 and j > 0) else 0.0
        m["tailw_bc"] = np.ascontiguousarray(np.broadcast_to(tw, (128, 16)))
        pw = np.array([[1.0 - par, float(par)]], np.float32)
        m["parw_bc"] = np.ascontiguousarray(np.broadcast_to(pw, (128, 2)))
        in_maps.append(m)
    return in_maps


SQ_T = 1.0 / math.sqrt(2048.0)
SQ_H = 1.0 / math.sqrt(128.0)

INPUT_SPECS = [
    ("xs", [TOK, D]), ("wq", [D, D]), ("wk", [D, D]), ("wv", [D, D]),
    ("wo", [D, D]), ("sim", [D, 16]), ("gates", [1, 16]),
    ("qcat", [D, D]), ("kcat", [D, D]), ("vcat", [D, D]), ("ocat", [D, D]),
    ("wgT", [D, 8192]), ("wuT", [D, 8192]), ("wdT", [8192, D]),
    ("winp", [D, 8320]), ("wout", [4096, D]),
    ("mcwx", [128, 128]), ("mcwbc", [128, 4]), ("mcbx", [128, 32]),
    ("mcbbc", [128, 1]), ("mnw", [128, 32]),
    ("alog", [64, 1]), ("dtbias", [64, 1]), ("mrow", [1, 64]),
    ("tailw_bc", [128, 16]), ("parw_bc", [128, 2]),
    ("ident", [128, 128]), ("ones_col", [128, 1]), ("ones_row", [1, 128]),
    ("cums", [128, 128]), ("trineg", [128, 128]),
    ("maskA_mult", [128, 128]), ("maskB_mult", [128, 128]),
    ("maskA_add", [128, 128]), ("maskB_add", [128, 128]),
    ("onehot", [128, 256]), ("suffsel", [16, 8]),
]


class Ker:

    def __init__(self, chain):
        self.chain = chain
        self.nc = bass.Bass("TRN2", target_bir_lowering=False, debug=False)
        self.dram = {}
        for name, shape in INPUT_SPECS:
            self.dram[name] = self.nc.dram_tensor(name, shape, F32, kind="ExternalInput")
        self.dram["coreinfo"] = self.nc.dram_tensor("coreinfo", [1, 2], I32, kind="ExternalInput")
        self.dram["out"] = self.nc.dram_tensor("out", [TOK, D], F32, kind="ExternalOutput")
        self.dram["dbg"] = self.nc.dram_tensor("dbg", [128, 16384], F32, kind="ExternalOutput")
        self.dram["dbg2"] = self.nc.dram_tensor("dbg2", [128, 512], F32, kind="ExternalOutput")
        self.dram["dbgb"] = self.nc.dram_tensor("dbgb", [128, 16384], BF16, kind="ExternalOutput")
        self.dram["agin"] = self.nc.dram_tensor("agin", [2, 2097152], BF16, kind="Internal")
        self.dram["agout"] = self.nc.dram_tensor("agout", [2, 2 * 2097152], BF16, kind="Internal")
        self.dram["suffdram"] = self.nc.dram_tensor("suffdram", [8, 2048], BF16, kind="Internal")
        self.dram["agin_s"] = self.nc.dram_tensor("agin_s", [262144], BF16, kind="Internal")
        self.dram["agout_s"] = self.nc.dram_tensor("agout_s", [2, 262144], BF16, kind="Internal")
        self.dram["agin_t"] = self.nc.dram_tensor("agin_t", [67584], BF16, kind="Internal")
        self.dram["agout_tA"] = self.nc.dram_tensor("agout_tA", [135168], BF16, kind="Internal")
        self.dram["agout_tB"] = self.nc.dram_tensor("agout_tB", [135168], BF16, kind="Internal")
        self.dram["agin_m"] = self.nc.dram_tensor("agin_m", [4, 65600], F32, kind="Internal")
        self.dram["agout_mA"] = self.nc.dram_tensor("agout_mA", [524800], F32, kind="Internal")
        self.dram["agout_mB"] = self.nc.dram_tensor("agout_mB", [524800], F32, kind="Internal")
        self.dram["msdram"] = self.nc.dram_tensor("msdram", [16, 4096], F32, kind="Internal")
        self.dram["gdram"] = self.nc.dram_tensor("gdram", [8, 128, 4096], BF16, kind="Internal")
        self.prog = Prog(self.nc)
        self.uid = 0

    def fresh(self, base):
        self.uid += 1
        return f"{base}{self.uid}"


class PsumGroup:

    def __init__(self):
        self.started = set()
        self.last = {}

    def next_flags(self, bank=0):
        f = {"start": bank not in self.started, "stop": False}
        self.started.add(bank)
        self.last[bank] = f
        return f

    def close(self):
        assert self.last, "empty psum group"
        for f in self.last.values():
            f["stop"] = True


def build(chain=("l0ffn", "l0attn", "l0ffn2", "l1ffn", "l1attn", "l1ffn2")):
    import contextlib
    K = Ker(chain)
    nc, prog = K.nc, K.prog
    with contextlib.ExitStack() as stack:
        mem = Mem(nc, stack)
        K.mem = mem

        mem.sbuf("x", NT * D, F32)
        mem.sbuf("p1", 16384, BF16)
        mem.sbuf("p2", 32768, BF16)
        mem.sbuf("p3", 8192, BF16)
        mem.sbuf("wb", 8192, BF16)
        mem.sbuf("small", 2304, F32)
        mem.sbuf("stat", 512, F32)
        for cn, fe, dt in (("ident", 128, F32), ("identb", 128, BF16),
                           ("ones_col", 1, F32),
                           ("trineg", 128, F32),
                           ("mAm", 128, BF16), ("mBm", 128, BF16),
                           ("mAa", 128, BF16), ("mBa", 128, BF16)):
            mem.sbuf(cn, fe, dt)
        mem.sbuf("ones_row", 128, F32)
        mem.sbuf("ones_rowb", 128, BF16)
        mem.sbuf("ones_colb", 1, BF16)
        mem.sbuf("suffsel", 8, BF16)
        mem.sbuf("cinfo", 2, I32)

        mem.sbuf("neg16", 16, F32)
        mem.sbuf("sgbc", 16, F32)
        mem.sbuf("wrtb", 128, BF16)
        mem.sbuf("tailw_bc", 16, F32)
        mem.sbuf("parw_bc", 2, F32)
        mem.sbuf("mabd", 4, F32)
        mem.sbuf("mcwx", 128, F32)
        mem.sbuf("mcwbc", 4, F32)
        mem.sbuf("mcbx", 32, F32)
        mem.sbuf("mcbbc", 1, F32)
        mem.sbuf("mnw", 32, F32)
        mem.sbuf("mrow", 64, F32)

        mem.psum("pbig", 2048)
        mem.psum("pa", 512)
        mem.psum("pb", 512)
        mem.psum("pc", 512)
        mem.psum("pd", 512)

        build_body(K)

        prog.finalize()
        prog.final_waits = [(o.target[0], o.target[1]) for o in K.final_wait_ops]
        sem_names = prog.sems_needed()
        with contextlib.ExitStack() as sstack:
            for sn in sem_names:
                prog.sem_handles[sn] = sstack.enter_context(nc.semaphore(sn))
            with nc.Block() as block:
                prog.replay(block)
    return nc


def load_consts(K):
    p, m, d = K.prog, K.mem, K.dram
    pairs = [("ident", "ident", F32), ("identb", "ident", BF16),
             ("ones_col", "ones_col", F32),
             ("trineg", "trineg", F32),
             ("mAm", "maskA_mult", BF16), ("mBm", "maskB_mult", BF16),
             ("mAa", "maskA_add", BF16), ("mBa", "maskB_add", BF16)]
    for dst, src, dt in pairs:
        ap = d[src].ap()
        queue = "gpsimd" if dt == BF16 else "sync"
        p.dma(queue, lambda e, dst=dst, ap=ap: e.dma_start(out=K.mem[dst][:, :], in_=ap),
              outs=[("c", dst)], tag=f"ld_{dst}")
    p.dma("sync", lambda e: e.dma_start(out=K.mem["ones_row"][0:1, :], in_=d["ones_row"].ap()),
          outs=[("c", "ones_row")], tag="ld_ones_row")
    for j in range(NT):
        p.dma("sync",
              lambda e, j=j: e.dma_start(out=K.mem["x"][:, j * D:(j + 1) * D],
                                         in_=d["xs"].ap()[j * 128:(j + 1) * 128, :]),
              outs=[("x", j)], tag=f"ld_x{j}")


def xtile(K, j):
    return K.mem["x"][:, j * D:(j + 1) * D]


def emit_ln(K, dst_htm=True, tag="ln", tiles=None, colbase=0):
    p, m = K.prog, K.mem
    if tiles is None:
        tiles = range(NT)
    for j in tiles:
        x = xtile(K, j)
        st = K.fresh("lnst")
        def f_stats(e, x=x, j=j):
            r = None
            for k in range(4):
                r = e.bn_stats(m["small"][:, j * 32 + k * 6: j * 32 + (k + 1) * 6],
                               x[:, k * 512:(k + 1) * 512])
            return r
        p.op("vector", f_stats, ins=[("x", j)], outs=[("small", 0)], tag=f"{tag}_bn{j}")
        p.op("vector",
             lambda e, j=j: e.bn_aggr(m["stat"][:, j * 2:j * 2 + 2],
                                      m["small"][:, j * 32:j * 32 + 24]),
             ins=[("small", 0)], outs=[("mv", j)], tag=f"{tag}_ag{j}")
        p.op("vector",
             lambda e, j=j: e.tensor_scalar(out=m["stat"][:, 256 + j:256 + j + 1],
                                            in0=m["stat"][:, j * 2 + 1:j * 2 + 2],
                                            scalar1=float(EPS), scalar2=None,
                                            op0=ALU.add),
             ins=[("mv", j)], outs=[("vpe", j)], tag=f"{tag}_ve{j}")
        p.op("scalar",
             lambda e, j=j: e.activation(m["stat"][:, 272 + j:272 + j + 1],
                                         m["stat"][:, 256 + j:256 + j + 1],
                                         AF.Ln),
             ins=[("vpe", j)], outs=[("lnv", j)], tag=f"{tag}_ln{j}")
        p.op("scalar",
             lambda e, j=j: e.activation(m["stat"][:, 288 + j:288 + j + 1],
                                         m["stat"][:, 272 + j:272 + j + 1],
                                         AF.Exp, scale=-0.5),
             ins=[("lnv", j)], outs=[("rstd", j)], tag=f"{tag}_rs{j}")
        p.op("vector",
             lambda e, j=j: e.tensor_tensor(m["stat"][:, 304 + j:304 + j + 1],
                                            m["stat"][:, j * 2:j * 2 + 1],
                                            m["stat"][:, 288 + j:288 + j + 1],
                                            ALU.mult),
             ins=[("rstd", j), ("mv", j)], outs=[("mrs", j)], tag=f"{tag}_mr{j}")
        p.op("vector",
             lambda e, j=j: e.tensor_scalar(out=m["stat"][:, 304 + j:304 + j + 1],
                                            in0=m["stat"][:, 304 + j:304 + j + 1],
                                            scalar1=-1.0, scalar2=None, op0=ALU.mult),
             ins=[("mrs", j)], outs=[("nmrs", j)], tag=f"{tag}_nm{j}")
        p.op("scalar",
             lambda e, j=j, colbase=colbase: e.activation(
                 m["p1"][:, colbase + j * D: colbase + (j + 1) * D], xtile(K, j),
                 AF.Identity,
                 bias=m["stat"][:, 304 + j:304 + j + 1],
                 scale=m["stat"][:, 288 + j:288 + j + 1]),
             ins=[("x", j), ("rstd", j), ("nmrs", j)],
             outs=[("p1", (colbase + j * D) // D)],
             tag=f"{tag}_h{j}")


def emit_hfm(K, groups=(0, 1), tag="hfm", src_colbase=0):
    p, m = K.prog, K.mem
    for g in groups:
        for jj in range(4):
            j = 4 * g + jj
            def ft(e, j=j, jj=jj, src_colbase=src_colbase):
                src = m["p1"][:, src_colbase + j * D: src_colbase + (j + 1) * D]
                dst = bass.AP(m.t["p3"], jj * 128,
                              [[8192, 128], [512, 16], [1, 128]])
                return e.dma_start_transpose(dst, src)
            p.dma("sync", ft, ins=[("p1", (src_colbase + j * D) // D)],
                  outs=[("p3", ci) for ci in range(16)], tag=f"{tag}{j}")


def emit_rank_regs(K):
    pass


def _stream_lhsT(K, queue, wname, row_stride, col0, nk, dst_ap, key, tag):
    p = K.prog

    def fw(e):
        src = bass.AP(K.dram[wname], col0,
                      [[row_stride, 128], [128 * row_stride, nk], [1, 128]])
        return e.dma_start(out=dst_ap, in_=src)
    return p.dma(queue, fw, outs=[key], tag=tag)


DBG = False
PROFILE = False


def store_out(K):
    p, m, d = K.prog, K.mem, K.dram
    waits = []
    if DBG:
        pass
    for j in range(NT):
        o = p.dma("sync",
                  lambda e, j=j: e.dma_start(out=d["out"].ap()[j * 128:(j + 1) * 128, :],
                                             in_=xtile(K, j)),
                  ins=[("x", j)], outs=[("outj", j)], tag=f"st_out{j}")
        waits.append(o)
    K.final_wait_ops = waits


def emit_l1ffn(K, site):
    p, m = K.prog, K.mem
    S = f"f{site}_"
    emit_ln(K, tag=S + "ln")
    for g in range(2):
        emit_hfm(K, groups=(g,), tag=S + f"hfm{g}_")
        for hc in range(64):
            slot = hc % 4
            gb = m["wb"][:, slot * 2048: (slot + 1) * 2048]
            ub = m["p1"][:, slot * 2048: (slot + 1) * 2048]
            gkey, ukey = ("wb", slot), ("p1", slot)
            for nm, dst, wname, wkey in (("wg", gb, "wgT", gkey),
                                         ("wu", ub, "wuT", ukey)):
                def fw(e, dst=dst, wname=wname, hc=hc):
                    src = bass.AP(K.dram[wname], hc * 128,
                                  [[8192, 128], [128 * 8192, 16], [1, 128]])
                    return e.dma_start(out=dst, in_=src)
                p.dma("gpsimd", fw, outs=[wkey], ins=(), tag=S + f"ldw_{nm}{hc}")
            ps_g = "pa" if hc % 2 == 0 else "pc"
            ps_u = "pb" if hc % 2 == 0 else "pd"
            grp_g, grp_u = PsumGroup(), PsumGroup()
            for kc in range(16):
                fg = grp_g.next_flags()
                p.op("tensor",
                     lambda e, kc=kc, gb=gb, ps=ps_g, fg=fg:
                     e.matmul(m[ps][:, :], gb[:, kc * 128:(kc + 1) * 128],
                              m["p3"][:, kc * 512:(kc + 1) * 512],
                              start=fg["start"], stop=fg["stop"]),
                     ins=[gkey, ("p3", kc)],
                     outs=[(ps_g,)], tag=S + f"mm_g{hc}_{kc}")
            grp_g.close()
            for kc in range(16):
                fu = grp_u.next_flags()
                p.op("tensor",
                     lambda e, kc=kc, ub=ub, ps=ps_u, fu=fu:
                     e.matmul(m[ps][:, :], ub[:, kc * 128:(kc + 1) * 128],
                              m["p3"][:, kc * 512:(kc + 1) * 512],
                              start=fu["start"], stop=fu["stop"]),
                     ins=[ukey, ("p3", kc)],
                     outs=[(ps_u,)], tag=S + f"mm_u{hc}_{kc}")
            grp_u.close()
            sc = 512 + (hc % 2) * 512
            skeys = [("small", 2 + 2 * (hc % 2)), ("small", 3 + 2 * (hc % 2))]
            p.op("scalar",
                 lambda e, ps=ps_g, sc=sc: e.activation(m["small"][:, sc:sc + 512],
                                                        m[ps][:, :], AF.Silu),
                 ins=[(ps_g,)], outs=skeys, tag=S + f"silu{hc}")
            p.op("vector",
                 lambda e, ps=ps_u, hc=hc, sc=sc: e.tensor_tensor(
                     m["p2"][:, hc * 512:(hc + 1) * 512],
                     m["small"][:, sc:sc + 512], m[ps][:, :], ALU.mult),
                 ins=skeys + [(ps_u,)], outs=[("p2", hc)],
                 tag=S + f"gated{hc}")
        for cc in range(16):
            ps_o = "pa" if cc % 2 == 0 else "pb"
            grp = PsumGroup()
            for q in range(4):
                slot = q % 4
                wdb = m["wb"][:, slot * 2048: (slot + 1) * 2048]
                def fw(e, wdb=wdb, cc=cc, q=q):
                    src = bass.AP(K.dram["wdT"], q * 16 * 128 * 2048 + cc * 128,
                                  [[2048, 128], [128 * 2048, 16], [1, 128]])
                    return e.dma_start(out=wdb, in_=src)
                p.dma("gpsimd", fw, outs=[("wb", slot)], tag=S + f"ldwd{cc}_{q}")
                for hh in range(16):
                    hc = q * 16 + hh
                    fl = grp.next_flags()
                    p.op("tensor",
                         lambda e, hh=hh, hc=hc, wdb=wdb, ps=ps_o, fl=fl:
                         e.matmul(m[ps][:, :], wdb[:, hh * 128:(hh + 1) * 128],
                                  m["p2"][:, hc * 512:(hc + 1) * 512],
                                  start=fl["start"], stop=fl["stop"]),
                         ins=[("wb", slot), ("p2", hc)],
                         outs=[(ps_o,)], tag=S + f"mm_d{cc}_{hc}")
            grp.close()
            p.op("scalar",
                 lambda e, ps=ps_o, cc=cc: e.activation(
                     m["p3"][:, cc * 512:(cc + 1) * 512], m[ps][:, :], AF.Identity),
                 ins=[(ps_o,)], outs=[("p3", cc)], tag=S + f"evac_d{cc}")
        for jj in range(4):
            j = 4 * g + jj
            for cb in range(4):
                for ci4 in range(4):
                    ci = cb * 4 + ci4
                    def ftr(e, ci=ci, ci4=ci4, jj=jj):
                        pdb = m["pd"][:, :].bitcast(BF16)
                        return e.transpose(pdb[:, ci4 * 128:(ci4 + 1) * 128],
                                           m["p3"][:, ci * 512 + jj * 128: ci * 512 + (jj + 1) * 128],
                                           m["identb"][:, :])
                    p.op("tensor", ftr, ins=[("p3", ci), ("c", "identb")],
                         outs=[("pd",)], tag=S + f"tr{j}_{cb}_{ci4}")
                p.op("vector",
                     lambda e, j=j, cb=cb: e.scalar_tensor_tensor(
                         out=xtile(K, j)[:, cb * 512:(cb + 1) * 512],
                         in0=m["pd"][:, :].bitcast(BF16)[:, 0:512], scalar=0.5,
                         in1=xtile(K, j)[:, cb * 512:(cb + 1) * 512],
                         op0=ALU.mult, op1=ALU.add),
                     ins=[("pd",), ("x", j)], outs=[("x", j)],
                     tag=S + f"res{j}_{cb}")


L0A_PHASES = ("proj", "ag", "vsum", "attn", "wo")


def emit_l0attn(K, site):
    p, m, d = K.prog, K.mem, K.dram
    S = f"a{site}_"
    emit_ln(K, tag=S + "ln")

    p.op("gpsimd", lambda e: e.memset(m["ones_rowb"][:, :], 1.0),
         outs=[("ones_rowb",)], tag=S + "orb")
    p.op("gpsimd", lambda e: e.memset(m["ones_colb"][:, :], 1.0),
         outs=[("ones_colb",)], tag=S + "ocb")

    for g in range(2) if "proj" in L0A_PHASES else ():
        emit_hfm(K, groups=(g,), tag=S + f"hfm{g}_")
        for nm, wname in (("q", "wq"), ("k", "wk")):
            for dhc in range(16):
                slot = dhc % 2
                wk_key = ("wb", slot)
                _stream_lhsT(K, "gpsimd", wname, 2048, dhc * 128, 16,
                             m["wb"][:, slot * 2048:(slot + 1) * 2048],
                             wk_key, S + f"ldw{nm}{g}_{dhc}")
                ps = "pa" if slot == 0 else "pb"
                grp = PsumGroup()
                for kc in range(16):
                    fl = grp.next_flags()
                    p.op("tensor",
                         lambda e, kc=kc, slot=slot, ps=ps, fl=fl:
                         e.matmul(m[ps][:, :],
                                  m["wb"][:, slot * 2048 + kc * 128: slot * 2048 + (kc + 1) * 128],
                                  m["p3"][:, kc * 512:(kc + 1) * 512],
                                  start=fl["start"], stop=fl["stop"]),
                         ins=[wk_key, ("p3", kc)], outs=[(ps,)],
                         tag=S + f"mm{nm}{g}_{dhc}_{kc}")
                grp.close()
                off = (0 if nm == "q" else 16384) + dhc * 1024 + g * 512
                dstk = ("p2", (0 if nm == "q" else 32) + 2 * dhc + g)
                p.op("scalar",
                     lambda e, ps=ps, off=off: e.activation(
                         m["p2"][:, off:off + 512], m[ps][:, :], AF.Identity),
                     ins=[(ps,)], outs=[dstk], tag=S + f"ev{nm}{g}_{dhc}")
        def fka(e, g=g):
            src = bass.AP(m.t["p2"], 16384 + g * 512,
                          [[32768, 128], [1024, 16], [1, 512]])
            dst = d["agin"].ap()[0:1, :].squeeze(0).rearrange(
                "(a b c) -> a b c", b=128, c=1024)[:, :, g * 512:(g + 1) * 512] \
                .transpose([1, 0, 2])
            return e.dma_start(out=dst, in_=src)
        p.dma("sync", fka, ins=[("p2", 32 + i) for i in range(32)],
              outs=[("aginK",)], tag=S + f"kag{g}")
        for tt in range(4):
            slot = tt % 2
            grp = PsumGroup()
            for kc in range(16):
                def fwv(e, kc=kc, slot=slot):
                    src = bass.AP(d["wv"], kc * 128 * 2048, [[2048, 128], [1, 2048]])
                    return e.dma_start(out=m["wb"][:, (2 + slot) * 2048:(3 + slot) * 2048], in_=src)
                p.dma("gpsimd", fwv, outs=[("wb", 2 + slot)], tag=S + f"lwv{g}_{tt}_{kc}")
                for nn in range(4):
                    fl = grp.next_flags(nn)
                    p.op("tensor",
                         lambda e, kc=kc, tt=tt, nn=nn, slot=slot, fl=fl:
                         e.matmul(m["pbig"][:, nn * 512:(nn + 1) * 512],
                                  m["p3"][:, kc * 512 + tt * 128: kc * 512 + tt * 128 + 128],
                                  m["wb"][:, (2 + slot) * 2048 + nn * 512:(2 + slot) * 2048 + (nn + 1) * 512],
                                  start=fl["start"], stop=fl["stop"]),
                         ins=[("wb", 2 + slot), ("p3", kc)], outs=[("pbig",)],
                         tag=S + f"mmv{g}_{tt}_{kc}_{nn}")
            grp.close()
            j = 4 * g + tt
            p.op("scalar",
                 lambda e, j=j: e.activation(m["p1"][:, j * D:(j + 1) * D],
                                             m["pbig"][:, :], AF.Identity),
                 ins=[("pbig",)], outs=[("p1", j)], tag=S + f"evv{g}_{tt}")
            def fva(e, j=j):
                dst = d["agin"].ap()[1:2, :].squeeze(0).rearrange(
                    "(a b c) -> a b c", b=128, c=2048)[j:j + 1].squeeze(0)
                return e.dma_start(out=dst, in_=m["p1"][:, j * D:(j + 1) * D])
            p.dma("sync", fva, ins=[("p1", j)], outs=[("aginV", j)], tag=S + f"vag{g}_{tt}")

    if "ag" not in L0A_PHASES:
        return
    p.collective(
        lambda e: e.collective_compute(
            "AllGather", ALU.bypass,
            replica_groups=[[0, 1], [2, 3], [4, 5], [6, 7]],
            ins=[d["agin"].ap()],
            outs=[d["agout"].ap()]),
        ins=[("aginK",)] + [("aginV", j) for j in range(NT)],
        outs=[("agoutK",), ("agoutV",)], tag=S + "ag")

    if "vsum" not in L0A_PHASES:
        return
    p.dma("gpsimd", lambda e: e.dma_start(out=m["p3"][:, 6656:6912], in_=d["onehot"].ap()),
          outs=[("p3", 13)], tag=S + "ld_oh")
    p.dma("gpsimd", lambda e: e.dma_start(out=m["suffsel"][0:16, :], in_=d["suffsel"].ap()),
          outs=[("suffsel",)], tag=S + "ld_ss")
    grp = PsumGroup()
    for sg in range(16):
        par, jloc = sg % 2, sg // 2
        slot = sg % 2
        def fvt(e, par=par, jloc=jloc, slot=slot):
            src = d["agout"].ap()[par:par + 1].squeeze(0)[2097152:2 * 2097152].rearrange(
                "(a b c) -> a b c", b=128, c=2048)[jloc:jloc + 1].squeeze(0)
            return e.dma_start(out=m["wb"][:, slot * 2048:(slot + 1) * 2048], in_=src)
        p.dma("gpsimd", fvt, ins=[("agoutV",)], outs=[("wb", slot)], tag=S + f"vls{sg}")
        for nn in range(4):
            fl = grp.next_flags(nn)
            p.op("tensor",
                 lambda e, sg=sg, nn=nn, slot=slot, fl=fl:
                 e.matmul(m["pbig"][0:16, nn * 512:(nn + 1) * 512],
                          m["p3"][:, 6656 + sg * 16: 6656 + (sg + 1) * 16],
                          m["wb"][:, slot * 2048 + nn * 512: slot * 2048 + (nn + 1) * 512],
                          start=fl["start"], stop=fl["stop"]),
                 ins=[("p3", 13), ("wb", slot)], outs=[("pbig",)],
                 tag=S + f"vsum{sg}_{nn}")
    grp.close()
    p.op("scalar", lambda e: e.activation(m["p3"][0:16, 0:2048], m["pbig"][0:16, :],
                                          AF.Identity),
         ins=[("pbig",)], outs=[("p3", ci) for ci in range(4)], tag=S + "ev_vsum")
    for nn in range(4):
        ps = "pc" if nn % 2 == 0 else "pd"
        p.op("tensor",
             lambda e, nn=nn, ps=ps:
             e.matmul(m[ps][0:8, :], m["suffsel"][0:16, 0:8],
                      m["p3"][0:16, nn * 512:(nn + 1) * 512],
                      start=True, stop=True),
             ins=[("suffsel",)] + [("p3", nn)], outs=[(ps,)],
             tag=S + f"suff{nn}")
        p.op("scalar",
             lambda e, nn=nn, ps=ps: e.activation(
                 m["p3"][0:8, 2048 + nn * 512: 2048 + (nn + 1) * 512],
                 m[ps][0:8, :], AF.Identity),
             ins=[(ps,)], outs=[("p3", 4 + nn)], tag=S + f"evs{nn}")
    def fsd(e):
        src = bass.AP(m.t["p3"], 2048, [[8192, 8], [1, 2048]])
        return e.dma_start(out=d["suffdram"].ap(), in_=src)
    p.dma("sync", fsd, ins=[("p3", 4 + nn) for nn in range(4)],
          outs=[("suffdram",)], tag=S + "st_suff")

    if "attn" not in L0A_PHASES:
        return
    for j in range(NT):
        nsig = 2 * j + 2
        grp_out = PsumGroup()
        grp_z = PsumGroup()
        def fsl(e, j=j):
            return e.dma_start(out=m["p3"][0:1, 6144:8192],
                               in_=d["suffdram"].ap()[j:j + 1, :])
        p.dma("scalar", fsl, ins=[("suffdram",)],
              outs=[("p3", 12), ("p3", 13), ("p3", 14), ("p3", 15)], tag=S + f"sufl{j}")
        for sg in range(nsig):
            par, jloc = sg % 2, sg // 2
            kslot = sg % 2
            def fkt(e, par=par, jloc=jloc, kslot=kslot):
                src = d["agout"].ap()[par:par + 1].squeeze(0)[0:2097152].rearrange(
                    "(a b c) -> a b c", b=128,
                    c=1024)[:, :, jloc * 128:(jloc + 1) * 128]
                return e.dma_start(out=bass.AP(m.t["wb"], kslot * 2048,
                                               [[8192, 128], [128, 16], [1, 128]]),
                                   in_=src.transpose([1, 0, 2]))
            p.dma("sync", fkt, ins=[("agoutK",)], outs=[("wb", kslot)], tag=S + f"kld{j}_{sg}")
            def fvt2(e, par=par, jloc=jloc, kslot=kslot):
                src = d["agout"].ap()[par:par + 1].squeeze(0)[2097152:2 * 2097152].rearrange(
                    "(a b c) -> a b c", b=128, c=2048)[jloc:jloc + 1].squeeze(0)
                return e.dma_start(out=m["wb"][:, (2 + kslot) * 2048:(3 + kslot) * 2048], in_=src)
            p.dma("gpsimd", fvt2, ins=[("agoutV",)], outs=[("wb", 2 + kslot)], tag=S + f"vld{j}_{sg}")
            for h in range(16):
                pslot = (sg * 16 + h) % 2
                ps = "pb" if pslot == 0 else "pc"
                ptk = ("p3", 8 + pslot)
                ptcol = 4096 + pslot * 128
                p.op("tensor",
                     lambda e, h=h, j=j, kslot=kslot, ps=ps:
                     e.matmul(m[ps][:, 0:128],
                              m["wb"][:, kslot * 2048 + h * 128: kslot * 2048 + (h + 1) * 128],
                              m["p2"][:, h * 1024 + j * 128: h * 1024 + (j + 1) * 128],
                              start=True, stop=True),
                     ins=[("wb", kslot), ("p2", 2 * h + j // 4)], outs=[(ps,)],
                     tag=S + f"st{j}_{sg}_{h}")
                if sg == 2 * j or sg == 2 * j + 1:
                    mk = "mAm" if sg == 2 * j else "mBm"
                    p.op("vector",
                         lambda e, ps=ps, mk=mk: e.tensor_tensor(
                             m[ps][:, 0:128], m[ps][:, 0:128], m[mk][:, :], ALU.mult),
                         ins=[(ps,), ("c", mk)], outs=[(ps,)], tag=S + f"msk{j}_{sg}_{h}")
                p.op("scalar",
                     lambda e, ps=ps, ptcol=ptcol: e.activation(
                         m["p3"][:, ptcol:ptcol + 128], m[ps][:, 0:128], AF.Exp,
                         scale=SQ_T),
                     ins=[(ps,)], outs=[ptk], tag=S + f"exp{j}_{sg}_{h}")
                fo = grp_out.next_flags(h // 4)
                p.op("tensor",
                     lambda e, h=h, kslot=kslot, ptcol=ptcol, fo=fo:
                     e.matmul(m["pbig"][:, h * 128:(h + 1) * 128],
                              m["p3"][:, ptcol:ptcol + 128],
                              m["wb"][:, (2 + kslot) * 2048 + h * 128:(2 + kslot) * 2048 + (h + 1) * 128],
                              start=fo["start"], stop=fo["stop"]),
                     ins=[ptk, ("wb", 2 + kslot)], outs=[("pbig",)],
                     tag=S + f"av{j}_{sg}_{h}")
                fz = grp_z.next_flags()
                p.op("tensor",
                     lambda e, h=h, ptcol=ptcol, fz=fz:
                     e.matmul(m["pa"][:, h:h + 1], m["p3"][:, ptcol:ptcol + 128],
                              m["ones_colb"][:, :],
                              start=fz["start"], stop=fz["stop"]),
                     ins=[ptk, ("ones_colb",)], outs=[("pa",)],
                     tag=S + f"z{j}_{sg}_{h}")
        for h in range(16):
            fo = grp_out.next_flags(h // 4)
            p.op("tensor",
                 lambda e, h=h, fo=fo:
                 e.matmul(m["pbig"][:, h * 128:(h + 1) * 128],
                          m["ones_rowb"][0:1, :],
                          m["p3"][0:1, 6144 + h * 128: 6144 + (h + 1) * 128],
                          start=fo["start"], stop=fo["stop"]),
                 ins=[("ones_rowb",), ("p3", 12), ("p3", 13)], outs=[("pbig",)],
                 tag=S + f"sufv{j}_{h}")
        grp_out.close()
        grp_z.close()
        cnt = float(128 * (14 - 2 * j))
        p.op("vector",
             lambda e, cnt=cnt: e.tensor_scalar(
                 out=m["small"][:, 256:272], in0=m["pa"][:, 0:16],
                 scalar1=cnt, scalar2=None, op0=ALU.add),
             ins=[("pa",)], outs=[("small", 1)], tag=S + f"zc{j}")
        p.op("vector",
             lambda e: e.reciprocal(m["small"][:, 256:272], m["small"][:, 256:272]),
             ins=[("small", 1)], outs=[("small", 1)], tag=S + f"zr{j}")
        for h in range(16):
            p.op("scalar",
                 lambda e, h=h, j=j: e.activation(
                     m["p1"][:, j * D + h * 128: j * D + (h + 1) * 128],
                     m["pbig"][:, h * 128:(h + 1) * 128], AF.Identity,
                     scale=m["small"][:, 256 + h:257 + h]),
                 ins=[("pbig",), ("small", 1)], outs=[("p1", j)],
                 tag=S + f"ao{j}_{h}")

    if "wo" not in L0A_PHASES:
        return
    for kc in range(16):
        def fwo(e, kc=kc):
            src = bass.AP(d["wo"], kc * 128 * 2048, [[2048, 128], [1, 2048]])
            return e.dma_start(out=m["p2"][:, kc * 2048:(kc + 1) * 2048], in_=src)
        p.dma("gpsimd", fwo, outs=[("p2", 4 * kc + i) for i in range(4)],
              tag=S + f"ldwo{kc}")
    for j in range(NT):
        def fat(e, j=j):
            src = m["p1"][:, j * D:(j + 1) * D]
            dst = bass.AP(m.t["p3"], 4096, [[8192, 128], [128, 16], [1, 128]])
            return e.dma_start_transpose(dst, src)
        p.dma("sync", fat, ins=[("p1", j)], outs=[("p3", 8), ("p3", 9), ("p3", 10), ("p3", 11)],
              tag=S + f"aot{j}")
        grp = PsumGroup()
        for kc in range(16):
            for nn in range(4):
                fl = grp.next_flags(nn)
                p.op("tensor",
                     lambda e, kc=kc, nn=nn, fl=fl:
                     e.matmul(m["pbig"][:, nn * 512:(nn + 1) * 512],
                              m["p3"][:, 4096 + kc * 128: 4096 + (kc + 1) * 128],
                              m["p2"][:, kc * 2048 + nn * 512: kc * 2048 + (nn + 1) * 512],
                              start=fl["start"], stop=fl["stop"]),
                     ins=[("p3", 8 + kc // 4), ("p2", 4 * kc + nn)],
                     outs=[("pbig",)], tag=S + f"wo{j}_{kc}_{nn}")
        grp.close()
        p.op("vector",
             lambda e, j=j: e.tensor_tensor(xtile(K, j), xtile(K, j),
                                            m["pbig"][:, :], ALU.add),
             ins=[("pbig",), ("x", j)], outs=[("x", j)], tag=S + f"res{j}")


def emit_l0ffn(K, site):
    p, m, d = K.prog, K.mem, K.dram
    S = f"m{site}_"
    emit_ln(K, tag=S + "ln")
    p.op("gpsimd", lambda e: e.memset(m["ones_rowb"][:, :], 1.0),
         outs=[("ones_rowb",)], tag=S + "orb")
    p.op("gpsimd", lambda e: e.memset(m["ones_colb"][:, :], 1.0),
         outs=[("ones_colb",)], tag=S + "ocb")
    p.op("gpsimd", lambda e: e.memset(m["neg16"][:, :], -1.0e9),
         outs=[("neg16",)], tag=S + "n16")

    p2f = m.t["p2"].ap().bitcast(F32)

    p.dma("sync", lambda e: e.dma_start(out=m["small"][0:1, 288:304], in_=d["gates"].ap()),
          outs=[("small", 1)], tag=S + "ldg")
    p.op("scalar", lambda e: e.activation(m["small"][0:1, 304:320], m["small"][0:1, 288:304],
                                          AF.Exp, scale=-1.0),
         ins=[("small", 1)], outs=[("small", 1)], tag=S + "sg1")
    p.op("vector", lambda e: e.tensor_scalar(out=m["small"][0:1, 304:320],
                                             in0=m["small"][0:1, 304:320],
                                             scalar1=1.0, scalar2=None, op0=ALU.add),
         ins=[("small", 1)], outs=[("small", 1)], tag=S + "sg2")
    p.op("vector", lambda e: e.reciprocal(m["small"][0:1, 304:320], m["small"][0:1, 304:320]),
         ins=[("small", 1)], outs=[("small", 1)], tag=S + "sg3")
    p.op("tensor",
         lambda e: e.matmul(m["pc"][:, 0:16], m["ones_row"][0:1, :],
                            m["small"][0:1, 304:320], start=True, stop=True),
         ins=[("small", 1), ("c", "ones_row")], outs=[("pc",)], tag=S + "sgbcm")
    p.op("vector", lambda e: e.tensor_copy(m["sgbc"][:, :], m["pc"][:, 0:16]),
         ins=[("pc",)], outs=[("sgbc",)], tag=S + "sgbce")

    def fsim(e):
        src = bass.AP(d["sim"], 0, [[16, 128], [128 * 16, 16], [1, 16]])
        return e.dma_start(out=p2f[:, 8448:8704], in_=src)
    p.dma("sync", fsim, outs=[("p2", 33)], tag=S + "ldsim")
    p.op("scalar", lambda e: e.activation(p2f[:, 8192:8448], p2f[:, 8448:8704], AF.Square),
         ins=[("p2", 33)], outs=[("p2", 32)], tag=S + "simsq")
    p.op("tensor",
         lambda e: e.matmul(m["pc"][0:1, 0:256], m["ones_col"][:, 0:1],
                            p2f[:, 8192:8448], start=True, stop=True),
         ins=[("p2", 32), ("c", "ones_col")], outs=[("pc",)], tag=S + "simcs")
    p.op("vector", lambda e: e.tensor_copy(m["small"][0:1, 0:256], m["pc"][0:1, 0:256]),
         ins=[("pc",)], outs=[("small", 0)], tag=S + "simev")
    for step in (128, 64, 32, 16):
        p.op("vector",
             lambda e, step=step: e.tensor_add(m["small"][0:1, 0:step],
                                               m["small"][0:1, 0:step],
                                               m["small"][0:1, step:2 * step]),
             ins=[("small", 0)], outs=[("small", 0)], tag=S + f"simtr{step}")
    p.op("scalar", lambda e: e.activation(m["small"][0:1, 16:32], m["small"][0:1, 0:16], AF.Ln),
         ins=[("small", 0)], outs=[("small", 0)], tag=S + "simln")
    p.op("scalar", lambda e: e.activation(m["small"][0:1, 0:16], m["small"][0:1, 16:32],
                                          AF.Exp, scale=-0.5),
         ins=[("small", 0)], outs=[("small", 0)], tag=S + "simexp")
    p.op("tensor",
         lambda e: e.matmul(m["pc"][:, 0:16], m["ones_row"][0:1, :],
                            m["small"][0:1, 0:16], start=True, stop=True),
         ins=[("small", 0), ("c", "ones_row")], outs=[("pc",)], tag=S + "rsnbc")
    p.op("vector",
         lambda e: e.tensor_tensor(
             p2f[:, 8448:8704], p2f[:, 8448:8704],
             bass.AP(m.t["pc"], 0, [[512, 128], [0, 16], [1, 16]]), ALU.mult),
         ins=[("pc",), ("p2", 33)], outs=[("p2", 33)], tag=S + "simu")
    p.op("tensor",
         lambda e: e.matmul(m["pc"][0:1, 0:256], m["ones_col"][:, 0:1],
                            p2f[:, 8448:8704], start=True, stop=True),
         ins=[("p2", 33), ("c", "ones_col")], outs=[("pc",)], tag=S + "simcs2")
    p.op("vector", lambda e: e.tensor_copy(m["small"][0:1, 0:256], m["pc"][0:1, 0:256]),
         ins=[("pc",)], outs=[("small", 0)], tag=S + "csev")
    for step in (128, 64, 32, 16):
        p.op("vector",
             lambda e, step=step: e.tensor_add(m["small"][0:1, 0:step],
                                               m["small"][0:1, 0:step],
                                               m["small"][0:1, step:2 * step]),
             ins=[("small", 0)], outs=[("small", 0)], tag=S + f"cstr{step}")
    p.op("tensor",
         lambda e: e.matmul(m["pc"][:, 0:16], m["ones_row"][0:1, :],
                            m["small"][0:1, 0:16], start=True, stop=True),
         ins=[("small", 0), ("c", "ones_row")], outs=[("pc",)], tag=S + "csbc")
    p.op("vector", lambda e: e.tensor_copy(m["small"][:, 192:208], m["pc"][:, 0:16]),
         ins=[("pc",)], outs=[("small", 0)], tag=S + "csbc2")

    for j in range(NT):
        for cb in range(4):
            grp = PsumGroup()
            for c4 in range(4):
                ci = cb * 4 + c4
                fl = grp.next_flags(0)
                def ftr(e, ci=ci, c4=c4, j=j):
                    return e.transpose(m["pd"][:, c4 * 128:(c4 + 1) * 128],
                                       xtile(K, j)[:, ci * 128:(ci + 1) * 128],
                                       m["ident"][:, :])
                p.op("tensor", ftr, ins=[("x", j), ("c", "ident")], outs=[("pd",)],
                     tag=S + f"xt{j}_{ci}")
            p.op("scalar",
                 lambda e, cb=cb: e.activation(
                     p2f[:, cb * 512:(cb + 1) * 512],
                     m["pd"][:, :], AF.Identity),
                 ins=[("pd",)], outs=[("p2", cb)], tag=S + f"xte{j}_{cb}")
        grp = PsumGroup()
        for kc in range(16):
            fl = grp.next_flags(0)
            p.op("tensor",
                 lambda e, kc=kc, fl=fl:
                 e.matmul(m["pa"][0:16, 0:128],
                          p2f[:, 8448 + kc * 16: 8448 + (kc + 1) * 16],
                          p2f[:, kc * 128:(kc + 1) * 128],
                          start=fl["start"], stop=fl["stop"]),
                 ins=[("p2", 33)] + [("p2", cb) for cb in range(4)],
                 outs=[("pa",)], tag=S + f"lg{j}_{kc}")
        grp.close()
        p.op("vector", lambda e: e.tensor_copy(m["small"][0:16, 64:192], m["pa"][0:16, 0:128]),
             ins=[("pa",)], outs=[("small", 0)], tag=S + f"lgev{j}")
        p.op("tensor",
             lambda e: e.transpose(m["pd"][:, 0:16], m["small"][0:16, 64:192],
                                   m["ident"][0:16, 0:16]),
             ins=[("small", 0), ("c", "ident")], outs=[("pd",)], tag=S + f"lgtr{j}")
        L = 512
        p.op("scalar",
             lambda e, j=j: e.activation(m["stat"][:, 448 + j:449 + j],
                                         m["stat"][:, j * 2 + 1:j * 2 + 2], AF.Ln),
             ins=[("mv", j)], outs=[("lnvr", j)], tag=S + f"lnv{j}")
        p.op("scalar",
             lambda e, j=j: e.activation(m["stat"][:, 456 + j:457 + j],
                                         m["stat"][:, 448 + j:449 + j], AF.Exp,
                                         scale=-0.5),
             ins=[("lnvr", j)], outs=[("rln", j)], tag=S + f"rln{j}")
        p.op("vector",
             lambda e, j=j: e.tensor_scalar(out=m["stat"][:, 464 + j:465 + j],
                                            in0=m["stat"][:, j * 2:j * 2 + 1],
                                            scalar1=-1.0, scalar2=None, op0=ALU.mult),
             ins=[("mv", j)], outs=[("negm", j)], tag=S + f"negm{j}")
        p.op("vector",
             lambda e, j=j, L=L: e.scalar_tensor_tensor(
                 out=m["small"][:, L:L + 16], in0=m["small"][:, 192:208],
                 scalar=m["stat"][:, 464 + j:465 + j], in1=m["pd"][:, 0:16],
                 op0=ALU.mult, op1=ALU.add),
             ins=[("small", 0), ("negm", j), ("pd",)], outs=[("small", 2)],
             tag=S + f"lgc{j}")
        p.op("vector",
             lambda e, j=j, L=L: e.tensor_scalar(
                 out=m["small"][:, L:L + 16], in0=m["small"][:, L:L + 16],
                 scalar1=m["stat"][:, 456 + j:457 + j],
                 scalar2=float(2048.0 ** -0.5), op0=ALU.mult, op1=ALU.mult),
             ins=[("small", 2), ("rln", j)], outs=[("small", 2)], tag=S + f"lgs{j}")
        p.op("vector",
             lambda e, L=L: e.tensor_sub(m["small"][:, L:L + 16],
                                         m["small"][:, L:L + 16], m["sgbc"][:, :]),
             ins=[("small", 2), ("sgbc",)], outs=[("small", 2)], tag=S + f"lgsg{j}")
        p.op("vector",
             lambda e, L=L: e.tensor_relu(m["small"][:, L + 16:L + 32], m["small"][:, L:L + 16]),
             ins=[("small", 2)], outs=[("small", 2)], tag=S + f"rl{j}")
        p.op("vector",
             lambda e, L=L: e.tensor_scalar(out=m["small"][:, L + 32:L + 48],
                                            in0=m["small"][:, L + 16:L + 32],
                                            scalar1=0.0, scalar2=None, op0=ALU.is_gt),
             ins=[("small", 2)], outs=[("small", 2)], tag=S + f"mk{j}")
        p.op("vector",
             lambda e, L=L: e.reduce_sum(m["small"][:, L + 48:L + 49],
                                         m["small"][:, L + 32:L + 48], axis=AX.X),
             ins=[("small", 2)], outs=[("small", 2)], tag=S + f"rs{j}")
        p.op("vector",
             lambda e, L=L: e.tensor_scalar(out=m["small"][:, L + 48:L + 49],
                                            in0=m["small"][:, L + 48:L + 49],
                                            scalar1=0.5, scalar2=None, op0=ALU.is_lt),
             ins=[("small", 2)], outs=[("small", 2)], tag=S + f"ia{j}")
        p.op("vector",
             lambda e, L=L: e.reduce_max(m["small"][:, L + 49:L + 50],
                                         m["small"][:, L:L + 16], axis=AX.X),
             ins=[("small", 2)], outs=[("small", 2)], tag=S + f"m1{j}")
        p.op("vector",
             lambda e, L=L: e.tensor_scalar(out=m["small"][:, L + 64:L + 80],
                                            in0=m["small"][:, L:L + 16],
                                            scalar1=m["small"][:, L + 49:L + 50],
                                            scalar2=None, op0=ALU.is_ge),
             ins=[("small", 2)], outs=[("small", 2)], tag=S + f"ge1{j}")
        p.op("vector",
             lambda e, L=L: e.scalar_tensor_tensor(
                 out=m["small"][:, L + 80:L + 96], in0=m["small"][:, L + 64:L + 80],
                 scalar=-2.0e9, in1=m["small"][:, L:L + 16],
                 op0=ALU.mult, op1=ALU.add),
             ins=[("small", 2)], outs=[("small", 2)], tag=S + f"msk2{j}")
        p.op("vector",
             lambda e, L=L: e.reduce_max(m["small"][:, L + 50:L + 51],
                                         m["small"][:, L + 80:L + 96], axis=AX.X),
             ins=[("small", 2)], outs=[("small", 2)], tag=S + f"m2{j}")
        p.op("vector",
             lambda e, L=L: e.tensor_scalar(out=m["small"][:, L + 96:L + 112],
                                            in0=m["small"][:, L:L + 16],
                                            scalar1=m["small"][:, L + 50:L + 51],
                                            scalar2=None, op0=ALU.is_ge),
             ins=[("small", 2)], outs=[("small", 2)], tag=S + f"fb{j}")
        p.op("vector",
             lambda e, L=L: e.tensor_sub(m["small"][:, L + 96:L + 112],
                                         m["small"][:, L + 96:L + 112],
                                         m["small"][:, L + 32:L + 48]),
             ins=[("small", 2)], outs=[("small", 2)], tag=S + f"fmd{j}")
        p.op("vector",
             lambda e, L=L: e.scalar_tensor_tensor(
                 out=m["small"][:, L + 32:L + 48],
                 in0=m["small"][:, L + 96:L + 112],
                 scalar=m["small"][:, L + 48:L + 49],
                 in1=m["small"][:, L + 32:L + 48],
                 op0=ALU.mult, op1=ALU.add),
             ins=[("small", 2)], outs=[("small", 2)], tag=S + f"fm{j}")
        p.op("vector",
             lambda e, L=L: e.tensor_scalar(
                 out=m["small"][:, L + 96:L + 112], in0=m["small"][:, L + 32:L + 48],
                 scalar1=1.0e9, scalar2=-1.0e9, op0=ALU.mult, op1=ALU.add),
             ins=[("small", 2)], outs=[("small", 2)], tag=S + f"mlt{j}")
        p.op("vector",
             lambda e, L=L: e.tensor_mul(m["small"][:, L + 112:L + 128],
                                         m["small"][:, L + 16:L + 32],
                                         m["small"][:, L + 32:L + 48]),
             ins=[("small", 2)], outs=[("small", 2)], tag=S + f"mlm{j}")
        p.op("vector",
             lambda e, L=L: e.tensor_add(m["small"][:, L + 112:L + 128],
                                         m["small"][:, L + 112:L + 128],
                                         m["small"][:, L + 96:L + 112]),
             ins=[("small", 2)], outs=[("small", 2)], tag=S + f"ml{j}")
        p.op("scalar",
             lambda e, L=L: e.activation(m["small"][:, L + 112:L + 128],
                                         m["small"][:, L + 112:L + 128], AF.Exp),
             ins=[("small", 2)], outs=[("small", 2)], tag=S + f"sme{j}")
        p.op("vector",
             lambda e, L=L: e.reduce_sum(m["small"][:, L + 51:L + 52],
                                         m["small"][:, L + 112:L + 128], axis=AX.X),
             ins=[("small", 2)], outs=[("small", 2)], tag=S + f"smz{j}")
        p.op("vector",
             lambda e, L=L: e.reciprocal(m["small"][:, L + 51:L + 52],
                                         m["small"][:, L + 51:L + 52]),
             ins=[("small", 2)], outs=[("small", 2)], tag=S + f"smr{j}")
        p.op("vector",
             lambda e, L=L, j=j: e.scalar_tensor_tensor(
                 out=m["stat"][:, 320 + j * 16: 336 + j * 16],
                 in0=m["small"][:, L + 112:L + 128],
                 scalar=m["small"][:, L + 51:L + 52],
                 in1=m["small"][:, L + 32:L + 48],
                 op0=ALU.mult, op1=ALU.mult),
             ins=[("small", 2)], outs=[("wrt", j)], tag=S + f"w{j}")
        p.op("scalar",
             lambda e, j=j: e.activation(m["wrtb"][:, j * 16:(j + 1) * 16],
                                         m["stat"][:, 320 + j * 16: 336 + j * 16],
                                         AF.Identity),
             ins=[("wrt", j)], outs=[("wrtb",)], tag=S + f"wb{j}")

    for g in range(2):
        emit_hfm(K, groups=(g,), tag=S + f"hfm{g}_")
        for tt in range(4):
            j = 4 * g + tt
            for pi, (pnm, wname) in enumerate((("q", "qcat"), ("k", "kcat"), ("v", "vcat"))):
                grp = PsumGroup()
                for kc in range(16):
                    slot = kc % 2
                    def fw(e, kc=kc, slot=slot, wname=wname):
                        src = bass.AP(d[wname], kc * 128 * 2048, [[2048, 128], [1, 2048]])
                        return e.dma_start(out=m["wb"][:, slot * 2048:(slot + 1) * 2048], in_=src)
                    p.dma("gpsimd", fw, outs=[("wb", slot)], tag=S + f"lw{pnm}{j}_{kc}")
                    for nn in range(4):
                        fl = grp.next_flags(nn)
                        p.op("tensor",
                             lambda e, kc=kc, tt=tt, nn=nn, slot=slot, fl=fl:
                             e.matmul(m["pbig"][:, nn * 512:(nn + 1) * 512],
                                      m["p3"][:, kc * 512 + tt * 128: kc * 512 + tt * 128 + 128],
                                      m["wb"][:, slot * 2048 + nn * 512: slot * 2048 + (nn + 1) * 512],
                                      start=fl["start"], stop=fl["stop"]),
                             ins=[("wb", slot), ("p3", kc)], outs=[("pbig",)],
                             tag=S + f"pe{pnm}{j}_{kc}_{nn}")
                grp.close()
                acol = 1536 + pi * 256
                for ei in range(16):
                    if ei == 0:
                        p.op("vector",
                             lambda e, j=j, acol=acol: e.tensor_scalar(
                                 out=m["small"][:, acol:acol + 128],
                                 in0=m["pbig"][:, 0:128],
                                 scalar1=m["stat"][:, 320 + j * 16:321 + j * 16],
                                 scalar2=None, op0=ALU.mult),
                             ins=[("pbig",), ("wrt", j)], outs=[("small", 6 + pi)],
                             tag=S + f"wr{pnm}{j}_0")
                    else:
                        p.op("vector",
                             lambda e, ei=ei, j=j, acol=acol: e.scalar_tensor_tensor(
                                 out=m["small"][:, acol:acol + 128],
                                 in0=m["pbig"][:, ei * 128:(ei + 1) * 128],
                                 scalar=m["stat"][:, 320 + j * 16 + ei:321 + j * 16 + ei],
                                 in1=m["small"][:, acol:acol + 128],
                                 op0=ALU.mult, op1=ALU.add),
                             ins=[("pbig",), ("wrt", j), ("small", 6 + pi)],
                             outs=[("small", 6 + pi)], tag=S + f"wr{pnm}{j}_{ei}")
                p.op("scalar",
                     lambda e, j=j, pi=pi, acol=acol: e.activation(
                         m["p1"][:, pi * 1024 + j * 128: pi * 1024 + (j + 1) * 128],
                         m["small"][:, acol:acol + 128], AF.Identity),
                     ins=[("small", 6 + pi)], outs=[("p1", pi // 2)],
                     tag=S + f"cv{pnm}{j}")

    for pi, pnm in ((0, "q"), (1, "k")):
        for j in range(NT):
            def ftr2(e, pi=pi, j=j):
                pdb = m["pd"][:, :].bitcast(BF16)
                return e.transpose(pdb[:, 0:128],
                                   m["p1"][:, pi * 1024 + j * 128: pi * 1024 + (j + 1) * 128],
                                   m["identb"][:, :])
            p.op("tensor", ftr2, ins=[("p1", pi // 2), ("c", "identb")],
                 outs=[("pd",)], tag=S + f"tf{pnm}{j}")
            p.op("scalar",
                 lambda e, pi=pi, j=j: e.activation(
                     m["p1"][:, 4096 + pi * 1024 + j * 128: 4096 + pi * 1024 + (j + 1) * 128],
                     m["pd"][:, :].bitcast(BF16)[:, 0:128], AF.Identity),
                 ins=[("pd",)], outs=[("p1", 2)], tag=S + f"tfe{pnm}{j}")

    def fag1(e):
        dst = d["agin_s"].ap()[0:131072].rearrange("(a b) -> a b", b=1024)
        return e.dma_start(out=dst, in_=m["p1"][:, 5120:6144])
    p.dma("sync", fag1, ins=[("p1", 2)], outs=[("agins", 0)], tag=S + "agk")
    def fag2(e):
        dst = bass.AP(d["agin_s"], 131072, [[128, 128], [16384, 8], [1, 128]])
        src = bass.AP(m.t["p1"], 2048, [[16384, 128], [128, 8], [1, 128]])
        return e.dma_start(out=dst, in_=src)
    p.dma("sync", fag2, ins=[("p1", 1)], outs=[("agins", 1)], tag=S + "agv")
    p.collective(
        lambda e: e.collective_compute(
            "AllGather", ALU.bypass,
            replica_groups=[[0, 1], [2, 3], [4, 5], [6, 7]],
            ins=[d["agin_s"].ap()], outs=[d["agout_s"].ap()]),
        ins=[("agins", 0), ("agins", 1)], outs=[("agouts",)], tag=S + "ag")

    for j in range(NT):
        grp_o = PsumGroup()
        grp_z = PsumGroup()
        for sg in range(2 * j + 2):
            par, jloc = sg % 2, sg // 2
            kslot = sg % 2
            def fk2(e, par=par, jloc=jloc, kslot=kslot):
                src = d["agout_s"].ap()[par:par + 1].squeeze(0)[0:131072] \
                    .rearrange("(a b) -> a b", b=1024)[:, jloc * 128:(jloc + 1) * 128]
                return e.dma_start(out=m["wb"][:, kslot * 2048: kslot * 2048 + 128], in_=src)
            p.dma("sync", fk2, ins=[("agouts",)], outs=[("wb", kslot)], tag=S + f"kl{j}_{sg}")
            def fv2(e, par=par, jloc=jloc, kslot=kslot):
                src = d["agout_s"].ap()[par:par + 1].squeeze(0)[131072:262144] \
                    .rearrange("(a b c) -> a b c", a=8, b=128)[jloc:jloc + 1].squeeze(0)
                return e.dma_start(out=m["wb"][:, (2 + kslot) * 2048: (2 + kslot) * 2048 + 128], in_=src)
            p.dma("gpsimd", fv2, ins=[("agouts",)], outs=[("wb", 2 + kslot)], tag=S + f"vl{j}_{sg}")
            pslot = sg % 2
            ps = "pb" if pslot == 0 else "pc"
            ptk = ("p3", 12 + pslot)
            ptcol = 6144 + pslot * 512
            p.op("tensor",
                 lambda e, j=j, kslot=kslot, ps=ps:
                 e.matmul(m[ps][:, 0:128],
                          m["wb"][:, kslot * 2048: kslot * 2048 + 128],
                          m["p1"][:, 4096 + j * 128: 4096 + (j + 1) * 128],
                          start=True, stop=True),
                 ins=[("wb", kslot), ("p1", 2)], outs=[(ps,)], tag=S + f"s{j}_{sg}")
            if sg == 2 * j or sg == 2 * j + 1:
                mk = "mAa" if sg == 2 * j else "mBa"
                p.op("vector",
                     lambda e, ps=ps, mk=mk: e.tensor_tensor(
                         m[ps][:, 0:128], m[ps][:, 0:128], m[mk][:, :], ALU.add),
                     ins=[(ps,), ("c", mk)], outs=[(ps,)], tag=S + f"mka{j}_{sg}")
            p.op("scalar",
                 lambda e, ps=ps, ptcol=ptcol: e.activation(
                     m["p3"][:, ptcol:ptcol + 128], m[ps][:, 0:128], AF.Exp, scale=SQ_H),
                 ins=[(ps,)], outs=[ptk], tag=S + f"ex{j}_{sg}")
            fo = grp_o.next_flags(0)
            p.op("tensor",
                 lambda e, ptcol=ptcol, kslot=kslot, fo=fo:
                 e.matmul(m["pa"][:, 0:128],
                          m["p3"][:, ptcol:ptcol + 128],
                          m["wb"][:, (2 + kslot) * 2048: (2 + kslot) * 2048 + 128],
                          start=fo["start"], stop=fo["stop"]),
                 ins=[ptk, ("wb", 2 + kslot)], outs=[("pa",)], tag=S + f"o{j}_{sg}")
            fz = grp_z.next_flags(0)
            p.op("tensor",
                 lambda e, ptcol=ptcol, fz=fz:
                 e.matmul(m["pd"][:, 0:1],
                          m["p3"][:, ptcol:ptcol + 128],
                          m["ones_colb"][:, :],
                          start=fz["start"], stop=fz["stop"]),
                 ins=[ptk, ("ones_colb",)], outs=[("pd",)], tag=S + f"zz{j}_{sg}")
        grp_o.close()
        grp_z.close()
        p.op("vector",
             lambda e: e.tensor_copy(m["small"][:, 272:273], m["pd"][:, 0:1]),
             ins=[("pd",)], outs=[("small", 1)], tag=S + f"zc{j}")
        p.op("vector",
             lambda e: e.reciprocal(m["small"][:, 272:273], m["small"][:, 272:273]),
             ins=[("small", 1)], outs=[("small", 1)], tag=S + f"zr{j}")
        p.op("scalar",
             lambda e, j=j: e.activation(
                 m["p1"][:, 3072 + j * 128: 3072 + (j + 1) * 128],
                 m["pa"][:, 0:128], AF.Identity,
                 scale=m["small"][:, 272:273]),
             ins=[("pa",), ("small", 1)], outs=[("p1", 1)], tag=S + f"aoev{j}")

    for j in range(NT):
        def fz(e, j=j):
            in0 = bass.AP(m.t["p1"], 3072 + j * 128, [[16384, 128], [0, 16], [1, 128]])
            in1 = bass.AP(m.t["wrtb"], j * 16, [[128, 128], [1, 16], [0, 128]])
            return e.tensor_tensor(m["p2"][:, j * 2048:(j + 1) * 2048], in0, in1, ALU.mult)
        p.op("vector", fz, ins=[("p1", 1), ("wrtb",)],
             outs=[("p2", 4 * j + i) for i in range(4)], tag=S + f"zt{j}")
        def fzt(e, j=j):
            src = m["p2"][:, j * 2048:(j + 1) * 2048]
            dst = bass.AP(m.t["p2"], 16384 + j * 128, [[32768, 128], [1024, 16], [1, 128]])
            return e.dma_start_transpose(dst, src)
        p.dma("sync", fzt, ins=[("p2", 4 * j + i) for i in range(4)],
              outs=[("p2", 32 + i) for i in range(32)], tag=S + f"ztt{j}")
    for j in range(NT):
        grp = PsumGroup()
        for kc in range(16):
            slot = kc % 2
            def fwo(e, kc=kc, slot=slot):
                src = bass.AP(d["ocat"], kc * 128 * 2048, [[2048, 128], [1, 2048]])
                return e.dma_start(out=m["wb"][:, slot * 2048:(slot + 1) * 2048], in_=src)
            p.dma("gpsimd", fwo, outs=[("wb", slot)], tag=S + f"lwo{j}_{kc}")
            for nn in range(4):
                fl = grp.next_flags(nn)
                p.op("tensor",
                     lambda e, kc=kc, nn=nn, slot=slot, j=j, fl=fl:
                     e.matmul(m["pbig"][:, nn * 512:(nn + 1) * 512],
                              bass.AP(m.t["p2"], 16384 + kc * 1024 + j * 128,
                                      [[32768, 128], [1, 128]]),
                              m["wb"][:, slot * 2048 + nn * 512: slot * 2048 + (nn + 1) * 512],
                              start=fl["start"], stop=fl["stop"]),
                     ins=[("wb", slot), ("p2", 32 + 2 * kc), ("p2", 33 + 2 * kc)],
                     outs=[("pbig",)], tag=S + f"op{j}_{kc}_{nn}")
        grp.close()
        p.op("vector",
             lambda e, j=j: e.scalar_tensor_tensor(
                 out=xtile(K, j), in0=m["pbig"][:, :], scalar=0.5,
                 in1=xtile(K, j), op0=ALU.mult, op1=ALU.add),
             ins=[("pbig",), ("x", j)], outs=[("x", j)], tag=S + f"res{j}")


MB = {
    "mbcd": 4096, "mdtr": 4728, "mdts": 5240, "mbcs": 5752, "msta": 7064,
    "tail0": 7576, "tail1": 7972,
}
MGT, MPT, MBTMB = 8192, 8320, 8448


def p2b(K, col, n):
    return K.mem["p2"][:, col:col + n]


def p1f(K, col, n):
    return K.mem.t["p1"].ap().bitcast(F32)[:, col:col + n]


def p1b(K, col, n):
    return K.mem["p1"][:, col:col + n]


def emit_l1attn(K, site):
    p, m, d = K.prog, K.mem, K.dram
    S = "s_"
    p.dma("sync", lambda e: e.dma_start(out=m["mabd"][64:128, 0:1], in_=d["alog"].ap()),
          outs=[("mabd",)], tag=S + "lda")
    p.dma("sync", lambda e: e.dma_start(out=m["mabd"][64:128, 1:2], in_=d["dtbias"].ap()),
          outs=[("mabd",)], tag=S + "ldb")
    p.op("scalar", lambda e: e.activation(m["mabd"][64:128, 2:3], m["mabd"][64:128, 0:1], AF.Exp),
         ins=[("mabd",)], outs=[("mabd",)], tag=S + "expA")
    p.op("vector", lambda e: e.tensor_scalar(out=m["mabd"][64:128, 2:3],
                                             in0=m["mabd"][64:128, 2:3],
                                             scalar1=-1.0, scalar2=None, op0=ALU.mult),
         ins=[("mabd",)], outs=[("mabd",)], tag=S + "negA")
    for nm in ("mcwx", "mcwbc", "mcbx", "mcbbc", "mnw", "tailw_bc", "parw_bc"):
        p.dma("sync", lambda e, nm=nm: e.dma_start(out=m[nm][:, :], in_=d[nm].ap()),
              outs=[(nm,)], tag=S + "ld" + nm)
    p.dma("sync", lambda e: e.dma_start(out=m["mrow"][0:1, :], in_=d["mrow"].ap()),
          outs=[("mrow",)], tag=S + "ldmrow")
    p.op("tensor",
         lambda e: e.matmul(m["pd"][:, 0:64], m["ones_row"][0:1, :],
                            m["mrow"][0:1, 0:64], start=True, stop=True),
         ins=[("mrow",), ("c", "ones_row")], outs=[("pd",)], tag=S + "dskbc")
    p.op("vector", lambda e: e.tensor_copy(m["small"][:, 2048:2112], m["pd"][:, 0:64]),
         ins=[("pd",)], outs=[("small", 8)], tag=S + "dskev")
    p.op("vector", lambda e: e.memset(
        m.t["p2"].ap().bitcast(F32)[0:16, 0:4096], 0.0),
        outs=[("p2", i) for i in range(16)], tag=S + "zeroT")
    p.dma("sync", lambda e: e.dma_start(out=d["msdram"].ap(),
                                        in_=m.t["p2"].ap().bitcast(F32)[0:16, 0:4096]),
          ins=[("p2", i) for i in range(16)], outs=[("msdram",)], tag=S + "savT0")
    for g in range(2):
        _mamba_group(K, site, g)
        if DBG and g == 0:
            p.dma("sync", lambda e: e.dma_start(out=d["dbg"].ap()[:, 0:8192],
                                                in_=m["x"][:, 0:8192]),
                  ins=[("x", j) for j in range(4)], outs=[("dbgt3",)],
                  tag=S + "tap3")
            p.dma("sync",
                  lambda e: e.dma_start(out=d["dbg"].ap()[0:16, 8192:12288],
                                        in_=d["msdram"].ap()),
                  ins=[("msdram",)], outs=[("dbgt4",)], tag=S + "tap4")


def _mamba_group(K, site, g):
    p, m, d = K.prog, K.mem, K.dram
    S = f"s{g}_"
    p2f = m.t["p2"].ap().bitcast(F32)
    emit_ln(K, tiles=range(4 * g, 4 * g + 4), colbase=-8192 * g, tag=S + "ln")
    emit_hfm(K, groups=(g,), src_colbase=-8192 * g, tag=S + "hfm")

    for tt in range(4):
        for half in range(2):
            grp = PsumGroup()
            for kc in range(16):
                slot = kc % 2
                def fw(e, kc=kc, slot=slot, half=half):
                    src = bass.AP(d["winp"], kc * 128 * 8320 + half * 2048,
                                  [[8320, 128], [1, 2048]])
                    return e.dma_start(out=m["wb"][:, slot * 2048:(slot + 1) * 2048], in_=src)
                p.dma("gpsimd", fw, outs=[("wb", slot)], tag=S + f"lwg{tt}_{half}_{kc}")
                for nn in range(4):
                    fl = grp.next_flags(nn)
                    p.op("tensor",
                         lambda e, kc=kc, tt=tt, nn=nn, slot=slot, fl=fl:
                         e.matmul(m["pbig"][:, nn * 512:(nn + 1) * 512],
                                  m["p3"][:, kc * 512 + tt * 128: kc * 512 + tt * 128 + 128],
                                  m["wb"][:, slot * 2048 + nn * 512: slot * 2048 + (nn + 1) * 512],
                                  start=fl["start"], stop=fl["stop"]),
                         ins=[("wb", slot), ("p3", kc)], outs=[("pbig",)],
                         tag=S + f"gmm{tt}_{half}_{kc}_{nn}")
            grp.close()
            p.op("scalar",
                 lambda e, half=half: e.activation(
                     m["wb"][:, 4096 + half * 2048: 4096 + (half + 1) * 2048],
                     m["pbig"][:, :], AF.Silu),
                 ins=[("pbig",)], outs=[("wb", 2 + half)], tag=S + f"gsilu{tt}_{half}")
        def fgd(e, tt=tt, g=g):
            return e.dma_start(out=d["gdram"].ap()[4 * g + tt: 4 * g + tt + 1].squeeze(0),
                               in_=m["wb"][:, 4096:8192])
        p.dma("sync", fgd, ins=[("wb", 2), ("wb", 3)], outs=[("gdram", 4 * g + tt)],
              tag=S + f"gst{tt}")
    for mc in range(32):
        slot = mc % 2
        def fw2(e, mc=mc, slot=slot):
            src = bass.AP(d["winp"], 4096 + mc * 128,
                          [[8320, 128], [128 * 8320, 16], [1, 128]])
            return e.dma_start(out=m["wb"][:, slot * 2048:(slot + 1) * 2048], in_=src)
        p.dma("gpsimd", fw2, outs=[("wb", slot)], tag=S + f"lwx{mc}")
        ps = "pa" if slot == 0 else "pb"
        grp = PsumGroup()
        for kc in range(16):
            fl = grp.next_flags(0)
            p.op("tensor",
                 lambda e, kc=kc, slot=slot, ps=ps, fl=fl:
                 e.matmul(m[ps][:, :],
                          m["wb"][:, slot * 2048 + kc * 128: slot * 2048 + (kc + 1) * 128],
                          m["p3"][:, kc * 512:(kc + 1) * 512],
                          start=fl["start"], stop=fl["stop"]),
                 ins=[("wb", slot), ("p3", kc)], outs=[(ps,)], tag=S + f"xmm{mc}_{kc}")
        grp.close()
        p.op("scalar",
             lambda e, ps=ps, mc=mc: e.activation(m["p2"][:, mc * 512:(mc + 1) * 512],
                                                  m[ps][:, :], AF.Identity),
             ins=[(ps,)], outs=[("p2", mc)], tag=S + f"xev{mc}")
    grp = PsumGroup()
    for kc in range(16):
        slot = kc % 2
        def fw3(e, kc=kc, slot=slot):
            src = bass.AP(d["winp"], 8192 + kc * 128 * 8320, [[8320, 128], [1, 128]])
            return e.dma_start(out=m["wb"][:, slot * 2048: slot * 2048 + 128], in_=src)
        p.dma("gpsimd", fw3, outs=[("wb", slot)], tag=S + f"lwb{kc}")
        for nn in range(4):
            fl = grp.next_flags(0)
            p.op("tensor",
                 lambda e, kc=kc, nn=nn, slot=slot, fl=fl:
                 e.matmul(m["pc"][:, nn * 128:(nn + 1) * 128],
                          m["wb"][:, slot * 2048: slot * 2048 + 128],
                          m["p3"][:, kc * 512 + nn * 128: kc * 512 + (nn + 1) * 128],
                          start=fl["start"], stop=fl["stop"]),
                 ins=[("wb", slot), ("p3", kc)], outs=[("pc",)], tag=S + f"bmm{kc}_{nn}")
    grp.close()
    for tc in range(4):
        p.op("scalar",
             lambda e, tc=tc: e.activation(
                 p1f(K, MB["mbcd"] + tc * 133 + 3, 128)[0:48, :],
                 m["pc"][0:48, tc * 128:(tc + 1) * 128], AF.Identity),
             ins=[("pc",)], outs=[("mbcd",)], tag=S + f"bcev{tc}")
    p.op("scalar",
         lambda e: e.activation(p1f(K, MB["mdtr"], 512)[64:128, :],
                                m["pc"][64:128, :], AF.Identity),
         ins=[("pc",)], outs=[("mdtr",)], tag=S + "dtev")

    for tc in range(4):
        def ft1(e, tc=tc):
            src = bass.AP(m.t["p2"], tc * 128 + 125, [[32768, 128], [512, 32], [1, 3]])
            dst = bass.AP(d["agin_t"], tc * 33 * 128 * 4, [[4, 128], [128 * 4, 32], [1, 3]])
            return e.dma_start(out=dst, in_=src)
        p.dma("sync", ft1, ins=[("p2", i) for i in range(32)], outs=[("agint", tc, 0)],
              tag=S + f"tl{tc}")
        def ft2(e, tc=tc):
            src = p1f(K, MB["mbcd"] + tc * 133 + 128, 3)
            dst = bass.AP(d["agin_t"], (tc * 33 + 32) * 128 * 4, [[4, 128], [1, 3]])
            return e.dma_start(out=dst, in_=src)
        p.dma("gpsimd", ft2, ins=[("mbcd",)], outs=[("agint", tc, 1)], tag=S + f"tlb{tc}")
    agout_t = "agout_tA" if g == 0 else "agout_tB"
    p.collective(
        lambda e, agout_t=agout_t: e.collective_compute(
            "AllGather", ALU.bypass,
            replica_groups=[[0, 1], [2, 3], [4, 5], [6, 7]],
            ins=[d["agin_t"].ap()], outs=[d[agout_t].ap()]),
        ins=[("agint", tc, i) for tc in range(4) for i in range(2)],
        outs=[(agout_t,)], tag=S + "agt")

    for tc in range(4):
        j = 4 * g + tc
        agA, idxA = agout_t, tc
        if tc > 0:
            agB, idxB = agout_t, tc - 1
        elif g == 1:
            agB, idxB = "agout_tA", 3
        else:
            agB, idxB = agout_t, 0
        def flA2(e, agA=agA, idxA=idxA):
            src = bass.AP(d[agA], idxA * 33 * 128 * 4, [[4, 128], [128 * 4, 33], [1, 3]])
            dst = p1f(K, MB["tail1"], 99).rearrange("p (a b) -> p a b", b=3)
            return e.dma_start(out=dst, in_=src)
        p.dma("gpsimd", flA2, ins=[(agA,)], outs=[("tailc",)], tag=S + f"tbA{tc}")
        p.op("vector",
             lambda e, j=j, tc=tc: e.tensor_scalar(
                 out=p1f(K, MB["tail0"] + tc * 99, 99), in0=p1f(K, MB["tail1"], 99),
                 scalar1=m["tailw_bc"][:, 2 * j:2 * j + 1], scalar2=None, op0=ALU.mult),
             ins=[("tailc",), ("tailw_bc",)], outs=[("tails",)], tag=S + f"tbs{tc}")
        def flB(e, agB=agB, idxB=idxB):
            src = bass.AP(d[agB], 4 * 33 * 128 * 4 + idxB * 33 * 128 * 4,
                          [[4, 128], [128 * 4, 33], [1, 3]])
            dst = p1f(K, MB["tail1"], 99).rearrange("p (a b) -> p a b", b=3)
            return e.dma_start(out=dst, in_=src)
        p.dma("gpsimd", flB, ins=[(agB,), ("tails",)], outs=[("tailc",)], tag=S + f"tbB{tc}")
        p.op("vector",
             lambda e, j=j, tc=tc: e.scalar_tensor_tensor(
                 out=p1f(K, MB["tail0"] + tc * 99, 99), in0=p1f(K, MB["tail1"], 99),
                 scalar=m["tailw_bc"][:, 2 * j + 1:2 * j + 2],
                 in1=p1f(K, MB["tail0"] + tc * 99, 99), op0=ALU.mult, op1=ALU.add),
             ins=[("tailc",), ("tails",), ("tailw_bc",)], outs=[("tails",)],
             tag=S + f"tbb{tc}")
        p.op("vector",
             lambda e, tc=tc: e.tensor_copy(
                 p1f(K, MB["mbcd"] + tc * 133, 3)[0:48, :],
                 p1f(K, MB["tail0"] + tc * 99 + 96, 3)[0:48, :]),
             ins=[("tails",)], outs=[("mbcd",)], tag=S + f"bcpre{tc}")

    for mc in range(32):
        ps = "pa" if mc % 2 == 0 else "pb"
        pkeys = [("p2", mc)]
        for k in range(4):
            nb = 3 - k
            if nb > 0:
                def fbnd(e, k=k, mc=mc, nb=nb, ps=ps, first=(k == 0)):
                    out = bass.AP(m.t[ps], 0, [[512, 128], [128, 4], [1, nb]])
                    in0 = bass.AP(m.t["p1"], 2 * (MB["tail0"] + mc * 3 + k),
                                  [[16384, 128], [99 * 2, 4], [2, nb]]).bitcast(F32) \
                        if False else None
                    in0 = p1f(K, MB["tail0"], 396).rearrange(
                        "p (t a b) -> p t a b", t=4, b=3)[:, :, mc, k:3]
                    if first:
                        in1 = m["mcbx"][:, mc:mc + 1].broadcast_to([128, 4, nb]) \
                            if False else bass.AP(m.t["mcbx"], mc, [[32, 128], [0, 4], [0, nb]])
                    else:
                        in1 = bass.AP(m.t[ps], 0, [[512, 128], [128, 4], [1, nb]])
                    return e.scalar_tensor_tensor(
                        out=out, in0=in0, scalar=m["mcwx"][:, mc * 4 + k: mc * 4 + k + 1],
                        in1=in1, op0=ALU.mult, op1=ALU.add)
                p.op("vector", fbnd, ins=[("tails",), ("mcwx",), ("mcbx",), (ps,)],
                     outs=[(ps,)], tag=S + f"cvb{mc}_{k}")
            def fmain(e, k=k, mc=mc, nb=nb, ps=ps, first=(k == 0)):
                out = bass.AP(m.t[ps], nb, [[512, 128], [128, 4], [1, 128 - nb]])
                in0 = bass.AP(m.t["p2"], mc * 512, [[32768, 128], [128, 4], [1, 128 - nb]])
                if first:
                    in1 = bass.AP(m.t["mcbx"], mc, [[32, 128], [0, 4], [0, 128 - nb]])
                else:
                    in1 = out
                return e.scalar_tensor_tensor(
                    out=out, in0=in0, scalar=m["mcwx"][:, mc * 4 + k: mc * 4 + k + 1],
                    in1=in1, op0=ALU.mult, op1=ALU.add)
            p.op("vector", fmain, ins=pkeys + [("mcwx",), ("mcbx",), (ps,)],
                 outs=[(ps,)], tag=S + f"cvm{mc}_{k}")
        p.op("scalar",
             lambda e, mc=mc, ps=ps: e.activation(
                 bass.AP(m.t["p1"], mc * 128, [[16384, 128], [4096, 2], [1, 128]]),
                 m[ps][:, 0:256], AF.Silu),
             ins=[(ps,)], outs=[("p1", 0), ("p1", 1), ("p1", 2), ("p1", 3)],
             tag=S + f"cse{mc}_a")
        p.op("scalar",
             lambda e, mc=mc, ps=ps: e.activation(
                 bass.AP(m.t["p3"], mc * 128, [[8192, 128], [4096, 2], [1, 128]]),
                 m[ps][:, 256:512], AF.Silu),
             ins=[(ps,)], outs=[("p3", i) for i in range(16)],
             tag=S + f"cse{mc}_b")

    for tc in range(4):
        j = 4 * g + tc
        for k in range(4):
            def fbc(e, k=k, tc=tc, first=(k == 0)):
                out = m["pd"][0:48, 0:128]
                in0 = p1f(K, MB["mbcd"] + tc * 133 + k, 128)[0:48, :]
                if first:
                    in1 = bass.AP(m.t["mcbbc"], 0, [[1, 128], [0, 128]])[0:48, :]
                else:
                    in1 = out
                return e.scalar_tensor_tensor(
                    out=out, in0=in0, scalar=m["mcwbc"][0:48, k:k + 1],
                    in1=in1, op0=ALU.mult, op1=ALU.add)
            p.op("vector", fbc, ins=[("mbcd",), ("mcwbc",), ("mcbbc",), ("pd",)],
                 outs=[("pd",)], tag=S + f"cvbc{tc}_{k}")
        p.op("scalar",
             lambda e, tc=tc: e.activation(p1f(K, MB["mbcs"] + tc * 128, 128)[0:48, :],
                                           m["pd"][0:48, 0:128], AF.Silu),
             ins=[("pd",)], outs=[("mbcs",)], tag=S + f"bcsilu{tc}")
        p.op("tensor",
             lambda e, tc=tc: e.transpose(m["pd"][:, 64:80],
                                          p1f(K, MB["mbcs"] + tc * 128, 128)[32:48, :],
                                          m["ident"][32:48, 32:48]),
             ins=[("mbcs",), ("c", "ident")], outs=[("pd",)], tag=S + f"ct1{tc}")
        p.op("scalar",
             lambda e: e.activation(p1f(K, MB["mbcs"] + 1024, 16),
                                    m["pd"][:, 64:80], AF.Identity),
             ins=[("pd",)], outs=[("mbcs",)], tag=S + f"ct1e{tc}")
        p.op("tensor",
             lambda e: e.transpose(m["pd"][0:16, 0:128],
                                   p1f(K, MB["mbcs"] + 1024, 16),
                                   m["ident"][:, :]),
             ins=[("mbcs",), ("c", "ident")], outs=[("pd",)], tag=S + f"ct2{tc}")
        p.op("scalar",
             lambda e, tc=tc: e.activation(p1f(K, MB["mbcs"] + 512 + tc * 128, 128)[0:16, :],
                                           m["pd"][0:16, 0:128], AF.Identity),
             ins=[("pd",)], outs=[("mbcs",)], tag=S + f"ct2e{tc}")

        dts = MB["mdts"]
        p.op("vector",
             lambda e, tc=tc: e.tensor_scalar(
                 out=p1f(K, dts, 128)[64:128, :],
                 in0=p1f(K, MB["mdtr"] + tc * 128, 128)[64:128, :],
                 scalar1=m["mabd"][64:128, 1:2], scalar2=20.0,
                 op0=ALU.add, op1=ALU.min),
             ins=[("mdtr",), ("mabd",)], outs=[("mdts",)], tag=S + f"dtz{tc}")
        p.op("scalar",
             lambda e: e.activation(p1f(K, dts + 128, 128)[64:128, :],
                                    p1f(K, dts, 128)[64:128, :], AF.Exp),
             ins=[("mdts",)], outs=[("mdts",)], tag=S + f"dte{tc}")
        p.op("vector",
             lambda e: e.tensor_scalar(
                 out=p1f(K, dts + 128, 128)[64:128, :],
                 in0=p1f(K, dts + 128, 128)[64:128, :],
                 scalar1=1.0, scalar2=None, op0=ALU.add),
             ins=[("mdts",)], outs=[("mdts",)], tag=S + f"dt1{tc}")
        p.op("scalar",
             lambda e: e.activation(p1f(K, dts + 128, 128)[64:128, :],
                                    p1f(K, dts + 128, 128)[64:128, :], AF.Ln),
             ins=[("mdts",)], outs=[("mdts",)], tag=S + f"dtv{tc}")
        p.op("scalar",
             lambda e: e.activation(p1f(K, dts + 256, 128)[64:128, :],
                                    p1f(K, dts + 128, 128)[64:128, :], AF.Ln),
             ins=[("mdts",)], outs=[("mdts",)], tag=S + f"lndt{tc}")
        p.op("vector",
             lambda e: e.tensor_scalar(
                 out=p1f(K, dts + 384, 128)[64:128, :],
                 in0=p1f(K, dts + 128, 128)[64:128, :],
                 scalar1=m["mabd"][64:128, 2:3], scalar2=None, op0=ALU.mult),
             ins=[("mdts",), ("mabd",)], outs=[("mdts",)], tag=S + f"dtA{tc}")
        p.op("vector",
             lambda e: e.tensor_tensor_scan(
                 out=p1f(K, dts, 128)[64:128, :],
                 data0=m["ones_col"][64:128, 0:1].broadcast_to([64, 128]),
                 data1=p1f(K, dts + 384, 128)[64:128, :],
                 initial=0.0, op0=ALU.mult, op1=ALU.add),
             ins=[("mdts",), ("c", "ones_col")], outs=[("mdts",)], tag=S + f"acum{tc}")

        p.op("tensor",
             lambda e, tc=tc: e.matmul(m["pc"][:, 0:128],
                                       p1f(K, MB["mbcs"] + tc * 128, 128)[0:16, :],
                                       p1f(K, MB["mbcs"] + 512 + tc * 128, 128)[0:16, :],
                                       start=True, stop=True),
             ins=[("mbcs",)], outs=[("pc",)], tag=S + f"gt{tc}")
        p.op("scalar",
             lambda e: e.activation(p2b(K, MGT, 128), m["pc"][:, 0:128], AF.Identity),
             ins=[("pc",)], outs=[("p2", 16)], tag=S + f"gte{tc}")
        p.op("tensor",
             lambda e, tc=tc: e.transpose(m["pd"][:, 0:16],
                                          p1f(K, MB["mbcs"] + tc * 128, 128)[0:16, :],
                                          m["ident"][0:16, 0:16]),
             ins=[("mbcs",), ("c", "ident")], outs=[("pd",)], tag=S + f"btm{tc}")
        p.op("scalar",
             lambda e: e.activation(p2b(K, MBTMB, 16), m["pd"][:, 0:16], AF.Identity),
             ins=[("pd",)], outs=[("p2", 16)], tag=S + f"btme{tc}")
        p.op("vector",
             lambda e: e.tensor_sub(p1f(K, dts + 384, 128)[64:128, :],
                                    p1f(K, dts, 128)[64:128, :],
                                    p1f(K, dts + 256, 128)[64:128, :]),
             ins=[("mdts",)], outs=[("mdts",)], tag=S + f"am{tc}")
        p.op("tensor",
             lambda e: e.transpose(m["pd"][:, 0:64], p1f(K, dts + 384, 128)[64:128, :],
                                   m["ident"][64:128, 64:128]),
             ins=[("mdts",), ("c", "ident")], outs=[("pd",)], tag=S + f"amt{tc}")
        p.op("scalar",
             lambda e: e.activation(m["small"][:, 1536:1600], m["pd"][:, 0:64],
                                    AF.Identity, scale=-1.0),
             ins=[("pd",)], outs=[("small", 6)], tag=S + f"amte{tc}")
        p.op("scalar",
             lambda e: e.activation(p1f(K, dts + 256, 128)[64:128, :],
                                    p1f(K, dts, 128)[64:128, :], AF.Exp),
             ins=[("mdts",)], outs=[("mdts",)], tag=S + f"expa{tc}")
        p.op("tensor",
             lambda e: e.transpose(m["pd"][:, 0:64], p1f(K, dts + 256, 128)[64:128, :],
                                   m["ident"][64:128, 64:128]),
             ins=[("mdts",), ("c", "ident")], outs=[("pd",)], tag=S + f"expat{tc}")
        p.op("scalar",
             lambda e, tc=tc: e.activation(m["small"][:, 1792 + tc * 64: 1856 + tc * 64],
                                           m["pd"][:, 0:64], AF.Identity),
             ins=[("pd",)], outs=[("small", 7)], tag=S + f"expate{tc}")
        p.op("scalar",
             lambda e: e.activation(m["small"][64:128, 1664:1665],
                                    p1f(K, dts, 128)[64:128, 127:128], AF.Exp),
             ins=[("mdts",)], outs=[("small", 6)], tag=S + f"dtot{tc}")
        def fdt(e, tc=tc):
            dst = bass.AP(d["agin_m"], tc * 65600 + 65536, [[1, 64], [1, 1]])
            return e.dma_start(out=dst, in_=m["small"][64:128, 1664:1665])
        p.dma("sync", fdt, ins=[("small", 6)], outs=[("aginm", tc, 8)], tag=S + f"dtst{tc}")
        p.op("scalar",
             lambda e: e.activation(p1f(K, dts + 384, 128)[64:128, :],
                                    p1f(K, dts, 128)[64:128, :], AF.Exp,
                                    scale=-1.0,
                                    bias=p1f(K, dts, 128)[64:128, 127:128]),
             ins=[("mdts",)], outs=[("mdts",)], tag=S + f"e2{tc}")
        p.op("vector",
             lambda e: e.tensor_mul(p1f(K, dts + 384, 128)[64:128, :],
                                    p1f(K, dts + 384, 128)[64:128, :],
                                    p1f(K, dts + 128, 128)[64:128, :]),
             ins=[("mdts",)], outs=[("mdts",)], tag=S + f"w2{tc}")
        p.op("tensor",
             lambda e: e.transpose(m["pd"][:, 0:64], p1f(K, dts + 384, 128)[64:128, :],
                                   m["ident"][64:128, 64:128]),
             ins=[("mdts",), ("c", "ident")], outs=[("pd",)], tag=S + f"w2t{tc}")
        p.op("scalar",
             lambda e: e.activation(m["small"][:, 2112:2176], m["pd"][:, 0:64], AF.Identity),
             ins=[("pd",)], outs=[("small", 8)], tag=S + f"w2te{tc}")
        def fxt1(e, tc=tc):
            if tc < 2:
                srcx = m["p1"][:, tc * 4096:(tc + 1) * 4096]
            else:
                srcx = m["p3"][:, (tc - 2) * 4096:(tc - 1) * 4096]
            dst = bass.AP(m.t["wb"], 0, [[8192, 128], [128, 32], [1, 128]])
            return e.dma_start_transpose(dst, srcx)
        p.dma("sync", fxt1,
              ins=([("p1", 2 * tc), ("p1", 2 * tc + 1)] if tc < 2 else
                   [("p3", i) for i in range((tc - 2) * 8, (tc - 1) * 8)]),
              outs=[("wb", 0), ("wb", 1)], tag=S + f"xt1{tc}")
        p.op("vector",
             lambda e: e.tensor_tensor(
                 m["wb"][:, 4096:8192], m["wb"][:, 0:4096],
                 bass.AP(m.t["small"], 2112, [[2304, 128], [1, 64], [0, 64]]),
                 ALU.mult),
             ins=[("wb", 0), ("wb", 1), ("small", 8)],
             outs=[("wb", 2), ("wb", 3)], tag=S + f"xe{tc}")
        grp_y = None
        grp_h = None
        for h in range(64):
            b8 = h % 8
            if b8 == 0:
                grp_y = PsumGroup()
                grp_h = PsumGroup()
            p.op("tensor",
                 lambda e, h=h: e.matmul(m["pb"][0:1, 0:128],
                                         m["ident"][64:128, 64 + h:65 + h],
                                         p1f(K, dts, 128)[64:128, :],
                                         start=True, stop=True),
                 ins=[("mdts",), ("c", "ident")], outs=[("pb",)], tag=S + f"ax{tc}_{h}")
            p.op("scalar",
                 lambda e: e.activation(m["small"][0:1, 2176:2304], m["pb"][0:1, 0:128],
                                        AF.Identity),
                 ins=[("pb",)], outs=[("small", 8)], tag=S + f"axe{tc}_{h}")
            p.op("tensor",
                 lambda e: e.matmul(m["pc"][:, 0:128], m["ones_row"][0:1, :],
                                    m["small"][0:1, 2176:2304], start=True, stop=False),
                 ins=[("small", 8), ("c", "ones_row")], outs=[("pc",)], tag=S + f"pb1{tc}_{h}")
            p.op("tensor",
                 lambda e: e.matmul(m["pc"][:, 0:128], m["ident"][:, :],
                                    m["trineg"][:, :], start=False, stop=True),
                 ins=[("c", "ident"), ("c", "trineg"), ("pc",)], outs=[("pc",)],
                 tag=S + f"pb2{tc}_{h}")
            p.op("scalar",
                 lambda e, h=h: e.activation(p2b(K, MPT, 128), m["pc"][:, 0:128],
                                             AF.Exp, bias=m["small"][:, 1536 + h:1537 + h]),
                 ins=[("pc",), ("small", 6)], outs=[("p2", 16)], tag=S + f"pex{tc}_{h}")
            p.op("vector",
                 lambda e: e.tensor_mul(p2b(K, MPT, 128), p2b(K, MPT, 128),
                                        p2b(K, MGT, 128)),
                 ins=[("p2", 16)], outs=[("p2", 16)], tag=S + f"pg{tc}_{h}")
            fy = grp_y.next_flags(0)
            p.op("tensor",
                 lambda e, h=h, b8=b8, fy=fy:
                 e.matmul(m["pa"][:, b8 * 64:(b8 + 1) * 64], p2b(K, MPT, 128),
                          m["wb"][:, h * 64:(h + 1) * 64],
                          start=fy["start"], stop=fy["stop"]),
                 ins=[("p2", 16), ("wb", 0), ("wb", 1)], outs=[("pa",)],
                 tag=S + f"yd{tc}_{h}")
            fh = grp_h.next_flags(0)
            p.op("tensor",
                 lambda e, h=h, b8=b8, fh=fh:
                 e.matmul(m["pd"][0:16, b8 * 64:(b8 + 1) * 64], p2b(K, MBTMB, 16),
                          m["wb"][:, 4096 + h * 64: 4096 + (h + 1) * 64],
                          start=fh["start"], stop=fh["stop"]),
                 ins=[("p2", 16), ("wb", 2), ("wb", 3)], outs=[("pd",)],
                 tag=S + f"hm{tc}_{h}")
            if b8 == 7:
                grp_y.close()
                grp_h.close()
                bi = h // 8
                p.op("scalar",
                     lambda e, bi=bi, tc=tc: e.activation(
                         m["p2"][:, 16384 + tc * 4096 + bi * 512: 16384 + tc * 4096 + (bi + 1) * 512],
                         m["pa"][:, :], AF.Identity),
                     ins=[("pa",)], outs=[("p2", 32 + tc * 8 + bi)],
                     tag=S + f"yev{tc}_{bi}")
                p.op("vector",
                     lambda e: e.tensor_copy(p1f(K, MB["msta"], 512)[0:16, :],
                                             m["pd"][0:16, :]),
                     ins=[("pd",)], outs=[("msta",)], tag=S + f"hev{tc}_{bi}")
                def fhst(e, tc=tc, bi=bi):
                    src = p1f(K, MB["msta"], 512)[0:16, :]
                    dst = bass.AP(d["agin_m"], tc * 65600 + bi * 512,
                                  [[4096, 16], [1, 512]])
                    return e.dma_start(out=dst, in_=src)
                p.dma("sync", fhst, ins=[("msta",)], outs=[("aginm", tc, bi)],
                      tag=S + f"hst{tc}_{bi}")

    if DBG and g == 0:
        p.dma("sync", lambda e: e.dma_start(out=d["dbgb"].ap(),
                                            in_=m["p2"][:, 16384:32768]),
              ins=[("p2", 32 + i) for i in range(32)], outs=[("dbgt1",)],
              tag=S + "tap1")
        p.dma("gpsimd", lambda e: e.dma_start(out=d["dbg2"].ap(),
                                              in_=m["small"][:, 1536:2048]),
              ins=[("small", 6), ("small", 7), ("small", 8)],
              outs=[("dbgt2",)], tag=S + "tap2")

    agout_m = "agout_mA" if g == 0 else "agout_mB"
    p.collective(
        lambda e, agout_m=agout_m: e.collective_compute(
            "AllGather", ALU.bypass,
            replica_groups=[[0, 1], [2, 3], [4, 5], [6, 7]],
            ins=[d["agin_m"].ap()], outs=[d[agout_m].ap()]),
        ins=[("aginm", tc, bi) for tc in range(4) for bi in list(range(8)) + [8]],
        outs=[(agout_m,)], tag=S + "agm")

    p2f = m.t["p2"].ap().bitcast(F32)
    p.dma("sync", lambda e: e.dma_start(out=p2f[0:16, 0:4096], in_=d["msdram"].ap()),
          ins=[("msdram",)], outs=[("p2", i) for i in range(16)], tag=S + "resT")
    for tc in range(4):
        j = 4 * g + tc
        _mamba_phase2(K, site, g, tc, j, agout_m)
    p.dma("sync", lambda e: e.dma_start(out=d["msdram"].ap(), in_=p2f[0:16, 0:4096]),
          ins=[("p2", i) for i in range(16)], outs=[("msdram",)], tag=S + "savT")

    _mamba_wout(K, site, g)


def _mamba_phase2(K, site, g, tc, j, agout_m):
    p, m, d = K.prog, K.mem, K.dram
    S = f"s{g}p{tc}_"
    p2f = m.t["p2"].ap().bitcast(F32)
    if j > 0:
        for a in (2 * j - 2, 2 * j - 1):
            slab, grp_i, idx = a % 2, a // 8, (a // 2) % 4
            agname = "agout_mA" if grp_i == 0 else "agout_mB"
            def fdk(e, agname=agname, slab=slab, idx=idx):
                src = bass.AP(d[agname], slab * 4 * 65600 + idx * 65600 + 65536,
                              [[64, 1], [1, 64]])
                return e.dma_start(out=m["small"][0:1, 2176:2240], in_=src)
            p.dma("scalar", fdk, ins=[(agname,)], outs=[("small", 8)], tag=S + f"dk{a}")
            p.op("tensor",
                 lambda e: e.matmul(m["pd"][:, 0:64], m["ones_row"][0:1, :],
                                    m["small"][0:1, 2176:2240], start=True, stop=True),
                 ins=[("small", 8), ("c", "ones_row")], outs=[("pd",)], tag=S + f"dkb{a}")
            p.op("vector",
                 lambda e: e.tensor_copy(m["small"][:, 1600:1664], m["pd"][:, 0:64]),
                 ins=[("pd",)], outs=[("small", 6)], tag=S + f"dke{a}")
            def fhl(e, agname=agname, slab=slab, idx=idx):
                src = bass.AP(d[agname], slab * 4 * 65600 + idx * 65600,
                              [[4096, 16], [1, 4096]])
                return e.dma_start(out=p2f[0:16, 4096:8192], in_=src)
            p.dma("scalar", fhl, ins=[(agname,)], outs=[("p2", 16 + i) for i in range(16)],
                  tag=S + f"hl{a}")
            p.op("vector",
                 lambda e: e.tensor_tensor(
                     p2f[0:16, 0:4096], p2f[0:16, 0:4096],
                     bass.AP(m.t["small"], 1600, [[2304, 128], [1, 64], [0, 64]])[0:16, :, :],
                     ALU.mult),
                 ins=[("small", 6)] + [("p2", i) for i in range(16)],
                 outs=[("p2", i) for i in range(16)], tag=S + f"tm{a}")
            p.op("vector",
                 lambda e: e.tensor_add(p2f[0:16, 0:4096], p2f[0:16, 0:4096],
                                        p2f[0:16, 4096:8192]),
                 ins=[("p2", 16 + i) for i in range(16)] + [("p2", i) for i in range(16)],
                 outs=[("p2", i) for i in range(16)], tag=S + f"ta{a}")
    a2 = 2 * j
    slab2, grp2, idx2 = a2 % 2, a2 // 8, (a2 // 2) % 4
    agn2 = "agout_mA" if grp2 == 0 else "agout_mB"
    def fdk2(e, agn2=agn2, slab2=slab2, idx2=idx2):
        src = bass.AP(d[agn2], slab2 * 4 * 65600 + idx2 * 65600 + 65536, [[64, 1], [1, 64]])
        return e.dma_start(out=m["small"][0:1, 2176:2240], in_=src)
    p.dma("scalar", fdk2, ins=[(agn2,)], outs=[("small", 8)], tag=S + "dk2")
    p.op("tensor",
         lambda e: e.matmul(m["pd"][:, 0:64], m["ones_row"][0:1, :],
                            m["small"][0:1, 2176:2240], start=True, stop=True),
         ins=[("small", 8), ("c", "ones_row")], outs=[("pd",)], tag=S + "dkb2")
    p.op("vector", lambda e: e.tensor_copy(m["small"][:, 1600:1664], m["pd"][:, 0:64]),
         ins=[("pd",)], outs=[("small", 6)], tag=S + "dke2")
    def fhl2(e, agn2=agn2, slab2=slab2, idx2=idx2):
        src = bass.AP(d[agn2], slab2 * 4 * 65600 + idx2 * 65600, [[4096, 16], [1, 4096]])
        return e.dma_start(out=p2f[0:16, 4096:8192], in_=src)
    p.dma("scalar", fhl2, ins=[(agn2,)], outs=[("p2", 16 + i) for i in range(16)],
          tag=S + "hl2")
    p.op("vector",
         lambda e, tc=tc: e.tensor_scalar(out=p1f(K, MB["mbcs"] + 1056, 128)[0:16, :],
                                          in0=p1f(K, MB["mbcs"] + 512 + tc * 128, 128)[0:16, :],
                                          scalar1=m["parw_bc"][0:16, 0:1], scalar2=None,
                                          op0=ALU.mult),
         ins=[("mbcs",), ("parw_bc",)], outs=[("mbcs",)], tag=S + "c0")
    p.op("vector",
         lambda e, tc=tc: e.tensor_scalar(out=p1f(K, MB["mbcs"] + 1184, 128)[0:16, :],
                                          in0=p1f(K, MB["mbcs"] + 512 + tc * 128, 128)[0:16, :],
                                          scalar1=m["parw_bc"][0:16, 1:2], scalar2=None,
                                          op0=ALU.mult),
         ins=[("mbcs",), ("parw_bc",)], outs=[("mbcs",)], tag=S + "c1")
    def fsg(e, j=j):
        return e.dma_start(out=m["wb"][:, 4096:8192],
                           in_=d["gdram"].ap()[j:j + 1].squeeze(0))
    p.dma("gpsimd", fsg, ins=[("gdram", j)], outs=[("wb", 2), ("wb", 3)], tag=S + "sg")
    def fxt1(e, tc=tc):
        if tc < 2:
            srcx = m["p1"][:, tc * 4096:(tc + 1) * 4096]
        else:
            srcx = m["p3"][:, (tc - 2) * 4096:(tc - 1) * 4096]
        dst = bass.AP(m.t["wb"], 0, [[8192, 128], [128, 32], [1, 128]])
        return e.dma_start_transpose(dst, srcx)
    p.dma("sync", fxt1,
          ins=([("p1", 2 * tc), ("p1", 2 * tc + 1)] if tc < 2 else
               [("p3", i) for i in range((tc - 2) * 8, (tc - 1) * 8)]),
          outs=[("wb", 0), ("wb", 1)], tag=S + "xt1")
    for sl in range(8):
        p.op("vector",
             lambda e, sl=sl: e.tensor_tensor(
                 p1f(K, MB["msta"], 512)[0:16, :],
                 p2f[0:16, sl * 512:(sl + 1) * 512],
                 bass.AP(m.t["small"], 1600 + sl * 8, [[2304, 128], [1, 8], [0, 64]])[0:16, :, :],
                 ALU.mult),
             ins=[("small", 6)] + [("p2", i) for i in range(16)], outs=[("msta",)],
             tag=S + f"td{sl}")
        grp = PsumGroup()
        f1 = grp.next_flags(0)
        p.op("tensor",
             lambda e, sl=sl, f1=f1: e.matmul(
                 m["pb"][:, :], p1f(K, MB["mbcs"] + 1056, 128)[0:16, :],
                 p2f[0:16, sl * 512:(sl + 1) * 512],
                 start=f1["start"], stop=f1["stop"]),
             ins=[("mbcs",)] + [("p2", i) for i in range(16)], outs=[("pb",)],
             tag=S + f"u0{sl}")
        f2 = grp.next_flags(0)
        p.op("tensor",
             lambda e, sl=sl, f2=f2: e.matmul(
                 m["pb"][:, :], p1f(K, MB["mbcs"] + 1184, 128)[0:16, :],
                 p1f(K, MB["msta"], 512)[0:16, :],
                 start=f2["start"], stop=f2["stop"]),
             ins=[("mbcs",), ("msta",)], outs=[("pb",)], tag=S + f"u1{sl}")
        f3 = grp.next_flags(0)
        p.op("tensor",
             lambda e, sl=sl, f3=f3: e.matmul(
                 m["pb"][:, :], p1f(K, MB["mbcs"] + 1184, 128)[0:16, :],
                 p2f[0:16, 4096 + sl * 512: 4096 + (sl + 1) * 512],
                 start=f3["start"], stop=f3["stop"]),
             ins=[("mbcs",)] + [("p2", 16 + i) for i in range(16)], outs=[("pb",)],
             tag=S + f"u2{sl}")
        grp.close()
        p.op("vector",
             lambda e, sl=sl, tc=tc: e.tensor_tensor(
                 p1f(K, MB["msta"], 512),
                 m["pb"][:, :],
                 bass.AP(m.t["small"], 1792 + tc * 64 + sl * 8,
                         [[2304, 128], [1, 8], [0, 64]]),
                 ALU.mult),
             ins=[("pb",), ("small", 7)], outs=[("msta",)], tag=S + f"ux{sl}")
        p.op("vector",
             lambda e, sl=sl, tc=tc: e.tensor_add(
                 p1f(K, MB["msta"], 512),
                 p1f(K, MB["msta"], 512),
                 m["p2"][:, 16384 + tc * 4096 + sl * 512: 16384 + tc * 4096 + (sl + 1) * 512]),
             ins=[("msta",), ("p2", 32 + tc * 8 + sl)],
             outs=[("msta",)], tag=S + f"ya{sl}")
        p.op("vector",
             lambda e, sl=sl: e.tensor_tensor(
                 p1f(K, MB["mdtr"], 512),
                 m["wb"][:, sl * 512:(sl + 1) * 512],
                 bass.AP(m.t["small"], 2048 + sl * 8, [[2304, 128], [1, 8], [0, 64]]),
                 ALU.mult),
             ins=[("wb", 0), ("wb", 1), ("small", 8)], outs=[("mdtr",)],
             tag=S + f"dx{sl}")
        p.op("vector",
             lambda e, sl=sl: e.tensor_add(p1f(K, MB["msta"], 512),
                                           p1f(K, MB["msta"], 512),
                                           p1f(K, MB["mdtr"], 512)),
             ins=[("msta",), ("mdtr",)], outs=[("msta",)], tag=S + f"yb{sl}")
        p.op("vector",
             lambda e, sl=sl, tc=tc: e.tensor_tensor(
                 m["p2"][:, 16384 + tc * 4096 + sl * 512: 16384 + tc * 4096 + (sl + 1) * 512],
                 p1f(K, MB["msta"], 512),
                 m["wb"][:, 4096 + sl * 512: 4096 + (sl + 1) * 512], ALU.mult),
             ins=[("msta",), ("wb", 2), ("wb", 3)],
             outs=[("p2", 32 + tc * 8 + sl)], tag=S + f"y2{sl}")
    for sl in range(8):
        p.op("vector",
             lambda e, sl=sl, tc=tc: e.bn_stats(
                 m["small"][:, sl * 6: (sl + 1) * 6],
                 m["p2"][:, 16384 + tc * 4096 + sl * 512: 16384 + tc * 4096 + (sl + 1) * 512]),
             ins=[("p2", 32 + tc * 8 + sl)], outs=[("small", 0)],
             tag=S + f"bn{sl}")
    p.op("vector",
         lambda e: e.bn_aggr(m["small"][:, 64:66], m["small"][:, 0:48]),
         ins=[("small", 0)], outs=[("small", 0)], tag=S + "ag")
    p.op("vector",
         lambda e: e.scalar_tensor_tensor(
             out=m["small"][:, 66:67], in0=m["small"][:, 64:65],
             scalar=m["small"][:, 64:65], in1=m["small"][:, 65:66],
             op0=ALU.mult, op1=ALU.add),
         ins=[("small", 0)], outs=[("small", 0)], tag=S + "msq")
    p.op("vector",
         lambda e: e.tensor_scalar(out=m["small"][:, 66:67], in0=m["small"][:, 66:67],
                                   scalar1=float(EPS), scalar2=None, op0=ALU.add),
         ins=[("small", 0)], outs=[("small", 0)], tag=S + "msqe")
    p.op("scalar",
         lambda e: e.activation(m["small"][:, 67:68], m["small"][:, 66:67], AF.Ln),
         ins=[("small", 0)], outs=[("small", 0)], tag=S + "lnm")
    p.op("scalar",
         lambda e: e.activation(m["small"][:, 66:67], m["small"][:, 67:68], AF.Exp,
                                scale=-0.5),
         ins=[("small", 0)], outs=[("small", 0)], tag=S + "rst")
    for sl in range(8):
        p.op("vector",
             lambda e, sl=sl, tc=tc: e.tensor_scalar(
                 out=m["p2"][:, 16384 + tc * 4096 + sl * 512: 16384 + tc * 4096 + (sl + 1) * 512],
                 in0=m["p2"][:, 16384 + tc * 4096 + sl * 512: 16384 + tc * 4096 + (sl + 1) * 512],
                 scalar1=m["small"][:, 66:67], scalar2=None, op0=ALU.mult),
             ins=[("small", 0), ("p2", 32 + tc * 8 + sl)],
             outs=[("p2", 32 + tc * 8 + sl)], tag=S + f"yn{sl}")
    def fyt1(e, tc=tc):
        src = m["p2"][:, 16384 + tc * 4096: 16384 + (tc + 1) * 4096]
        if tc < 2:
            dst = bass.AP(m.t["p1"], tc * 4096, [[16384, 128], [128, 32], [1, 128]])
        else:
            dst = bass.AP(m.t["p3"], (tc - 2) * 4096, [[8192, 128], [128, 32], [1, 128]])
        return e.dma_start_transpose(dst, src)
    p.dma("sync", fyt1, ins=[("p2", 32 + tc * 8 + i) for i in range(8)],
          outs=([("p1", 2 * tc), ("p1", 2 * tc + 1)] if tc < 2 else
                [("p3", i) for i in range((tc - 2) * 8, (tc - 1) * 8)]),
          tag=S + "yt1")


def _mamba_wout(K, site, g):
    p, m, d = K.prog, K.mem, K.dram
    S = f"s{g}w_"
    p2f = m.t["p2"].ap().bitcast(F32)
    for cc in range(16):
        slot = cc % 2
        def fwo(e, cc=cc, slot=slot):
            src = bass.AP(d["wout"], cc * 128, [[2048, 128], [128 * 2048, 32], [1, 128]])
            return e.dma_start(out=m["wb"][:, slot * 4096:(slot + 1) * 4096], in_=src)
        p.dma("gpsimd", fwo, outs=[("wb", 2 * slot), ("wb", 2 * slot + 1)],
              tag=S + f"lwo{cc}")
        p.op("vector",
             lambda e, slot=slot: e.tensor_tensor(
                 m["wb"][:, slot * 4096:(slot + 1) * 4096],
                 m["wb"][:, slot * 4096:(slot + 1) * 4096],
                 bass.AP(m.t["mnw"], 0, [[32, 128], [1, 32], [0, 128]]),
                 ALU.mult),
             ins=[("wb", 2 * slot), ("wb", 2 * slot + 1), ("mnw",)],
             outs=[("wb", 2 * slot), ("wb", 2 * slot + 1)], tag=S + f"nw{cc}")
        ps = "pa" if slot == 0 else "pb"
        grp = PsumGroup()
        for kc in range(32):
            for half in range(2):
                fl = grp.next_flags(0)
                def fmm(e, kc=kc, slot=slot, ps=ps, half=half, fl=fl):
                    if half == 0:
                        rhs = bass.AP(m.t["p1"], kc * 128,
                                      [[16384, 128], [4096, 2], [1, 128]])
                    else:
                        rhs = bass.AP(m.t["p3"], kc * 128,
                                      [[8192, 128], [4096, 2], [1, 128]])
                    return e.matmul(
                        m[ps][:, half * 256:(half + 1) * 256],
                        m["wb"][:, slot * 4096 + kc * 128: slot * 4096 + (kc + 1) * 128],
                        rhs, start=fl["start"], stop=fl["stop"])
                p.op("tensor", fmm,
                     ins=[("wb", 2 * slot), ("wb", 2 * slot + 1)] +
                         ([("p1", 0), ("p1", 1), ("p1", 2), ("p1", 3)] if half == 0 else
                          [("p3", i) for i in range(16)]),
                     outs=[(ps,)], tag=S + f"om{cc}_{kc}_{half}")
        grp.close()
        p.op("scalar",
             lambda e, ps=ps, cc=cc: e.activation(m["p2"][:, cc * 512:(cc + 1) * 512],
                                                  m[ps][:, :], AF.Identity),
             ins=[(ps,)], outs=[("p2", cc)], tag=S + f"oev{cc}")
    for jj in range(4):
        j = 4 * g + jj
        for cb in range(4):
            for ci4 in range(4):
                ci = cb * 4 + ci4
                def ftr(e, ci=ci, ci4=ci4, jj=jj):
                    pdb = m["pd"][:, :].bitcast(BF16)
                    return e.transpose(pdb[:, ci4 * 128:(ci4 + 1) * 128],
                                       m["p2"][:, ci * 512 + jj * 128: ci * 512 + (jj + 1) * 128],
                                       m["identb"][:, :])
                p.op("tensor", ftr, ins=[("p2", ci), ("c", "identb")],
                     outs=[("pd",)], tag=S + f"tr{j}_{cb}_{ci4}")
            p.op("vector",
                 lambda e, j=j, cb=cb: e.tensor_add(
                     xtile(K, j)[:, cb * 512:(cb + 1) * 512],
                     m["pd"][:, :].bitcast(BF16)[:, 0:512],
                     xtile(K, j)[:, cb * 512:(cb + 1) * 512]),
                 ins=[("pd",), ("x", j)], outs=[("x", j)], tag=S + f"res{j}_{cb}")


CHAIN = ('l0ffn', 'l0attn', 'l0ffn2', 'l1ffn', 'l1attn', 'l1ffn2')


EMIT_RANK_REGS = False


def build_body(K):
    load_consts(K)
    if EMIT_RANK_REGS:
        emit_rank_regs(K)
    if not K.chain:
        emit_ln(K)
        emit_hfm(K)
    for site, name in enumerate(K.chain):
        if name in ("l1ffn", "l1ffn2"):
            emit_l1ffn(K, site)
        elif name == "l0attn":
            emit_l0attn(K, site)
        elif name in ("l0ffn", "l0ffn2"):
            emit_l0ffn(K, site)
        elif name == "l1attn":
            emit_l1attn(K, site)
        else:
            raise ValueError(name)
    store_out(K)


def kernel(**inputs):
    in_maps = host_prepare(inputs)
    nc = build(CHAIN)
    res = run_bass_kernel_spmd(nc, in_maps, core_ids=list(range(NCORES)))
    out = np.zeros((NSEQ * T, D), np.float32)
    for c in range(NCORES):
        out[_interleave_index(c)] = res.results[c]["out"]
    return out.reshape(NSEQ, T, D)


if __name__ == "__main__":
    import reference as _r
    ins = _r.setup_inputs(0)
    o = kernel(**{k: np.asarray(v) for k, v in ins.items()})
    print("ran", o.shape, o.dtype)


# version 03
# speedup vs baseline: 1.3530x; 1.3499x over previous
import math
import numpy as np
import concourse.bass as bass
import concourse.mybir as mybir
from concourse.bass_utils import run_bass_kernel_spmd

F32 = mybir.dt.float32
BF16 = mybir.dt.bfloat16
I32 = mybir.dt.int32
AF = mybir.ActivationFunctionType
ALU = mybir.AluOpType
AX = mybir.AxisListType

NCORES = 8
NSEQ = 4
T = 2048
D = 2048
NT = 8
TOK = NT * 128
NTA = 16
EPS = 1e-5

ENGINES = ("tensor", "vector", "scalar", "gpsimd", "sync")
DMA_QUEUES = ("sync", "gpsimd", "scalar")


class Op:
    __slots__ = ("engine", "fn", "ins", "outs", "is_dma", "sem", "seq",
                 "marked", "target", "waits", "tag", "deps")

    def __init__(self, engine, fn, ins, outs, is_dma, sem, seq, tag):
        self.engine = engine
        self.fn = fn
        self.ins = ins
        self.outs = outs
        self.is_dma = is_dma
        self.sem = sem
        self.seq = seq
        self.marked = False
        self.target = None
        self.waits = []
        self.tag = tag


class Prog:
    def __init__(self, nc, n_dma_sems=10):
        self.nc = nc
        self.ops = []
        self.last_writer = {}
        self.readers = {}
        self.key_sem = {}
        self.dma_sem_names = {}
        self.sem_handles = {}
        self._sem_ctxs = []
        self.engine_sem = {e: f"es_{e}" for e in ENGINES}
        self._dma_pools = {q: [f"ds_{q}_{i}" for i in range(n_dma_sems)]
                           for q in DMA_QUEUES}
        self._dma_rr = {q: 0 for q in DMA_QUEUES}
        self._outstanding = {}
        self._cc_idx = 0

    def op(self, engine, fn, ins=(), outs=(), tag=""):
        o = Op(engine, fn, tuple(ins), tuple(outs), False, None,
               len(self.ops), tag)
        self._link(o)
        self.ops.append(o)
        return o

    def dma(self, queue, fn, ins=(), outs=(), tag=""):
        assert queue in DMA_QUEUES
        key = outs[0] if outs else ("dmaout", len(self.ops))
        sem = self.key_sem.get(key)
        if sem is None:
            pool = self._dma_pools[queue]
            sem = pool[self._dma_rr[queue] % len(pool)]
            self._dma_rr[queue] += 1
            self.key_sem[key] = sem
        o = Op(queue, fn, tuple(ins), tuple(outs), True, sem,
               len(self.ops), tag)
        prev = self._outstanding.get(sem)
        if prev is not None and prev.outs and prev.outs[0] != key:
            o.ins = o.ins + prev.outs
        self._link(o)
        self._outstanding[sem] = o
        self.ops.append(o)
        return o

    def collective(self, fn, ins=(), outs=(), tag="cc"):
        sem = f"cc_{self._cc_idx}"
        self._cc_idx += 1
        o = Op("gpsimd", fn, tuple(ins), tuple(outs), False, sem,
               len(self.ops), tag)
        self._link(o)
        self.ops.append(o)
        return o

    def _link(self, o):
        def tt(w):
            return (w.engine == "tensor" and o.engine == "tensor"
                    and not w.is_dma and not o.is_dma
                    and w.sem is None and o.sem is None)
        deps = set()
        for k in o.ins:
            w = self.last_writer.get(k)
            if w is not None:
                deps.add(w)
            self.readers.setdefault(k, [])
        for k in o.outs:
            w = self.last_writer.get(k)
            if w is not None and not tt(w):
                deps.add(w)
            for r in self.readers.get(k, ()):
                if not tt(r):
                    deps.add(r)
        deps.discard(o)
        o.deps = deps
        for k in o.ins:
            self.readers.setdefault(k, []).append(o)
        for k in o.outs:
            self.last_writer[k] = o
            self.readers[k] = []

    def finalize(self):
        for o in self.ops:
            for d in o.deps:
                d.marked = True
        counts = {}
        for o in self.ops:
            if o.is_dma or o.sem is not None:
                o.marked = True
            if o.marked:
                sem = o.sem if o.sem is not None else self.engine_sem[o.engine]
                inc = 16 if o.is_dma else 1
                counts[sem] = counts.get(sem, 0) + inc
                o.target = (sem, counts[sem], inc)
        watermark = {}
        engine_pos = {e: 0 for e in ENGINES}
        for o in self.ops:
            ws = {}
            for d in o.deps:
                if d.target is None:
                    continue
                sem, val, _ = d.target
                if ws.get(sem, -1) < val:
                    ws[sem] = val
            for sem, val in ws.items():
                wm = watermark.get((o.engine, sem), -1)
                if val > wm:
                    o.waits.append((sem, val))
                    watermark[(o.engine, sem)] = val
        self.final_waits = []

    def sems_needed(self):
        s = set()
        for o in self.ops:
            if o.target is not None:
                s.add(o.target[0])
            for sem, _ in o.waits:
                s.add(sem)
        for sem, _ in self.final_waits:
            s.add(sem)
        return sorted(s)

    def replay(self, block):
        streams = {e: [] for e in ENGINES}
        for o in self.ops:
            streams[o.engine].append(o)

        import contextlib
        import re

        def scope_of(tag):
            return re.sub(r'[0-9_]+$', '', re.sub(r'[0-9]+(_[0-9]+)*$', '', tag))

        def mk(engine, ops):
            def run(eng):
                for o in ops:
                    for sem, val in o.waits:
                        eng.wait_ge(self.sem_handles[sem], val)
                    ctx = (self.nc.named_scope(scope_of(o.tag)) if PROFILE
                           else contextlib.nullcontext())
                    with ctx:
                        inst = o.fn(eng)
                    if o.target is not None:
                        sem, val, inc = o.target
                        if inst is None:
                            raise RuntimeError(f"op {o.tag} returned no inst")
                        inst.then_inc(self.sem_handles[sem], inc)
                if engine == "sync":
                    for sem, val in self.final_waits:
                        eng.wait_ge(self.sem_handles[sem], val)
            return run

        for e in ENGINES:
            getattr(block, e)(mk(e, streams[e]))


class Mem:

    def __init__(self, nc, stack):
        self.nc = nc
        self.stack = stack
        self.t = {}

    def sbuf(self, name, free_elems, dtype):
        h = self.stack.enter_context(self.nc.sbuf_tensor("sb_" + name, [128, free_elems], dtype))
        self.t[name] = h
        return h

    def psum(self, name, free_elems, dtype=F32):
        h = self.stack.enter_context(self.nc.psum_tensor(name, [128, free_elems], dtype))
        self.t[name] = h
        return h

    def __getitem__(self, name):
        return self.t[name]


def _interleave_index(core):
    b = core // 2
    par = core % 2
    tiles = [b * T + (2 * j + par) * 128 for j in range(NT)]
    idx = np.concatenate([np.arange(t0, t0 + 128) for t0 in tiles])
    return idx


def host_prepare(inputs):
    xs_full = np.ascontiguousarray(np.asarray(inputs["L0_attn_x"], np.float32).reshape(NSEQ * T, D))

    w = {}
    w["wq"] = np.ascontiguousarray(np.asarray(inputs["L0_attn_Wq"], np.float32))
    w["wk"] = np.ascontiguousarray(np.asarray(inputs["L0_attn_Wk"], np.float32))
    w["wv"] = np.ascontiguousarray(np.asarray(inputs["L0_attn_Wv"], np.float32))
    w["wo"] = np.ascontiguousarray(np.asarray(inputs["L0_attn_Wo"], np.float32))
    w["sim"] = np.ascontiguousarray(np.asarray(inputs["L0_ffn_sim_matrix"], np.float32))
    w["gates"] = np.ascontiguousarray(np.asarray(inputs["L0_ffn_gates"], np.float32).reshape(1, 16))
    for nm, src in (("qcat", "L0_ffn_q_proj"), ("kcat", "L0_ffn_k_proj"), ("vcat", "L0_ffn_v_proj")):
        a = np.asarray(inputs[src], np.float32)
        w[nm] = np.ascontiguousarray(a.transpose(1, 0, 2).reshape(D, 16 * 128))
    w["ocat"] = np.ascontiguousarray(np.asarray(inputs["L0_ffn_o_proj"], np.float32).reshape(16 * 128, D))
    w["wgT"] = np.ascontiguousarray(np.asarray(inputs["L1_ffn_w_gate"], np.float32).T)
    w["wuT"] = np.ascontiguousarray(np.asarray(inputs["L1_ffn_w_up"], np.float32).T)
    w["wdT"] = np.ascontiguousarray(np.asarray(inputs["L1_ffn_w_down"], np.float32).T)
    win = np.asarray(inputs["L1_attn_W_in"], np.float32)
    winp = np.zeros((D, 8320), np.float32)
    winp[:, 0:8192] = win[:, 0:8192]
    winp[:, 8192:8208] = win[:, 8192:8208]
    winp[:, 8224:8240] = win[:, 8208:8224]
    winp[:, 8256:8320] = win[:, 8224:8288]
    w["winp"] = np.ascontiguousarray(winp)
    w["wout"] = np.ascontiguousarray(np.asarray(inputs["L1_attn_W_out"], np.float32))
    convw = np.asarray(inputs["L1_attn_conv_w"], np.float32)
    convb = np.asarray(inputs["L1_attn_conv_b"], np.float32)
    mcwx = np.zeros((128, 128), np.float32)
    mcbx = np.zeros((128, 32), np.float32)
    for mc in range(32):
        mcwx[:, mc * 4:(mc + 1) * 4] = convw[mc * 128:(mc + 1) * 128, :]
        mcbx[:, mc] = convb[mc * 128:(mc + 1) * 128]
    w["mcwx"] = np.ascontiguousarray(mcwx)
    w["mcbx"] = np.ascontiguousarray(mcbx)
    mcwbc = np.zeros((128, 4), np.float32)
    mcbbc = np.zeros((128, 1), np.float32)
    mcwbc[0:16] = convw[4096:4112]
    mcwbc[32:48] = convw[4112:4128]
    mcbbc[0:16, 0] = convb[4096:4112]
    mcbbc[32:48, 0] = convb[4112:4128]
    w["mcwbc"] = np.ascontiguousarray(mcwbc)
    w["mcbbc"] = np.ascontiguousarray(mcbbc)
    w["mnw"] = np.ascontiguousarray(
        np.asarray(inputs["L1_attn_norm_w"], np.float32).reshape(32, 128).T)
    w["alog"] = np.ascontiguousarray(np.asarray(inputs["L1_attn_A_log"], np.float32).reshape(64, 1))
    w["dtbias"] = np.ascontiguousarray(np.asarray(inputs["L1_attn_dt_bias"], np.float32).reshape(64, 1))
    w["mrow"] = np.ascontiguousarray(np.asarray(inputs["L1_attn_D_skip"], np.float32).reshape(1, 64))

    consts = {}
    consts["ident"] = np.eye(128, dtype=np.float32)
    consts["ones_col"] = np.ones((128, 1), np.float32)
    consts["ones_row"] = np.ones((1, 128), np.float32)
    consts["cums"] = np.triu(np.ones((128, 128), np.float32))
    consts["trineg"] = np.where(np.triu(np.ones((128, 128), bool)), 0.0, -1e9).astype(np.float32)

    oh = np.zeros((128, 16 * 16), np.float32)
    for sg in range(16):
        oh[:, sg * 16 + sg] = 1.0
    consts["onehot"] = oh
    ss = np.zeros((16, 8), np.float32)
    for sg in range(16):
        for j in range(8):
            if sg >= 2 * j + 2:
                ss[sg, j] = 1.0
    consts["suffsel"] = ss

    tri_mult = np.triu(np.ones((128, 128), np.float32))
    ones_m = np.ones((128, 128), np.float32)
    zeros_m = np.zeros((128, 128), np.float32)
    tri_add = np.where(tri_mult > 0, 0.0, -1e9).astype(np.float32)
    zeros_a = np.zeros((128, 128), np.float32)
    full_a = np.full((128, 128), -1e9, np.float32)

    in_maps = []
    for c in range(NCORES):
        par = c % 2
        m = {"xs": np.ascontiguousarray(xs_full[_interleave_index(c)])}
        m.update(w)
        m.update(consts)
        m["maskA_mult"] = tri_mult if par == 0 else ones_m
        m["maskB_mult"] = zeros_m if par == 0 else tri_mult
        m["maskA_add"] = tri_add if par == 0 else zeros_a
        m["maskB_add"] = full_a if par == 0 else tri_add
        m["coreinfo"] = np.array([[2 * (c // 2), 2 * (c // 2) + 1]], np.int32)
        tw = np.zeros((1, 16), np.float32)
        for j in range(8):
            tw[0, 2 * j] = 1.0 if par == 1 else 0.0
            tw[0, 2 * j + 1] = 1.0 if (par == 0 and j > 0) else 0.0
        m["tailw_bc"] = np.ascontiguousarray(np.broadcast_to(tw, (128, 16)))
        pw = np.array([[1.0 - par, float(par)]], np.float32)
        m["parw_bc"] = np.ascontiguousarray(np.broadcast_to(pw, (128, 2)))
        in_maps.append(m)
    return in_maps


SQ_T = 1.0 / math.sqrt(2048.0)
SQ_H = 1.0 / math.sqrt(128.0)

INPUT_SPECS = [
    ("xs", [TOK, D]), ("wq", [D, D]), ("wk", [D, D]), ("wv", [D, D]),
    ("wo", [D, D]), ("sim", [D, 16]), ("gates", [1, 16]),
    ("qcat", [D, D]), ("kcat", [D, D]), ("vcat", [D, D]), ("ocat", [D, D]),
    ("wgT", [D, 8192]), ("wuT", [D, 8192]), ("wdT", [8192, D]),
    ("winp", [D, 8320]), ("wout", [4096, D]),
    ("mcwx", [128, 128]), ("mcwbc", [128, 4]), ("mcbx", [128, 32]),
    ("mcbbc", [128, 1]), ("mnw", [128, 32]),
    ("alog", [64, 1]), ("dtbias", [64, 1]), ("mrow", [1, 64]),
    ("tailw_bc", [128, 16]), ("parw_bc", [128, 2]),
    ("ident", [128, 128]), ("ones_col", [128, 1]), ("ones_row", [1, 128]),
    ("cums", [128, 128]), ("trineg", [128, 128]),
    ("maskA_mult", [128, 128]), ("maskB_mult", [128, 128]),
    ("maskA_add", [128, 128]), ("maskB_add", [128, 128]),
    ("onehot", [128, 256]), ("suffsel", [16, 8]),
]


class Ker:

    def __init__(self, chain):
        self.chain = chain
        self.nc = bass.Bass("TRN2", target_bir_lowering=False, debug=False)
        self.dram = {}
        for name, shape in INPUT_SPECS:
            self.dram[name] = self.nc.dram_tensor(name, shape, F32, kind="ExternalInput")
        self.dram["coreinfo"] = self.nc.dram_tensor("coreinfo", [1, 2], I32, kind="ExternalInput")
        self.dram["out"] = self.nc.dram_tensor("out", [TOK, D], F32, kind="ExternalOutput")
        self.dram["dbg"] = self.nc.dram_tensor("dbg", [128, 16384], F32, kind="ExternalOutput")
        self.dram["dbg2"] = self.nc.dram_tensor("dbg2", [128, 512], F32, kind="ExternalOutput")
        self.dram["dbgb"] = self.nc.dram_tensor("dbgb", [128, 16384], BF16, kind="ExternalOutput")
        self.dram["agin"] = self.nc.dram_tensor("agin", [2, 2097152], BF16, kind="Internal")
        self.dram["agout"] = self.nc.dram_tensor("agout", [2, 2 * 2097152], BF16, kind="Internal")
        self.dram["suffdram"] = self.nc.dram_tensor("suffdram", [8, 2048], BF16, kind="Internal")
        self.dram["agin_s"] = self.nc.dram_tensor("agin_s", [262144], BF16, kind="Internal")
        self.dram["agout_s"] = self.nc.dram_tensor("agout_s", [2, 262144], BF16, kind="Internal")
        self.dram["agin_t"] = self.nc.dram_tensor("agin_t", [67584], BF16, kind="Internal")
        self.dram["agout_tA"] = self.nc.dram_tensor("agout_tA", [135168], BF16, kind="Internal")
        self.dram["agout_tB"] = self.nc.dram_tensor("agout_tB", [135168], BF16, kind="Internal")
        self.dram["agin_m"] = self.nc.dram_tensor("agin_m", [4, 65600], F32, kind="Internal")
        self.dram["agout_mA"] = self.nc.dram_tensor("agout_mA", [524800], F32, kind="Internal")
        self.dram["agout_mB"] = self.nc.dram_tensor("agout_mB", [524800], F32, kind="Internal")
        self.dram["msdram"] = self.nc.dram_tensor("msdram", [16, 4096], F32, kind="Internal")
        self.dram["gdram"] = self.nc.dram_tensor("gdram", [8, 128, 4096], BF16, kind="Internal")
        self.prog = Prog(self.nc)
        self.uid = 0

    def fresh(self, base):
        self.uid += 1
        return f"{base}{self.uid}"


class PsumGroup:

    def __init__(self):
        self.started = set()
        self.last = {}

    def next_flags(self, bank=0):
        f = {"start": bank not in self.started, "stop": False}
        self.started.add(bank)
        self.last[bank] = f
        return f

    def close(self):
        assert self.last, "empty psum group"
        for f in self.last.values():
            f["stop"] = True


def build(chain=("l0ffn", "l0attn", "l0ffn2", "l1ffn", "l1attn", "l1ffn2")):
    import contextlib
    K = Ker(chain)
    nc, prog = K.nc, K.prog
    with contextlib.ExitStack() as stack:
        mem = Mem(nc, stack)
        K.mem = mem

        mem.sbuf("x", NT * D, F32)
        mem.sbuf("p1", 16384, BF16)
        mem.sbuf("p2", 32768, BF16)
        mem.sbuf("p3", 8192, BF16)
        mem.sbuf("wb", 8192, BF16)
        mem.sbuf("small", 2304, F32)
        mem.sbuf("stat", 512, F32)
        for cn, fe, dt in (("ident", 128, F32), ("identb", 128, BF16),
                           ("ones_col", 1, F32),
                           ("trineg", 128, F32),
                           ("mAm", 128, BF16), ("mBm", 128, BF16),
                           ("mAa", 128, BF16), ("mBa", 128, BF16)):
            mem.sbuf(cn, fe, dt)
        mem.sbuf("ones_row", 128, F32)
        mem.sbuf("ones_rowb", 128, BF16)
        mem.sbuf("ones_colb", 1, BF16)
        mem.sbuf("suffsel", 8, BF16)
        mem.sbuf("cinfo", 2, I32)

        mem.sbuf("neg16", 16, F32)
        mem.sbuf("sgbc", 16, F32)
        mem.sbuf("wrtb", 128, BF16)
        mem.sbuf("tailw_bc", 16, F32)
        mem.sbuf("parw_bc", 2, F32)
        mem.sbuf("mabd", 4, F32)
        mem.sbuf("mcwx", 128, F32)
        mem.sbuf("mcwbc", 4, F32)
        mem.sbuf("mcbx", 32, F32)
        mem.sbuf("mcbbc", 1, F32)
        mem.sbuf("mnw", 32, F32)
        mem.sbuf("mrow", 64, F32)

        mem.psum("pbig", 2048)
        mem.psum("pa", 512)
        mem.psum("pb", 512)
        mem.psum("pc", 512)
        mem.psum("pd", 512)

        build_body(K)

        prog.finalize()
        prog.final_waits = [(o.target[0], o.target[1]) for o in K.final_wait_ops]
        sem_names = prog.sems_needed()
        with contextlib.ExitStack() as sstack:
            for sn in sem_names:
                prog.sem_handles[sn] = sstack.enter_context(nc.semaphore(sn))
            with nc.Block() as block:
                prog.replay(block)
    return nc


def load_consts(K):
    p, m, d = K.prog, K.mem, K.dram
    pairs = [("ident", "ident", F32), ("identb", "ident", BF16),
             ("ones_col", "ones_col", F32),
             ("trineg", "trineg", F32),
             ("mAm", "maskA_mult", BF16), ("mBm", "maskB_mult", BF16),
             ("mAa", "maskA_add", BF16), ("mBa", "maskB_add", BF16)]
    for dst, src, dt in pairs:
        ap = d[src].ap()
        queue = "gpsimd" if dt == BF16 else "sync"
        p.dma(queue, lambda e, dst=dst, ap=ap: e.dma_start(out=K.mem[dst][:, :], in_=ap),
              outs=[("c", dst)], tag=f"ld_{dst}")
    p.dma("sync", lambda e: e.dma_start(out=K.mem["ones_row"][0:1, :], in_=d["ones_row"].ap()),
          outs=[("c", "ones_row")], tag="ld_ones_row")
    for j in range(NT):
        p.dma("sync",
              lambda e, j=j: e.dma_start(out=K.mem["x"][:, j * D:(j + 1) * D],
                                         in_=d["xs"].ap()[j * 128:(j + 1) * 128, :]),
              outs=[("x", j)], tag=f"ld_x{j}")


def xtile(K, j):
    return K.mem["x"][:, j * D:(j + 1) * D]


def emit_ln(K, dst_htm=True, tag="ln", tiles=None, colbase=0):
    p, m = K.prog, K.mem
    if tiles is None:
        tiles = range(NT)
    for j in tiles:
        x = xtile(K, j)
        st = K.fresh("lnst")
        def f_stats(e, x=x, j=j):
            r = None
            for k in range(4):
                r = e.bn_stats(m["small"][:, j * 32 + k * 6: j * 32 + (k + 1) * 6],
                               x[:, k * 512:(k + 1) * 512])
            return r
        p.op("vector", f_stats, ins=[("x", j)], outs=[("small", 0)], tag=f"{tag}_bn{j}")
        p.op("vector",
             lambda e, j=j: e.bn_aggr(m["stat"][:, j * 2:j * 2 + 2],
                                      m["small"][:, j * 32:j * 32 + 24]),
             ins=[("small", 0)], outs=[("mv", j)], tag=f"{tag}_ag{j}")
        p.op("vector",
             lambda e, j=j: e.tensor_scalar(out=m["stat"][:, 256 + j:256 + j + 1],
                                            in0=m["stat"][:, j * 2 + 1:j * 2 + 2],
                                            scalar1=float(EPS), scalar2=None,
                                            op0=ALU.add),
             ins=[("mv", j)], outs=[("vpe", j)], tag=f"{tag}_ve{j}")
        p.op("scalar",
             lambda e, j=j: e.activation(m["stat"][:, 272 + j:272 + j + 1],
                                         m["stat"][:, 256 + j:256 + j + 1],
                                         AF.Ln),
             ins=[("vpe", j)], outs=[("lnv", j)], tag=f"{tag}_ln{j}")
        p.op("scalar",
             lambda e, j=j: e.activation(m["stat"][:, 288 + j:288 + j + 1],
                                         m["stat"][:, 272 + j:272 + j + 1],
                                         AF.Exp, scale=-0.5),
             ins=[("lnv", j)], outs=[("rstd", j)], tag=f"{tag}_rs{j}")
        p.op("vector",
             lambda e, j=j: e.tensor_tensor(m["stat"][:, 304 + j:304 + j + 1],
                                            m["stat"][:, j * 2:j * 2 + 1],
                                            m["stat"][:, 288 + j:288 + j + 1],
                                            ALU.mult),
             ins=[("rstd", j), ("mv", j)], outs=[("mrs", j)], tag=f"{tag}_mr{j}")
        p.op("vector",
             lambda e, j=j: e.tensor_scalar(out=m["stat"][:, 304 + j:304 + j + 1],
                                            in0=m["stat"][:, 304 + j:304 + j + 1],
                                            scalar1=-1.0, scalar2=None, op0=ALU.mult),
             ins=[("mrs", j)], outs=[("nmrs", j)], tag=f"{tag}_nm{j}")
        p.op("scalar",
             lambda e, j=j, colbase=colbase: e.activation(
                 m["p1"][:, colbase + j * D: colbase + (j + 1) * D], xtile(K, j),
                 AF.Identity,
                 bias=m["stat"][:, 304 + j:304 + j + 1],
                 scale=m["stat"][:, 288 + j:288 + j + 1]),
             ins=[("x", j), ("rstd", j), ("nmrs", j)],
             outs=[("p1", (colbase + j * D) // D)],
             tag=f"{tag}_h{j}")


def emit_hfm(K, groups=(0, 1), tag="hfm", src_colbase=0):
    p, m = K.prog, K.mem
    for g in groups:
        for jj in range(4):
            j = 4 * g + jj
            def ft(e, j=j, jj=jj, src_colbase=src_colbase):
                src = m["p1"][:, src_colbase + j * D: src_colbase + (j + 1) * D]
                dst = bass.AP(m.t["p3"], jj * 128,
                              [[8192, 128], [512, 16], [1, 128]])
                return e.dma_start_transpose(dst, src)
            p.dma("sync", ft, ins=[("p1", (src_colbase + j * D) // D)],
                  outs=[("p3", ci) for ci in range(16)], tag=f"{tag}{j}")


def emit_rank_regs(K):
    pass


def _stream_lhsT(K, queue, wname, row_stride, col0, nk, dst_ap, key, tag):
    p = K.prog

    def fw(e):
        src = bass.AP(K.dram[wname], col0,
                      [[row_stride, 128], [128 * row_stride, nk], [1, 128]])
        return e.dma_start(out=dst_ap, in_=src)
    return p.dma(queue, fw, outs=[key], tag=tag)


DBG = False
PROFILE = False


def store_out(K):
    p, m, d = K.prog, K.mem, K.dram
    waits = []
    if DBG:
        pass
    for j in range(NT):
        o = p.dma("sync",
                  lambda e, j=j: e.dma_start(out=d["out"].ap()[j * 128:(j + 1) * 128, :],
                                             in_=xtile(K, j)),
                  ins=[("x", j)], outs=[("outj", j)], tag=f"st_out{j}")
        waits.append(o)
    K.final_wait_ops = waits


def emit_l1ffn(K, site):
    p, m = K.prog, K.mem
    S = f"f{site}_"
    emit_ln(K, tag=S + "ln")
    for g in range(2):
        emit_hfm(K, groups=(g,), tag=S + f"hfm{g}_")
        for hc in range(64):
            slot = hc % 4
            gb = m["wb"][:, slot * 2048: (slot + 1) * 2048]
            ub = m["p1"][:, slot * 2048: (slot + 1) * 2048]
            gkey, ukey = ("wb", slot), ("p1", slot)
            for nm, dst, wname, wkey in (("wg", gb, "wgT", gkey),
                                         ("wu", ub, "wuT", ukey)):
                def fw(e, dst=dst, wname=wname, hc=hc):
                    src = bass.AP(K.dram[wname], hc * 128,
                                  [[8192, 128], [128 * 8192, 16], [1, 128]])
                    return e.dma_start(out=dst, in_=src)
                p.dma("gpsimd", fw, outs=[wkey], ins=(), tag=S + f"ldw_{nm}{hc}")
            ps_g = "pa" if hc % 2 == 0 else "pc"
            ps_u = "pb" if hc % 2 == 0 else "pd"
            grp_g, grp_u = PsumGroup(), PsumGroup()
            for kc in range(16):
                fg = grp_g.next_flags()
                p.op("tensor",
                     lambda e, kc=kc, gb=gb, ps=ps_g, fg=fg:
                     e.matmul(m[ps][:, :], gb[:, kc * 128:(kc + 1) * 128],
                              m["p3"][:, kc * 512:(kc + 1) * 512],
                              start=fg["start"], stop=fg["stop"]),
                     ins=[gkey, ("p3", kc)],
                     outs=[(ps_g,)], tag=S + f"mm_g{hc}_{kc}")
            grp_g.close()
            for kc in range(16):
                fu = grp_u.next_flags()
                p.op("tensor",
                     lambda e, kc=kc, ub=ub, ps=ps_u, fu=fu:
                     e.matmul(m[ps][:, :], ub[:, kc * 128:(kc + 1) * 128],
                              m["p3"][:, kc * 512:(kc + 1) * 512],
                              start=fu["start"], stop=fu["stop"]),
                     ins=[ukey, ("p3", kc)],
                     outs=[(ps_u,)], tag=S + f"mm_u{hc}_{kc}")
            grp_u.close()
            sc = 512 + (hc % 2) * 512
            skeys = [("small", 2 + 2 * (hc % 2)), ("small", 3 + 2 * (hc % 2))]
            p.op("scalar",
                 lambda e, ps=ps_g, sc=sc: e.activation(m["small"][:, sc:sc + 512],
                                                        m[ps][:, :], AF.Silu),
                 ins=[(ps_g,)], outs=skeys, tag=S + f"silu{hc}")
            p.op("vector",
                 lambda e, ps=ps_u, hc=hc, sc=sc: e.tensor_tensor(
                     m["p2"][:, hc * 512:(hc + 1) * 512],
                     m["small"][:, sc:sc + 512], m[ps][:, :], ALU.mult),
                 ins=skeys + [(ps_u,)], outs=[("p2", hc)],
                 tag=S + f"gated{hc}")
        for cc in range(16):
            ps_o = "pa" if cc % 2 == 0 else "pb"
            grp = PsumGroup()
            for q in range(4):
                slot = q % 4
                wdb = m["wb"][:, slot * 2048: (slot + 1) * 2048]
                def fw(e, wdb=wdb, cc=cc, q=q):
                    src = bass.AP(K.dram["wdT"], q * 16 * 128 * 2048 + cc * 128,
                                  [[2048, 128], [128 * 2048, 16], [1, 128]])
                    return e.dma_start(out=wdb, in_=src)
                p.dma("gpsimd", fw, outs=[("wb", slot)], tag=S + f"ldwd{cc}_{q}")
                for hh in range(16):
                    hc = q * 16 + hh
                    fl = grp.next_flags()
                    p.op("tensor",
                         lambda e, hh=hh, hc=hc, wdb=wdb, ps=ps_o, fl=fl:
                         e.matmul(m[ps][:, :], wdb[:, hh * 128:(hh + 1) * 128],
                                  m["p2"][:, hc * 512:(hc + 1) * 512],
                                  start=fl["start"], stop=fl["stop"]),
                         ins=[("wb", slot), ("p2", hc)],
                         outs=[(ps_o,)], tag=S + f"mm_d{cc}_{hc}")
            grp.close()
            p.op("scalar",
                 lambda e, ps=ps_o, cc=cc: e.activation(
                     m["p3"][:, cc * 512:(cc + 1) * 512], m[ps][:, :], AF.Identity),
                 ins=[(ps_o,)], outs=[("p3", cc)], tag=S + f"evac_d{cc}")
        for jj in range(4):
            j = 4 * g + jj
            for cb in range(4):
                for ci4 in range(4):
                    ci = cb * 4 + ci4
                    def ftr(e, ci=ci, ci4=ci4, jj=jj):
                        pdb = m["pd"][:, :].bitcast(BF16)
                        return e.transpose(pdb[:, ci4 * 128:(ci4 + 1) * 128],
                                           m["p3"][:, ci * 512 + jj * 128: ci * 512 + (jj + 1) * 128],
                                           m["identb"][:, :])
                    p.op("tensor", ftr, ins=[("p3", ci), ("c", "identb")],
                         outs=[("pd",)], tag=S + f"tr{j}_{cb}_{ci4}")
                p.op("vector",
                     lambda e, j=j, cb=cb: e.scalar_tensor_tensor(
                         out=xtile(K, j)[:, cb * 512:(cb + 1) * 512],
                         in0=m["pd"][:, :].bitcast(BF16)[:, 0:512], scalar=0.5,
                         in1=xtile(K, j)[:, cb * 512:(cb + 1) * 512],
                         op0=ALU.mult, op1=ALU.add),
                     ins=[("pd",), ("x", j)], outs=[("x", j)],
                     tag=S + f"res{j}_{cb}")


L0A_PHASES = ("proj", "ag", "vsum", "attn", "wo")


def emit_l0attn(K, site):
    p, m, d = K.prog, K.mem, K.dram
    S = f"a{site}_"
    emit_ln(K, tag=S + "ln")

    p.op("gpsimd", lambda e: e.memset(m["ones_rowb"][:, :], 1.0),
         outs=[("ones_rowb",)], tag=S + "orb")
    p.op("gpsimd", lambda e: e.memset(m["ones_colb"][:, :], 1.0),
         outs=[("ones_colb",)], tag=S + "ocb")

    for g in range(2) if "proj" in L0A_PHASES else ():
        emit_hfm(K, groups=(g,), tag=S + f"hfm{g}_")
        for nm, wname in (("q", "wq"), ("k", "wk")):
            for dhc in range(16):
                slot = dhc % 2
                wk_key = ("wb", slot)
                _stream_lhsT(K, "gpsimd", wname, 2048, dhc * 128, 16,
                             m["wb"][:, slot * 2048:(slot + 1) * 2048],
                             wk_key, S + f"ldw{nm}{g}_{dhc}")
                ps = "pa" if slot == 0 else "pb"
                grp = PsumGroup()
                for kc in range(16):
                    fl = grp.next_flags()
                    p.op("tensor",
                         lambda e, kc=kc, slot=slot, ps=ps, fl=fl:
                         e.matmul(m[ps][:, :],
                                  m["wb"][:, slot * 2048 + kc * 128: slot * 2048 + (kc + 1) * 128],
                                  m["p3"][:, kc * 512:(kc + 1) * 512],
                                  start=fl["start"], stop=fl["stop"]),
                         ins=[wk_key, ("p3", kc)], outs=[(ps,)],
                         tag=S + f"mm{nm}{g}_{dhc}_{kc}")
                grp.close()
                off = (0 if nm == "q" else 16384) + dhc * 1024 + g * 512
                dstk = ("p2", (0 if nm == "q" else 32) + 2 * dhc + g)
                p.op("scalar",
                     lambda e, ps=ps, off=off: e.activation(
                         m["p2"][:, off:off + 512], m[ps][:, :], AF.Identity),
                     ins=[(ps,)], outs=[dstk], tag=S + f"ev{nm}{g}_{dhc}")
        def fka(e, g=g):
            src = bass.AP(m.t["p2"], 16384 + g * 512,
                          [[32768, 128], [1024, 16], [1, 512]])
            dst = d["agin"].ap()[0:1, :].squeeze(0).rearrange(
                "(a b c) -> a b c", b=128, c=1024)[:, :, g * 512:(g + 1) * 512] \
                .transpose([1, 0, 2])
            return e.dma_start(out=dst, in_=src)
        p.dma("sync", fka, ins=[("p2", 32 + i) for i in range(32)],
              outs=[("aginK",)], tag=S + f"kag{g}")
        for tt in range(4):
            slot = tt % 2
            grp = PsumGroup()
            for kc in range(16):
                def fwv(e, kc=kc, slot=slot):
                    src = bass.AP(d["wv"], kc * 128 * 2048, [[2048, 128], [1, 2048]])
                    return e.dma_start(out=m["wb"][:, (2 + slot) * 2048:(3 + slot) * 2048], in_=src)
                p.dma("gpsimd", fwv, outs=[("wb", 2 + slot)], tag=S + f"lwv{g}_{tt}_{kc}")
                for nn in range(4):
                    fl = grp.next_flags(nn)
                    p.op("tensor",
                         lambda e, kc=kc, tt=tt, nn=nn, slot=slot, fl=fl:
                         e.matmul(m["pbig"][:, nn * 512:(nn + 1) * 512],
                                  m["p3"][:, kc * 512 + tt * 128: kc * 512 + tt * 128 + 128],
                                  m["wb"][:, (2 + slot) * 2048 + nn * 512:(2 + slot) * 2048 + (nn + 1) * 512],
                                  start=fl["start"], stop=fl["stop"]),
                         ins=[("wb", 2 + slot), ("p3", kc)], outs=[("pbig",)],
                         tag=S + f"mmv{g}_{tt}_{kc}_{nn}")
            grp.close()
            j = 4 * g + tt
            p.op("scalar",
                 lambda e, j=j: e.activation(m["p1"][:, j * D:(j + 1) * D],
                                             m["pbig"][:, :], AF.Identity),
                 ins=[("pbig",)], outs=[("p1", j)], tag=S + f"evv{g}_{tt}")
            def fva(e, j=j):
                dst = d["agin"].ap()[1:2, :].squeeze(0).rearrange(
                    "(a b c) -> a b c", b=128, c=2048)[j:j + 1].squeeze(0)
                return e.dma_start(out=dst, in_=m["p1"][:, j * D:(j + 1) * D])
            p.dma("sync", fva, ins=[("p1", j)], outs=[("aginV", j)], tag=S + f"vag{g}_{tt}")

    if "ag" not in L0A_PHASES:
        return
    p.collective(
        lambda e: e.collective_compute(
            "AllGather", ALU.bypass,
            replica_groups=[[0, 1], [2, 3], [4, 5], [6, 7]],
            ins=[d["agin"].ap()],
            outs=[d["agout"].ap()]),
        ins=[("aginK",)] + [("aginV", j) for j in range(NT)],
        outs=[("agoutK",), ("agoutV",)], tag=S + "ag")

    if "vsum" not in L0A_PHASES:
        return
    p.dma("gpsimd", lambda e: e.dma_start(out=m["p3"][:, 6656:6912], in_=d["onehot"].ap()),
          outs=[("p3", 13)], tag=S + "ld_oh")
    p.dma("gpsimd", lambda e: e.dma_start(out=m["suffsel"][0:16, :], in_=d["suffsel"].ap()),
          outs=[("suffsel",)], tag=S + "ld_ss")
    grp = PsumGroup()
    for sg in range(16):
        par, jloc = sg % 2, sg // 2
        slot = sg % 2
        def fvt(e, par=par, jloc=jloc, slot=slot):
            src = d["agout"].ap()[par:par + 1].squeeze(0)[2097152:2 * 2097152].rearrange(
                "(a b c) -> a b c", b=128, c=2048)[jloc:jloc + 1].squeeze(0)
            return e.dma_start(out=m["wb"][:, slot * 2048:(slot + 1) * 2048], in_=src)
        p.dma("gpsimd", fvt, ins=[("agoutV",)], outs=[("wb", slot)], tag=S + f"vls{sg}")
        for nn in range(4):
            fl = grp.next_flags(nn)
            p.op("tensor",
                 lambda e, sg=sg, nn=nn, slot=slot, fl=fl:
                 e.matmul(m["pbig"][0:16, nn * 512:(nn + 1) * 512],
                          m["p3"][:, 6656 + sg * 16: 6656 + (sg + 1) * 16],
                          m["wb"][:, slot * 2048 + nn * 512: slot * 2048 + (nn + 1) * 512],
                          start=fl["start"], stop=fl["stop"]),
                 ins=[("p3", 13), ("wb", slot)], outs=[("pbig",)],
                 tag=S + f"vsum{sg}_{nn}")
    grp.close()
    p.op("scalar", lambda e: e.activation(m["p3"][0:16, 0:2048], m["pbig"][0:16, :],
                                          AF.Identity),
         ins=[("pbig",)], outs=[("p3", ci) for ci in range(4)], tag=S + "ev_vsum")
    for nn in range(4):
        ps = "pc" if nn % 2 == 0 else "pd"
        p.op("tensor",
             lambda e, nn=nn, ps=ps:
             e.matmul(m[ps][0:8, :], m["suffsel"][0:16, 0:8],
                      m["p3"][0:16, nn * 512:(nn + 1) * 512],
                      start=True, stop=True),
             ins=[("suffsel",)] + [("p3", nn)], outs=[(ps,)],
             tag=S + f"suff{nn}")
        p.op("scalar",
             lambda e, nn=nn, ps=ps: e.activation(
                 m["p3"][0:8, 2048 + nn * 512: 2048 + (nn + 1) * 512],
                 m[ps][0:8, :], AF.Identity),
             ins=[(ps,)], outs=[("p3", 4 + nn)], tag=S + f"evs{nn}")
    def fsd(e):
        src = bass.AP(m.t["p3"], 2048, [[8192, 8], [1, 2048]])
        return e.dma_start(out=d["suffdram"].ap(), in_=src)
    p.dma("sync", fsd, ins=[("p3", 4 + nn) for nn in range(4)],
          outs=[("suffdram",)], tag=S + "st_suff")

    if "attn" not in L0A_PHASES:
        return
    for j in range(NT):
        nsig = 2 * j + 2
        grp_out = PsumGroup()
        grp_z = PsumGroup()
        def fsl(e, j=j):
            return e.dma_start(out=m["p3"][0:1, 6144:8192],
                               in_=d["suffdram"].ap()[j:j + 1, :])
        p.dma("scalar", fsl, ins=[("suffdram",)],
              outs=[("p3", 12), ("p3", 13), ("p3", 14), ("p3", 15)], tag=S + f"sufl{j}")
        for sg in range(nsig):
            par, jloc = sg % 2, sg // 2
            kslot = sg % 2
            def fkt(e, par=par, jloc=jloc, kslot=kslot):
                src = d["agout"].ap()[par:par + 1].squeeze(0)[0:2097152].rearrange(
                    "(a b c) -> a b c", b=128,
                    c=1024)[:, :, jloc * 128:(jloc + 1) * 128]
                return e.dma_start(out=bass.AP(m.t["wb"], kslot * 2048,
                                               [[8192, 128], [128, 16], [1, 128]]),
                                   in_=src.transpose([1, 0, 2]))
            p.dma("sync", fkt, ins=[("agoutK",)], outs=[("wb", kslot)], tag=S + f"kld{j}_{sg}")
            def fvt2(e, par=par, jloc=jloc, kslot=kslot):
                src = d["agout"].ap()[par:par + 1].squeeze(0)[2097152:2 * 2097152].rearrange(
                    "(a b c) -> a b c", b=128, c=2048)[jloc:jloc + 1].squeeze(0)
                return e.dma_start(out=m["wb"][:, (2 + kslot) * 2048:(3 + kslot) * 2048], in_=src)
            p.dma("gpsimd", fvt2, ins=[("agoutV",)], outs=[("wb", 2 + kslot)], tag=S + f"vld{j}_{sg}")
            for h in range(16):
                pslot = (sg * 16 + h) % 2
                ps = "pb" if pslot == 0 else "pc"
                ptk = ("p3", 8 + pslot)
                ptcol = 4096 + pslot * 128
                p.op("tensor",
                     lambda e, h=h, j=j, kslot=kslot, ps=ps:
                     e.matmul(m[ps][:, 0:128],
                              m["wb"][:, kslot * 2048 + h * 128: kslot * 2048 + (h + 1) * 128],
                              m["p2"][:, h * 1024 + j * 128: h * 1024 + (j + 1) * 128],
                              start=True, stop=True),
                     ins=[("wb", kslot), ("p2", 2 * h + j // 4)], outs=[(ps,)],
                     tag=S + f"st{j}_{sg}_{h}")
                if sg == 2 * j or sg == 2 * j + 1:
                    mk = "mAm" if sg == 2 * j else "mBm"
                    p.op("vector",
                         lambda e, ps=ps, mk=mk: e.tensor_tensor(
                             m[ps][:, 0:128], m[ps][:, 0:128], m[mk][:, :], ALU.mult),
                         ins=[(ps,), ("c", mk)], outs=[(ps,)], tag=S + f"msk{j}_{sg}_{h}")
                p.op("scalar",
                     lambda e, ps=ps, ptcol=ptcol: e.activation(
                         m["p3"][:, ptcol:ptcol + 128], m[ps][:, 0:128], AF.Exp,
                         scale=SQ_T),
                     ins=[(ps,)], outs=[ptk], tag=S + f"exp{j}_{sg}_{h}")
                fo = grp_out.next_flags(h // 4)
                p.op("tensor",
                     lambda e, h=h, kslot=kslot, ptcol=ptcol, fo=fo:
                     e.matmul(m["pbig"][:, h * 128:(h + 1) * 128],
                              m["p3"][:, ptcol:ptcol + 128],
                              m["wb"][:, (2 + kslot) * 2048 + h * 128:(2 + kslot) * 2048 + (h + 1) * 128],
                              start=fo["start"], stop=fo["stop"]),
                     ins=[ptk, ("wb", 2 + kslot)], outs=[("pbig",)],
                     tag=S + f"av{j}_{sg}_{h}")
                fz = grp_z.next_flags()
                p.op("tensor",
                     lambda e, h=h, ptcol=ptcol, fz=fz:
                     e.matmul(m["pa"][:, h:h + 1], m["p3"][:, ptcol:ptcol + 128],
                              m["ones_colb"][:, :],
                              start=fz["start"], stop=fz["stop"]),
                     ins=[ptk, ("ones_colb",)], outs=[("pa",)],
                     tag=S + f"z{j}_{sg}_{h}")
        for h in range(16):
            fo = grp_out.next_flags(h // 4)
            p.op("tensor",
                 lambda e, h=h, fo=fo:
                 e.matmul(m["pbig"][:, h * 128:(h + 1) * 128],
                          m["ones_rowb"][0:1, :],
                          m["p3"][0:1, 6144 + h * 128: 6144 + (h + 1) * 128],
                          start=fo["start"], stop=fo["stop"]),
                 ins=[("ones_rowb",), ("p3", 12), ("p3", 13)], outs=[("pbig",)],
                 tag=S + f"sufv{j}_{h}")
        grp_out.close()
        grp_z.close()
        cnt = float(128 * (14 - 2 * j))
        p.op("vector",
             lambda e, cnt=cnt: e.tensor_scalar(
                 out=m["small"][:, 256:272], in0=m["pa"][:, 0:16],
                 scalar1=cnt, scalar2=None, op0=ALU.add),
             ins=[("pa",)], outs=[("small", 1)], tag=S + f"zc{j}")
        p.op("vector",
             lambda e: e.reciprocal(m["small"][:, 256:272], m["small"][:, 256:272]),
             ins=[("small", 1)], outs=[("small", 1)], tag=S + f"zr{j}")
        for h in range(16):
            p.op("scalar",
                 lambda e, h=h, j=j: e.activation(
                     m["p1"][:, j * D + h * 128: j * D + (h + 1) * 128],
                     m["pbig"][:, h * 128:(h + 1) * 128], AF.Identity,
                     scale=m["small"][:, 256 + h:257 + h]),
                 ins=[("pbig",), ("small", 1)], outs=[("p1", j)],
                 tag=S + f"ao{j}_{h}")

    if "wo" not in L0A_PHASES:
        return
    for kc in range(16):
        def fwo(e, kc=kc):
            src = bass.AP(d["wo"], kc * 128 * 2048, [[2048, 128], [1, 2048]])
            return e.dma_start(out=m["p2"][:, kc * 2048:(kc + 1) * 2048], in_=src)
        p.dma("gpsimd", fwo, outs=[("p2", 4 * kc + i) for i in range(4)],
              tag=S + f"ldwo{kc}")
    for j in range(NT):
        def fat(e, j=j):
            src = m["p1"][:, j * D:(j + 1) * D]
            dst = bass.AP(m.t["p3"], 4096, [[8192, 128], [128, 16], [1, 128]])
            return e.dma_start_transpose(dst, src)
        p.dma("sync", fat, ins=[("p1", j)], outs=[("p3", 8), ("p3", 9), ("p3", 10), ("p3", 11)],
              tag=S + f"aot{j}")
        grp = PsumGroup()
        for kc in range(16):
            for nn in range(4):
                fl = grp.next_flags(nn)
                p.op("tensor",
                     lambda e, kc=kc, nn=nn, fl=fl:
                     e.matmul(m["pbig"][:, nn * 512:(nn + 1) * 512],
                              m["p3"][:, 4096 + kc * 128: 4096 + (kc + 1) * 128],
                              m["p2"][:, kc * 2048 + nn * 512: kc * 2048 + (nn + 1) * 512],
                              start=fl["start"], stop=fl["stop"]),
                     ins=[("p3", 8 + kc // 4), ("p2", 4 * kc + nn)],
                     outs=[("pbig",)], tag=S + f"wo{j}_{kc}_{nn}")
        grp.close()
        p.op("vector",
             lambda e, j=j: e.tensor_tensor(xtile(K, j), xtile(K, j),
                                            m["pbig"][:, :], ALU.add),
             ins=[("pbig",), ("x", j)], outs=[("x", j)], tag=S + f"res{j}")


def emit_l0ffn(K, site):
    p, m, d = K.prog, K.mem, K.dram
    S = f"m{site}_"
    emit_ln(K, tag=S + "ln")
    p.op("gpsimd", lambda e: e.memset(m["ones_rowb"][:, :], 1.0),
         outs=[("ones_rowb",)], tag=S + "orb")
    p.op("gpsimd", lambda e: e.memset(m["ones_colb"][:, :], 1.0),
         outs=[("ones_colb",)], tag=S + "ocb")
    p.op("gpsimd", lambda e: e.memset(m["neg16"][:, :], -1.0e9),
         outs=[("neg16",)], tag=S + "n16")

    p2f = m.t["p2"].ap().bitcast(F32)

    p.dma("sync", lambda e: e.dma_start(out=m["small"][0:1, 288:304], in_=d["gates"].ap()),
          outs=[("small", 1)], tag=S + "ldg")
    p.op("scalar", lambda e: e.activation(m["small"][0:1, 304:320], m["small"][0:1, 288:304],
                                          AF.Exp, scale=-1.0),
         ins=[("small", 1)], outs=[("small", 1)], tag=S + "sg1")
    p.op("vector", lambda e: e.tensor_scalar(out=m["small"][0:1, 304:320],
                                             in0=m["small"][0:1, 304:320],
                                             scalar1=1.0, scalar2=None, op0=ALU.add),
         ins=[("small", 1)], outs=[("small", 1)], tag=S + "sg2")
    p.op("vector", lambda e: e.reciprocal(m["small"][0:1, 304:320], m["small"][0:1, 304:320]),
         ins=[("small", 1)], outs=[("small", 1)], tag=S + "sg3")
    p.op("tensor",
         lambda e: e.matmul(m["pc"][:, 0:16], m["ones_row"][0:1, :],
                            m["small"][0:1, 304:320], start=True, stop=True),
         ins=[("small", 1), ("c", "ones_row")], outs=[("pc",)], tag=S + "sgbcm")
    p.op("vector", lambda e: e.tensor_copy(m["sgbc"][:, :], m["pc"][:, 0:16]),
         ins=[("pc",)], outs=[("sgbc",)], tag=S + "sgbce")

    def fsim(e):
        src = bass.AP(d["sim"], 0, [[16, 128], [128 * 16, 16], [1, 16]])
        return e.dma_start(out=p2f[:, 8448:8704], in_=src)
    p.dma("sync", fsim, outs=[("p2", 33)], tag=S + "ldsim")
    p.op("scalar", lambda e: e.activation(p2f[:, 8192:8448], p2f[:, 8448:8704], AF.Square),
         ins=[("p2", 33)], outs=[("p2", 32)], tag=S + "simsq")
    p.op("tensor",
         lambda e: e.matmul(m["pc"][0:1, 0:256], m["ones_col"][:, 0:1],
                            p2f[:, 8192:8448], start=True, stop=True),
         ins=[("p2", 32), ("c", "ones_col")], outs=[("pc",)], tag=S + "simcs")
    p.op("vector", lambda e: e.tensor_copy(m["small"][0:1, 0:256], m["pc"][0:1, 0:256]),
         ins=[("pc",)], outs=[("small", 0)], tag=S + "simev")
    for step in (128, 64, 32, 16):
        p.op("vector",
             lambda e, step=step: e.tensor_add(m["small"][0:1, 0:step],
                                               m["small"][0:1, 0:step],
                                               m["small"][0:1, step:2 * step]),
             ins=[("small", 0)], outs=[("small", 0)], tag=S + f"simtr{step}")
    p.op("scalar", lambda e: e.activation(m["small"][0:1, 16:32], m["small"][0:1, 0:16], AF.Ln),
         ins=[("small", 0)], outs=[("small", 0)], tag=S + "simln")
    p.op("scalar", lambda e: e.activation(m["small"][0:1, 0:16], m["small"][0:1, 16:32],
                                          AF.Exp, scale=-0.5),
         ins=[("small", 0)], outs=[("small", 0)], tag=S + "simexp")
    p.op("tensor",
         lambda e: e.matmul(m["pc"][:, 0:16], m["ones_row"][0:1, :],
                            m["small"][0:1, 0:16], start=True, stop=True),
         ins=[("small", 0), ("c", "ones_row")], outs=[("pc",)], tag=S + "rsnbc")
    p.op("vector",
         lambda e: e.tensor_tensor(
             p2f[:, 8448:8704], p2f[:, 8448:8704],
             bass.AP(m.t["pc"], 0, [[512, 128], [0, 16], [1, 16]]), ALU.mult),
         ins=[("pc",), ("p2", 33)], outs=[("p2", 33)], tag=S + "simu")
    p.op("tensor",
         lambda e: e.matmul(m["pc"][0:1, 0:256], m["ones_col"][:, 0:1],
                            p2f[:, 8448:8704], start=True, stop=True),
         ins=[("p2", 33), ("c", "ones_col")], outs=[("pc",)], tag=S + "simcs2")
    p.op("vector", lambda e: e.tensor_copy(m["small"][0:1, 0:256], m["pc"][0:1, 0:256]),
         ins=[("pc",)], outs=[("small", 0)], tag=S + "csev")
    for step in (128, 64, 32, 16):
        p.op("vector",
             lambda e, step=step: e.tensor_add(m["small"][0:1, 0:step],
                                               m["small"][0:1, 0:step],
                                               m["small"][0:1, step:2 * step]),
             ins=[("small", 0)], outs=[("small", 0)], tag=S + f"cstr{step}")
    p.op("tensor",
         lambda e: e.matmul(m["pc"][:, 0:16], m["ones_row"][0:1, :],
                            m["small"][0:1, 0:16], start=True, stop=True),
         ins=[("small", 0), ("c", "ones_row")], outs=[("pc",)], tag=S + "csbc")
    p.op("vector", lambda e: e.tensor_copy(m["small"][:, 192:208], m["pc"][:, 0:16]),
         ins=[("pc",)], outs=[("small", 0)], tag=S + "csbc2")

    for j in range(NT):
        for cb in range(4):
            grp = PsumGroup()
            for c4 in range(4):
                ci = cb * 4 + c4
                fl = grp.next_flags(0)
                def ftr(e, ci=ci, c4=c4, j=j):
                    return e.transpose(m["pd"][:, c4 * 128:(c4 + 1) * 128],
                                       xtile(K, j)[:, ci * 128:(ci + 1) * 128],
                                       m["ident"][:, :])
                p.op("tensor", ftr, ins=[("x", j), ("c", "ident")], outs=[("pd",)],
                     tag=S + f"xt{j}_{ci}")
            p.op("scalar",
                 lambda e, cb=cb: e.activation(
                     p2f[:, cb * 512:(cb + 1) * 512],
                     m["pd"][:, :], AF.Identity),
                 ins=[("pd",)], outs=[("p2", cb)], tag=S + f"xte{j}_{cb}")
        grp = PsumGroup()
        for kc in range(16):
            fl = grp.next_flags(0)
            p.op("tensor",
                 lambda e, kc=kc, fl=fl:
                 e.matmul(m["pa"][0:16, 0:128],
                          p2f[:, 8448 + kc * 16: 8448 + (kc + 1) * 16],
                          p2f[:, kc * 128:(kc + 1) * 128],
                          start=fl["start"], stop=fl["stop"]),
                 ins=[("p2", 33)] + [("p2", cb) for cb in range(4)],
                 outs=[("pa",)], tag=S + f"lg{j}_{kc}")
        grp.close()
        p.op("vector", lambda e: e.tensor_copy(m["small"][0:16, 64:192], m["pa"][0:16, 0:128]),
             ins=[("pa",)], outs=[("small", 0)], tag=S + f"lgev{j}")
        p.op("tensor",
             lambda e: e.transpose(m["pd"][:, 0:16], m["small"][0:16, 64:192],
                                   m["ident"][0:16, 0:16]),
             ins=[("small", 0), ("c", "ident")], outs=[("pd",)], tag=S + f"lgtr{j}")
        L = 512
        p.op("scalar",
             lambda e, j=j: e.activation(m["stat"][:, 448 + j:449 + j],
                                         m["stat"][:, j * 2 + 1:j * 2 + 2], AF.Ln),
             ins=[("mv", j)], outs=[("lnvr", j)], tag=S + f"lnv{j}")
        p.op("scalar",
             lambda e, j=j: e.activation(m["stat"][:, 456 + j:457 + j],
                                         m["stat"][:, 448 + j:449 + j], AF.Exp,
                                         scale=-0.5),
             ins=[("lnvr", j)], outs=[("rln", j)], tag=S + f"rln{j}")
        p.op("vector",
             lambda e, j=j: e.tensor_scalar(out=m["stat"][:, 464 + j:465 + j],
                                            in0=m["stat"][:, j * 2:j * 2 + 1],
                                            scalar1=-1.0, scalar2=None, op0=ALU.mult),
             ins=[("mv", j)], outs=[("negm", j)], tag=S + f"negm{j}")
        p.op("vector",
             lambda e, j=j, L=L: e.scalar_tensor_tensor(
                 out=m["small"][:, L:L + 16], in0=m["small"][:, 192:208],
                 scalar=m["stat"][:, 464 + j:465 + j], in1=m["pd"][:, 0:16],
                 op0=ALU.mult, op1=ALU.add),
             ins=[("small", 0), ("negm", j), ("pd",)], outs=[("small", 2)],
             tag=S + f"lgc{j}")
        p.op("vector",
             lambda e, j=j, L=L: e.tensor_scalar(
                 out=m["small"][:, L:L + 16], in0=m["small"][:, L:L + 16],
                 scalar1=m["stat"][:, 456 + j:457 + j],
                 scalar2=float(2048.0 ** -0.5), op0=ALU.mult, op1=ALU.mult),
             ins=[("small", 2), ("rln", j)], outs=[("small", 2)], tag=S + f"lgs{j}")
        p.op("vector",
             lambda e, L=L: e.tensor_sub(m["small"][:, L:L + 16],
                                         m["small"][:, L:L + 16], m["sgbc"][:, :]),
             ins=[("small", 2), ("sgbc",)], outs=[("small", 2)], tag=S + f"lgsg{j}")
        p.op("vector",
             lambda e, L=L: e.tensor_relu(m["small"][:, L + 16:L + 32], m["small"][:, L:L + 16]),
             ins=[("small", 2)], outs=[("small", 2)], tag=S + f"rl{j}")
        p.op("vector",
             lambda e, L=L: e.tensor_scalar(out=m["small"][:, L + 32:L + 48],
                                            in0=m["small"][:, L + 16:L + 32],
                                            scalar1=0.0, scalar2=None, op0=ALU.is_gt),
             ins=[("small", 2)], outs=[("small", 2)], tag=S + f"mk{j}")
        p.op("vector",
             lambda e, L=L: e.reduce_sum(m["small"][:, L + 48:L + 49],
                                         m["small"][:, L + 32:L + 48], axis=AX.X),
             ins=[("small", 2)], outs=[("small", 2)], tag=S + f"rs{j}")
        p.op("vector",
             lambda e, L=L: e.tensor_scalar(out=m["small"][:, L + 48:L + 49],
                                            in0=m["small"][:, L + 48:L + 49],
                                            scalar1=0.5, scalar2=None, op0=ALU.is_lt),
             ins=[("small", 2)], outs=[("small", 2)], tag=S + f"ia{j}")
        p.op("vector",
             lambda e, L=L: e.reduce_max(m["small"][:, L + 49:L + 50],
                                         m["small"][:, L:L + 16], axis=AX.X),
             ins=[("small", 2)], outs=[("small", 2)], tag=S + f"m1{j}")
        p.op("vector",
             lambda e, L=L: e.tensor_scalar(out=m["small"][:, L + 64:L + 80],
                                            in0=m["small"][:, L:L + 16],
                                            scalar1=m["small"][:, L + 49:L + 50],
                                            scalar2=None, op0=ALU.is_ge),
             ins=[("small", 2)], outs=[("small", 2)], tag=S + f"ge1{j}")
        p.op("vector",
             lambda e, L=L: e.scalar_tensor_tensor(
                 out=m["small"][:, L + 80:L + 96], in0=m["small"][:, L + 64:L + 80],
                 scalar=-2.0e9, in1=m["small"][:, L:L + 16],
                 op0=ALU.mult, op1=ALU.add),
             ins=[("small", 2)], outs=[("small", 2)], tag=S + f"msk2{j}")
        p.op("vector",
             lambda e, L=L: e.reduce_max(m["small"][:, L + 50:L + 51],
                                         m["small"][:, L + 80:L + 96], axis=AX.X),
             ins=[("small", 2)], outs=[("small", 2)], tag=S + f"m2{j}")
        p.op("vector",
             lambda e, L=L: e.tensor_scalar(out=m["small"][:, L + 96:L + 112],
                                            in0=m["small"][:, L:L + 16],
                                            scalar1=m["small"][:, L + 50:L + 51],
                                            scalar2=None, op0=ALU.is_ge),
             ins=[("small", 2)], outs=[("small", 2)], tag=S + f"fb{j}")
        p.op("vector",
             lambda e, L=L: e.tensor_sub(m["small"][:, L + 96:L + 112],
                                         m["small"][:, L + 96:L + 112],
                                         m["small"][:, L + 32:L + 48]),
             ins=[("small", 2)], outs=[("small", 2)], tag=S + f"fmd{j}")
        p.op("vector",
             lambda e, L=L: e.scalar_tensor_tensor(
                 out=m["small"][:, L + 32:L + 48],
                 in0=m["small"][:, L + 96:L + 112],
                 scalar=m["small"][:, L + 48:L + 49],
                 in1=m["small"][:, L + 32:L + 48],
                 op0=ALU.mult, op1=ALU.add),
             ins=[("small", 2)], outs=[("small", 2)], tag=S + f"fm{j}")
        p.op("vector",
             lambda e, L=L: e.tensor_scalar(
                 out=m["small"][:, L + 96:L + 112], in0=m["small"][:, L + 32:L + 48],
                 scalar1=1.0e9, scalar2=-1.0e9, op0=ALU.mult, op1=ALU.add),
             ins=[("small", 2)], outs=[("small", 2)], tag=S + f"mlt{j}")
        p.op("vector",
             lambda e, L=L: e.tensor_mul(m["small"][:, L + 112:L + 128],
                                         m["small"][:, L + 16:L + 32],
                                         m["small"][:, L + 32:L + 48]),
             ins=[("small", 2)], outs=[("small", 2)], tag=S + f"mlm{j}")
        p.op("vector",
             lambda e, L=L: e.tensor_add(m["small"][:, L + 112:L + 128],
                                         m["small"][:, L + 112:L + 128],
                                         m["small"][:, L + 96:L + 112]),
             ins=[("small", 2)], outs=[("small", 2)], tag=S + f"ml{j}")
        p.op("scalar",
             lambda e, L=L: e.activation(m["small"][:, L + 112:L + 128],
                                         m["small"][:, L + 112:L + 128], AF.Exp),
             ins=[("small", 2)], outs=[("small", 2)], tag=S + f"sme{j}")
        p.op("vector",
             lambda e, L=L: e.reduce_sum(m["small"][:, L + 51:L + 52],
                                         m["small"][:, L + 112:L + 128], axis=AX.X),
             ins=[("small", 2)], outs=[("small", 2)], tag=S + f"smz{j}")
        p.op("vector",
             lambda e, L=L: e.reciprocal(m["small"][:, L + 51:L + 52],
                                         m["small"][:, L + 51:L + 52]),
             ins=[("small", 2)], outs=[("small", 2)], tag=S + f"smr{j}")
        p.op("vector",
             lambda e, L=L, j=j: e.scalar_tensor_tensor(
                 out=m["stat"][:, 320 + j * 16: 336 + j * 16],
                 in0=m["small"][:, L + 112:L + 128],
                 scalar=m["small"][:, L + 51:L + 52],
                 in1=m["small"][:, L + 32:L + 48],
                 op0=ALU.mult, op1=ALU.mult),
             ins=[("small", 2)], outs=[("wrt", j)], tag=S + f"w{j}")
        p.op("scalar",
             lambda e, j=j: e.activation(m["wrtb"][:, j * 16:(j + 1) * 16],
                                         m["stat"][:, 320 + j * 16: 336 + j * 16],
                                         AF.Identity),
             ins=[("wrt", j)], outs=[("wrtb",)], tag=S + f"wb{j}")

    for g in range(2):
        emit_hfm(K, groups=(g,), tag=S + f"hfm{g}_")
        for tt in range(4):
            j = 4 * g + tt
            for pi, (pnm, wname) in enumerate((("q", "qcat"), ("k", "kcat"), ("v", "vcat"))):
                grp = PsumGroup()
                for kc in range(16):
                    slot = kc % 2
                    def fw(e, kc=kc, slot=slot, wname=wname):
                        src = bass.AP(d[wname], kc * 128 * 2048, [[2048, 128], [1, 2048]])
                        return e.dma_start(out=m["wb"][:, slot * 2048:(slot + 1) * 2048], in_=src)
                    p.dma("gpsimd", fw, outs=[("wb", slot)], tag=S + f"lw{pnm}{j}_{kc}")
                    for nn in range(4):
                        fl = grp.next_flags(nn)
                        p.op("tensor",
                             lambda e, kc=kc, tt=tt, nn=nn, slot=slot, fl=fl:
                             e.matmul(m["pbig"][:, nn * 512:(nn + 1) * 512],
                                      m["p3"][:, kc * 512 + tt * 128: kc * 512 + tt * 128 + 128],
                                      m["wb"][:, slot * 2048 + nn * 512: slot * 2048 + (nn + 1) * 512],
                                      start=fl["start"], stop=fl["stop"]),
                             ins=[("wb", slot), ("p3", kc)], outs=[("pbig",)],
                             tag=S + f"pe{pnm}{j}_{kc}_{nn}")
                grp.close()
                acol = 1536 + pi * 256
                for ei in range(16):
                    if ei == 0:
                        p.op("vector",
                             lambda e, j=j, acol=acol: e.tensor_scalar(
                                 out=m["small"][:, acol:acol + 128],
                                 in0=m["pbig"][:, 0:128],
                                 scalar1=m["stat"][:, 320 + j * 16:321 + j * 16],
                                 scalar2=None, op0=ALU.mult),
                             ins=[("pbig",), ("wrt", j)], outs=[("small", 6 + pi)],
                             tag=S + f"wr{pnm}{j}_0")
                    else:
                        p.op("vector",
                             lambda e, ei=ei, j=j, acol=acol: e.scalar_tensor_tensor(
                                 out=m["small"][:, acol:acol + 128],
                                 in0=m["pbig"][:, ei * 128:(ei + 1) * 128],
                                 scalar=m["stat"][:, 320 + j * 16 + ei:321 + j * 16 + ei],
                                 in1=m["small"][:, acol:acol + 128],
                                 op0=ALU.mult, op1=ALU.add),
                             ins=[("pbig",), ("wrt", j), ("small", 6 + pi)],
                             outs=[("small", 6 + pi)], tag=S + f"wr{pnm}{j}_{ei}")
                p.op("scalar",
                     lambda e, j=j, pi=pi, acol=acol: e.activation(
                         m["p1"][:, pi * 1024 + j * 128: pi * 1024 + (j + 1) * 128],
                         m["small"][:, acol:acol + 128], AF.Identity),
                     ins=[("small", 6 + pi)], outs=[("p1", pi // 2)],
                     tag=S + f"cv{pnm}{j}")

    for pi, pnm in ((0, "q"), (1, "k")):
        for j in range(NT):
            def ftr2(e, pi=pi, j=j):
                pdb = m["pd"][:, :].bitcast(BF16)
                return e.transpose(pdb[:, 0:128],
                                   m["p1"][:, pi * 1024 + j * 128: pi * 1024 + (j + 1) * 128],
                                   m["identb"][:, :])
            p.op("tensor", ftr2, ins=[("p1", pi // 2), ("c", "identb")],
                 outs=[("pd",)], tag=S + f"tf{pnm}{j}")
            p.op("scalar",
                 lambda e, pi=pi, j=j: e.activation(
                     m["p1"][:, 4096 + pi * 1024 + j * 128: 4096 + pi * 1024 + (j + 1) * 128],
                     m["pd"][:, :].bitcast(BF16)[:, 0:128], AF.Identity),
                 ins=[("pd",)], outs=[("p1", 2)], tag=S + f"tfe{pnm}{j}")

    def fag1(e):
        dst = d["agin_s"].ap()[0:131072].rearrange("(a b) -> a b", b=1024)
        return e.dma_start(out=dst, in_=m["p1"][:, 5120:6144])
    p.dma("sync", fag1, ins=[("p1", 2)], outs=[("agins", 0)], tag=S + "agk")
    def fag2(e):
        dst = bass.AP(d["agin_s"], 131072, [[128, 128], [16384, 8], [1, 128]])
        src = bass.AP(m.t["p1"], 2048, [[16384, 128], [128, 8], [1, 128]])
        return e.dma_start(out=dst, in_=src)
    p.dma("sync", fag2, ins=[("p1", 1)], outs=[("agins", 1)], tag=S + "agv")
    p.collective(
        lambda e: e.collective_compute(
            "AllGather", ALU.bypass,
            replica_groups=[[0, 1], [2, 3], [4, 5], [6, 7]],
            ins=[d["agin_s"].ap()], outs=[d["agout_s"].ap()]),
        ins=[("agins", 0), ("agins", 1)], outs=[("agouts",)], tag=S + "ag")

    for j in range(NT):
        grp_o = PsumGroup()
        grp_z = PsumGroup()
        for sg in range(2 * j + 2):
            par, jloc = sg % 2, sg // 2
            kslot = sg % 2
            def fk2(e, par=par, jloc=jloc, kslot=kslot):
                src = d["agout_s"].ap()[par:par + 1].squeeze(0)[0:131072] \
                    .rearrange("(a b) -> a b", b=1024)[:, jloc * 128:(jloc + 1) * 128]
                return e.dma_start(out=m["wb"][:, kslot * 2048: kslot * 2048 + 128], in_=src)
            p.dma("sync", fk2, ins=[("agouts",)], outs=[("wb", kslot)], tag=S + f"kl{j}_{sg}")
            def fv2(e, par=par, jloc=jloc, kslot=kslot):
                src = d["agout_s"].ap()[par:par + 1].squeeze(0)[131072:262144] \
                    .rearrange("(a b c) -> a b c", a=8, b=128)[jloc:jloc + 1].squeeze(0)
                return e.dma_start(out=m["wb"][:, (2 + kslot) * 2048: (2 + kslot) * 2048 + 128], in_=src)
            p.dma("gpsimd", fv2, ins=[("agouts",)], outs=[("wb", 2 + kslot)], tag=S + f"vl{j}_{sg}")
            pslot = sg % 2
            ps = "pb" if pslot == 0 else "pc"
            ptk = ("p3", 12 + pslot)
            ptcol = 6144 + pslot * 512
            p.op("tensor",
                 lambda e, j=j, kslot=kslot, ps=ps:
                 e.matmul(m[ps][:, 0:128],
                          m["wb"][:, kslot * 2048: kslot * 2048 + 128],
                          m["p1"][:, 4096 + j * 128: 4096 + (j + 1) * 128],
                          start=True, stop=True),
                 ins=[("wb", kslot), ("p1", 2)], outs=[(ps,)], tag=S + f"s{j}_{sg}")
            if sg == 2 * j or sg == 2 * j + 1:
                mk = "mAa" if sg == 2 * j else "mBa"
                p.op("vector",
                     lambda e, ps=ps, mk=mk: e.tensor_tensor(
                         m[ps][:, 0:128], m[ps][:, 0:128], m[mk][:, :], ALU.add),
                     ins=[(ps,), ("c", mk)], outs=[(ps,)], tag=S + f"mka{j}_{sg}")
            p.op("scalar",
                 lambda e, ps=ps, ptcol=ptcol: e.activation(
                     m["p3"][:, ptcol:ptcol + 128], m[ps][:, 0:128], AF.Exp, scale=SQ_H),
                 ins=[(ps,)], outs=[ptk], tag=S + f"ex{j}_{sg}")
            fo = grp_o.next_flags(0)
            p.op("tensor",
                 lambda e, ptcol=ptcol, kslot=kslot, fo=fo:
                 e.matmul(m["pa"][:, 0:128],
                          m["p3"][:, ptcol:ptcol + 128],
                          m["wb"][:, (2 + kslot) * 2048: (2 + kslot) * 2048 + 128],
                          start=fo["start"], stop=fo["stop"]),
                 ins=[ptk, ("wb", 2 + kslot)], outs=[("pa",)], tag=S + f"o{j}_{sg}")
            fz = grp_z.next_flags(0)
            p.op("tensor",
                 lambda e, ptcol=ptcol, fz=fz:
                 e.matmul(m["pd"][:, 0:1],
                          m["p3"][:, ptcol:ptcol + 128],
                          m["ones_colb"][:, :],
                          start=fz["start"], stop=fz["stop"]),
                 ins=[ptk, ("ones_colb",)], outs=[("pd",)], tag=S + f"zz{j}_{sg}")
        grp_o.close()
        grp_z.close()
        p.op("vector",
             lambda e: e.tensor_copy(m["small"][:, 272:273], m["pd"][:, 0:1]),
             ins=[("pd",)], outs=[("small", 1)], tag=S + f"zc{j}")
        p.op("vector",
             lambda e: e.reciprocal(m["small"][:, 272:273], m["small"][:, 272:273]),
             ins=[("small", 1)], outs=[("small", 1)], tag=S + f"zr{j}")
        p.op("scalar",
             lambda e, j=j: e.activation(
                 m["p1"][:, 3072 + j * 128: 3072 + (j + 1) * 128],
                 m["pa"][:, 0:128], AF.Identity,
                 scale=m["small"][:, 272:273]),
             ins=[("pa",), ("small", 1)], outs=[("p1", 1)], tag=S + f"aoev{j}")

    for j in range(NT):
        def fz(e, j=j):
            in0 = bass.AP(m.t["p1"], 3072 + j * 128, [[16384, 128], [0, 16], [1, 128]])
            in1 = bass.AP(m.t["wrtb"], j * 16, [[128, 128], [1, 16], [0, 128]])
            return e.tensor_tensor(m["p2"][:, j * 2048:(j + 1) * 2048], in0, in1, ALU.mult)
        p.op("vector", fz, ins=[("p1", 1), ("wrtb",)],
             outs=[("p2", 4 * j + i) for i in range(4)], tag=S + f"zt{j}")
        def fzt(e, j=j):
            src = m["p2"][:, j * 2048:(j + 1) * 2048]
            dst = bass.AP(m.t["p2"], 16384 + j * 128, [[32768, 128], [1024, 16], [1, 128]])
            return e.dma_start_transpose(dst, src)
        p.dma("sync", fzt, ins=[("p2", 4 * j + i) for i in range(4)],
              outs=[("p2", 32 + i) for i in range(32)], tag=S + f"ztt{j}")
    for j in range(NT):
        grp = PsumGroup()
        for kc in range(16):
            slot = kc % 2
            def fwo(e, kc=kc, slot=slot):
                src = bass.AP(d["ocat"], kc * 128 * 2048, [[2048, 128], [1, 2048]])
                return e.dma_start(out=m["wb"][:, slot * 2048:(slot + 1) * 2048], in_=src)
            p.dma("gpsimd", fwo, outs=[("wb", slot)], tag=S + f"lwo{j}_{kc}")
            for nn in range(4):
                fl = grp.next_flags(nn)
                p.op("tensor",
                     lambda e, kc=kc, nn=nn, slot=slot, j=j, fl=fl:
                     e.matmul(m["pbig"][:, nn * 512:(nn + 1) * 512],
                              bass.AP(m.t["p2"], 16384 + kc * 1024 + j * 128,
                                      [[32768, 128], [1, 128]]),
                              m["wb"][:, slot * 2048 + nn * 512: slot * 2048 + (nn + 1) * 512],
                              start=fl["start"], stop=fl["stop"]),
                     ins=[("wb", slot), ("p2", 32 + 2 * kc), ("p2", 33 + 2 * kc)],
                     outs=[("pbig",)], tag=S + f"op{j}_{kc}_{nn}")
        grp.close()
        p.op("vector",
             lambda e, j=j: e.scalar_tensor_tensor(
                 out=xtile(K, j), in0=m["pbig"][:, :], scalar=0.5,
                 in1=xtile(K, j), op0=ALU.mult, op1=ALU.add),
             ins=[("pbig",), ("x", j)], outs=[("x", j)], tag=S + f"res{j}")


MB = {
    "mbcd": 4096, "mdtr": 4728, "mdts": 5240, "mbcs": 5752, "msta": 7064,
    "tail0": 7576, "tail1": 7972,
}
MGT, MPT, MBTMB = 8192, 8320, 8448


def p2b(K, col, n):
    return K.mem["p2"][:, col:col + n]


def p1f(K, col, n):
    return K.mem.t["p1"].ap().bitcast(F32)[:, col:col + n]


def p1b(K, col, n):
    return K.mem["p1"][:, col:col + n]


def emit_l1attn(K, site):
    p, m, d = K.prog, K.mem, K.dram
    S = "s_"
    p.dma("sync", lambda e: e.dma_start(out=m["mabd"][64:128, 0:1], in_=d["alog"].ap()),
          outs=[("mabd",)], tag=S + "lda")
    p.dma("sync", lambda e: e.dma_start(out=m["mabd"][64:128, 1:2], in_=d["dtbias"].ap()),
          outs=[("mabd",)], tag=S + "ldb")
    p.op("scalar", lambda e: e.activation(m["mabd"][64:128, 2:3], m["mabd"][64:128, 0:1], AF.Exp),
         ins=[("mabd",)], outs=[("mabd",)], tag=S + "expA")
    p.op("vector", lambda e: e.tensor_scalar(out=m["mabd"][64:128, 2:3],
                                             in0=m["mabd"][64:128, 2:3],
                                             scalar1=-1.0, scalar2=None, op0=ALU.mult),
         ins=[("mabd",)], outs=[("mabd",)], tag=S + "negA")
    for nm in ("mcwx", "mcwbc", "mcbx", "mcbbc", "mnw", "tailw_bc", "parw_bc"):
        p.dma("sync", lambda e, nm=nm: e.dma_start(out=m[nm][:, :], in_=d[nm].ap()),
              outs=[(nm,)], tag=S + "ld" + nm)
    p.dma("sync", lambda e: e.dma_start(out=m["mrow"][0:1, :], in_=d["mrow"].ap()),
          outs=[("mrow",)], tag=S + "ldmrow")
    p.op("tensor",
         lambda e: e.matmul(m["pd"][:, 0:64], m["ones_row"][0:1, :],
                            m["mrow"][0:1, 0:64], start=True, stop=True),
         ins=[("mrow",), ("c", "ones_row")], outs=[("pd",)], tag=S + "dskbc")
    p.op("vector", lambda e: e.tensor_copy(m["small"][:, 2048:2112], m["pd"][:, 0:64]),
         ins=[("pd",)], outs=[("small", 8)], tag=S + "dskev")
    p.op("vector", lambda e: e.memset(
        m.t["p2"].ap().bitcast(F32)[0:16, 0:4096], 0.0),
        outs=[("p2", i) for i in range(16)], tag=S + "zeroT")
    p.dma("sync", lambda e: e.dma_start(out=d["msdram"].ap(),
                                        in_=m.t["p2"].ap().bitcast(F32)[0:16, 0:4096]),
          ins=[("p2", i) for i in range(16)], outs=[("msdram",)], tag=S + "savT0")
    for g in range(2):
        _mamba_group(K, site, g)
        if DBG and g == 0:
            p.dma("sync", lambda e: e.dma_start(out=d["dbg"].ap()[:, 0:8192],
                                                in_=m["x"][:, 0:8192]),
                  ins=[("x", j) for j in range(4)], outs=[("dbgt3",)],
                  tag=S + "tap3")
            p.dma("sync",
                  lambda e: e.dma_start(out=d["dbg"].ap()[0:16, 8192:12288],
                                        in_=d["msdram"].ap()),
                  ins=[("msdram",)], outs=[("dbgt4",)], tag=S + "tap4")


def _mamba_group(K, site, g):
    p, m, d = K.prog, K.mem, K.dram
    S = f"s{g}_"
    p2f = m.t["p2"].ap().bitcast(F32)
    emit_ln(K, tiles=range(4 * g, 4 * g + 4), colbase=-8192 * g, tag=S + "ln")
    emit_hfm(K, groups=(g,), src_colbase=-8192 * g, tag=S + "hfm")

    for tt in range(4):
        for half in range(2):
            grp = PsumGroup()
            for kc in range(16):
                slot = kc % 2
                def fw(e, kc=kc, slot=slot, half=half):
                    src = bass.AP(d["winp"], kc * 128 * 8320 + half * 2048,
                                  [[8320, 128], [1, 2048]])
                    return e.dma_start(out=m["wb"][:, slot * 2048:(slot + 1) * 2048], in_=src)
                p.dma("gpsimd", fw, outs=[("wb", slot)], tag=S + f"lwg{tt}_{half}_{kc}")
                for nn in range(4):
                    fl = grp.next_flags(nn)
                    p.op("tensor",
                         lambda e, kc=kc, tt=tt, nn=nn, slot=slot, fl=fl:
                         e.matmul(m["pbig"][:, nn * 512:(nn + 1) * 512],
                                  m["p3"][:, kc * 512 + tt * 128: kc * 512 + tt * 128 + 128],
                                  m["wb"][:, slot * 2048 + nn * 512: slot * 2048 + (nn + 1) * 512],
                                  start=fl["start"], stop=fl["stop"]),
                         ins=[("wb", slot), ("p3", kc)], outs=[("pbig",)],
                         tag=S + f"gmm{tt}_{half}_{kc}_{nn}")
            grp.close()
            p.op("scalar",
                 lambda e, half=half: e.activation(
                     m["wb"][:, 4096 + half * 2048: 4096 + (half + 1) * 2048],
                     m["pbig"][:, :], AF.Silu),
                 ins=[("pbig",)], outs=[("wb", 2 + half)], tag=S + f"gsilu{tt}_{half}")
        def fgd(e, tt=tt, g=g):
            return e.dma_start(out=d["gdram"].ap()[4 * g + tt: 4 * g + tt + 1].squeeze(0),
                               in_=m["wb"][:, 4096:8192])
        p.dma("sync", fgd, ins=[("wb", 2), ("wb", 3)], outs=[("gdram", 4 * g + tt)],
              tag=S + f"gst{tt}")
    for mc in range(32):
        slot = mc % 2
        def fw2(e, mc=mc, slot=slot):
            src = bass.AP(d["winp"], 4096 + mc * 128,
                          [[8320, 128], [128 * 8320, 16], [1, 128]])
            return e.dma_start(out=m["wb"][:, slot * 2048:(slot + 1) * 2048], in_=src)
        p.dma("gpsimd", fw2, outs=[("wb", slot)], tag=S + f"lwx{mc}")
        ps = "pa" if slot == 0 else "pb"
        grp = PsumGroup()
        for kc in range(16):
            fl = grp.next_flags(0)
            p.op("tensor",
                 lambda e, kc=kc, slot=slot, ps=ps, fl=fl:
                 e.matmul(m[ps][:, :],
                          m["wb"][:, slot * 2048 + kc * 128: slot * 2048 + (kc + 1) * 128],
                          m["p3"][:, kc * 512:(kc + 1) * 512],
                          start=fl["start"], stop=fl["stop"]),
                 ins=[("wb", slot), ("p3", kc)], outs=[(ps,)], tag=S + f"xmm{mc}_{kc}")
        grp.close()
        p.op("scalar",
             lambda e, ps=ps, mc=mc: e.activation(m["p2"][:, mc * 512:(mc + 1) * 512],
                                                  m[ps][:, :], AF.Identity),
             ins=[(ps,)], outs=[("p2", mc)], tag=S + f"xev{mc}")
    grp = PsumGroup()
    for kc in range(16):
        slot = kc % 2
        def fw3(e, kc=kc, slot=slot):
            src = bass.AP(d["winp"], 8192 + kc * 128 * 8320, [[8320, 128], [1, 128]])
            return e.dma_start(out=m["wb"][:, slot * 2048: slot * 2048 + 128], in_=src)
        p.dma("gpsimd", fw3, outs=[("wb", slot)], tag=S + f"lwb{kc}")
        for nn in range(4):
            fl = grp.next_flags(0)
            p.op("tensor",
                 lambda e, kc=kc, nn=nn, slot=slot, fl=fl:
                 e.matmul(m["pc"][:, nn * 128:(nn + 1) * 128],
                          m["wb"][:, slot * 2048: slot * 2048 + 128],
                          m["p3"][:, kc * 512 + nn * 128: kc * 512 + (nn + 1) * 128],
                          start=fl["start"], stop=fl["stop"]),
                 ins=[("wb", slot), ("p3", kc)], outs=[("pc",)], tag=S + f"bmm{kc}_{nn}")
    grp.close()
    for tc in range(4):
        p.op("scalar",
             lambda e, tc=tc: e.activation(
                 p1f(K, MB["mbcd"] + tc * 133 + 3, 128)[0:48, :],
                 m["pc"][0:48, tc * 128:(tc + 1) * 128], AF.Identity),
             ins=[("pc",)], outs=[("mbcd",)], tag=S + f"bcev{tc}")
    p.op("scalar",
         lambda e: e.activation(p1f(K, MB["mdtr"], 512)[64:128, :],
                                m["pc"][64:128, :], AF.Identity),
         ins=[("pc",)], outs=[("mdtr",)], tag=S + "dtev")

    for tc in range(4):
        def ft1(e, tc=tc):
            src = bass.AP(m.t["p2"], tc * 128 + 125, [[32768, 128], [512, 32], [1, 3]])
            dst = bass.AP(d["agin_t"], tc * 33 * 128 * 4, [[4, 128], [128 * 4, 32], [1, 3]])
            return e.dma_start(out=dst, in_=src)
        p.dma("sync", ft1, ins=[("p2", i) for i in range(32)], outs=[("agint", tc, 0)],
              tag=S + f"tl{tc}")
        def ft2(e, tc=tc):
            src = p1f(K, MB["mbcd"] + tc * 133 + 128, 3)
            dst = bass.AP(d["agin_t"], (tc * 33 + 32) * 128 * 4, [[4, 128], [1, 3]])
            return e.dma_start(out=dst, in_=src)
        p.dma("gpsimd", ft2, ins=[("mbcd",)], outs=[("agint", tc, 1)], tag=S + f"tlb{tc}")
    agout_t = "agout_tA" if g == 0 else "agout_tB"
    p.collective(
        lambda e, agout_t=agout_t: e.collective_compute(
            "AllGather", ALU.bypass,
            replica_groups=[[0, 1], [2, 3], [4, 5], [6, 7]],
            ins=[d["agin_t"].ap()], outs=[d[agout_t].ap()]),
        ins=[("agint", tc, i) for tc in range(4) for i in range(2)],
        outs=[(agout_t,)], tag=S + "agt")

    for tc in range(4):
        j = 4 * g + tc
        agA, idxA = agout_t, tc
        if tc > 0:
            agB, idxB = agout_t, tc - 1
        elif g == 1:
            agB, idxB = "agout_tA", 3
        else:
            agB, idxB = agout_t, 0
        def flA2(e, agA=agA, idxA=idxA):
            src = bass.AP(d[agA], idxA * 33 * 128 * 4, [[4, 128], [128 * 4, 33], [1, 3]])
            dst = p1f(K, MB["tail1"], 99).rearrange("p (a b) -> p a b", b=3)
            return e.dma_start(out=dst, in_=src)
        p.dma("gpsimd", flA2, ins=[(agA,)], outs=[("tailc",)], tag=S + f"tbA{tc}")
        p.op("vector",
             lambda e, j=j, tc=tc: e.tensor_scalar(
                 out=p1f(K, MB["tail0"] + tc * 99, 99), in0=p1f(K, MB["tail1"], 99),
                 scalar1=m["tailw_bc"][:, 2 * j:2 * j + 1], scalar2=None, op0=ALU.mult),
             ins=[("tailc",), ("tailw_bc",)], outs=[("tails",)], tag=S + f"tbs{tc}")
        def flB(e, agB=agB, idxB=idxB):
            src = bass.AP(d[agB], 4 * 33 * 128 * 4 + idxB * 33 * 128 * 4,
                          [[4, 128], [128 * 4, 33], [1, 3]])
            dst = p1f(K, MB["tail1"], 99).rearrange("p (a b) -> p a b", b=3)
            return e.dma_start(out=dst, in_=src)
        p.dma("gpsimd", flB, ins=[(agB,), ("tails",)], outs=[("tailc",)], tag=S + f"tbB{tc}")
        p.op("vector",
             lambda e, j=j, tc=tc: e.scalar_tensor_tensor(
                 out=p1f(K, MB["tail0"] + tc * 99, 99), in0=p1f(K, MB["tail1"], 99),
                 scalar=m["tailw_bc"][:, 2 * j + 1:2 * j + 2],
                 in1=p1f(K, MB["tail0"] + tc * 99, 99), op0=ALU.mult, op1=ALU.add),
             ins=[("tailc",), ("tails",), ("tailw_bc",)], outs=[("tails",)],
             tag=S + f"tbb{tc}")
        p.op("vector",
             lambda e, tc=tc: e.tensor_copy(
                 p1f(K, MB["mbcd"] + tc * 133, 3)[0:48, :],
                 p1f(K, MB["tail0"] + tc * 99 + 96, 3)[0:48, :]),
             ins=[("tails",)], outs=[("mbcd",)], tag=S + f"bcpre{tc}")

    for mc in range(32):
        ps = "pa" if mc % 2 == 0 else "pb"
        pkeys = [("p2", mc)]
        for k in range(4):
            nb = 3 - k
            if nb > 0:
                def fbnd(e, k=k, mc=mc, nb=nb, ps=ps, first=(k == 0)):
                    out = bass.AP(m.t[ps], 0, [[512, 128], [128, 4], [1, nb]])
                    in0 = bass.AP(m.t["p1"], 2 * (MB["tail0"] + mc * 3 + k),
                                  [[16384, 128], [99 * 2, 4], [2, nb]]).bitcast(F32) \
                        if False else None
                    in0 = p1f(K, MB["tail0"], 396).rearrange(
                        "p (t a b) -> p t a b", t=4, b=3)[:, :, mc, k:3]
                    if first:
                        in1 = m["mcbx"][:, mc:mc + 1].broadcast_to([128, 4, nb]) \
                            if False else bass.AP(m.t["mcbx"], mc, [[32, 128], [0, 4], [0, nb]])
                    else:
                        in1 = bass.AP(m.t[ps], 0, [[512, 128], [128, 4], [1, nb]])
                    return e.scalar_tensor_tensor(
                        out=out, in0=in0, scalar=m["mcwx"][:, mc * 4 + k: mc * 4 + k + 1],
                        in1=in1, op0=ALU.mult, op1=ALU.add)
                p.op("vector", fbnd, ins=[("tails",), ("mcwx",), ("mcbx",), (ps,)],
                     outs=[(ps,)], tag=S + f"cvb{mc}_{k}")
            def fmain(e, k=k, mc=mc, nb=nb, ps=ps, first=(k == 0)):
                out = bass.AP(m.t[ps], nb, [[512, 128], [128, 4], [1, 128 - nb]])
                in0 = bass.AP(m.t["p2"], mc * 512, [[32768, 128], [128, 4], [1, 128 - nb]])
                if first:
                    in1 = bass.AP(m.t["mcbx"], mc, [[32, 128], [0, 4], [0, 128 - nb]])
                else:
                    in1 = out
                return e.scalar_tensor_tensor(
                    out=out, in0=in0, scalar=m["mcwx"][:, mc * 4 + k: mc * 4 + k + 1],
                    in1=in1, op0=ALU.mult, op1=ALU.add)
            p.op("vector", fmain, ins=pkeys + [("mcwx",), ("mcbx",), (ps,)],
                 outs=[(ps,)], tag=S + f"cvm{mc}_{k}")
        p.op("scalar",
             lambda e, mc=mc, ps=ps: e.activation(
                 bass.AP(m.t["p1"], mc * 128, [[16384, 128], [4096, 2], [1, 128]]),
                 m[ps][:, 0:256], AF.Silu),
             ins=[(ps,)], outs=[("p1", 0), ("p1", 1), ("p1", 2), ("p1", 3)],
             tag=S + f"cse{mc}_a")
        p.op("scalar",
             lambda e, mc=mc, ps=ps: e.activation(
                 bass.AP(m.t["p3"], mc * 128, [[8192, 128], [4096, 2], [1, 128]]),
                 m[ps][:, 256:512], AF.Silu),
             ins=[(ps,)], outs=[("p3", i) for i in range(16)],
             tag=S + f"cse{mc}_b")

    for tc in range(4):
        j = 4 * g + tc
        for k in range(4):
            def fbc(e, k=k, tc=tc, first=(k == 0)):
                out = m["pd"][0:48, 0:128]
                in0 = p1f(K, MB["mbcd"] + tc * 133 + k, 128)[0:48, :]
                if first:
                    in1 = bass.AP(m.t["mcbbc"], 0, [[1, 128], [0, 128]])[0:48, :]
                else:
                    in1 = out
                return e.scalar_tensor_tensor(
                    out=out, in0=in0, scalar=m["mcwbc"][0:48, k:k + 1],
                    in1=in1, op0=ALU.mult, op1=ALU.add)
            p.op("vector", fbc, ins=[("mbcd",), ("mcwbc",), ("mcbbc",), ("pd",)],
                 outs=[("pd",)], tag=S + f"cvbc{tc}_{k}")
        p.op("scalar",
             lambda e, tc=tc: e.activation(p1f(K, MB["mbcs"] + tc * 128, 128)[0:48, :],
                                           m["pd"][0:48, 0:128], AF.Silu),
             ins=[("pd",)], outs=[("mbcs",)], tag=S + f"bcsilu{tc}")
        p.op("tensor",
             lambda e, tc=tc: e.transpose(m["pd"][:, 64:80],
                                          p1f(K, MB["mbcs"] + tc * 128, 128)[32:48, :],
                                          m["ident"][32:48, 32:48]),
             ins=[("mbcs",), ("c", "ident")], outs=[("pd",)], tag=S + f"ct1{tc}")
        p.op("scalar",
             lambda e: e.activation(p1f(K, MB["mbcs"] + 1024, 16),
                                    m["pd"][:, 64:80], AF.Identity),
             ins=[("pd",)], outs=[("mbcs",)], tag=S + f"ct1e{tc}")
        p.op("tensor",
             lambda e: e.transpose(m["pd"][0:16, 0:128],
                                   p1f(K, MB["mbcs"] + 1024, 16),
                                   m["ident"][:, :]),
             ins=[("mbcs",), ("c", "ident")], outs=[("pd",)], tag=S + f"ct2{tc}")
        p.op("scalar",
             lambda e, tc=tc: e.activation(p1f(K, MB["mbcs"] + 512 + tc * 128, 128)[0:16, :],
                                           m["pd"][0:16, 0:128], AF.Identity),
             ins=[("pd",)], outs=[("mbcs",)], tag=S + f"ct2e{tc}")

        dts = MB["mdts"]
        p.op("vector",
             lambda e, tc=tc: e.tensor_scalar(
                 out=p1f(K, dts, 128)[64:128, :],
                 in0=p1f(K, MB["mdtr"] + tc * 128, 128)[64:128, :],
                 scalar1=m["mabd"][64:128, 1:2], scalar2=20.0,
                 op0=ALU.add, op1=ALU.min),
             ins=[("mdtr",), ("mabd",)], outs=[("mdts",)], tag=S + f"dtz{tc}")
        p.op("scalar",
             lambda e: e.activation(p1f(K, dts + 128, 128)[64:128, :],
                                    p1f(K, dts, 128)[64:128, :], AF.Exp),
             ins=[("mdts",)], outs=[("mdts",)], tag=S + f"dte{tc}")
        p.op("vector",
             lambda e: e.tensor_scalar(
                 out=p1f(K, dts + 128, 128)[64:128, :],
                 in0=p1f(K, dts + 128, 128)[64:128, :],
                 scalar1=1.0, scalar2=None, op0=ALU.add),
             ins=[("mdts",)], outs=[("mdts",)], tag=S + f"dt1{tc}")
        p.op("scalar",
             lambda e: e.activation(p1f(K, dts + 128, 128)[64:128, :],
                                    p1f(K, dts + 128, 128)[64:128, :], AF.Ln),
             ins=[("mdts",)], outs=[("mdts",)], tag=S + f"dtv{tc}")
        p.op("scalar",
             lambda e: e.activation(p1f(K, dts + 256, 128)[64:128, :],
                                    p1f(K, dts + 128, 128)[64:128, :], AF.Ln),
             ins=[("mdts",)], outs=[("mdts",)], tag=S + f"lndt{tc}")
        p.op("vector",
             lambda e: e.tensor_scalar(
                 out=p1f(K, dts + 384, 128)[64:128, :],
                 in0=p1f(K, dts + 128, 128)[64:128, :],
                 scalar1=m["mabd"][64:128, 2:3], scalar2=None, op0=ALU.mult),
             ins=[("mdts",), ("mabd",)], outs=[("mdts",)], tag=S + f"dtA{tc}")
        p.op("vector",
             lambda e: e.tensor_tensor_scan(
                 out=p1f(K, dts, 128)[64:128, :],
                 data0=m["ones_col"][64:128, 0:1].broadcast_to([64, 128]),
                 data1=p1f(K, dts + 384, 128)[64:128, :],
                 initial=0.0, op0=ALU.mult, op1=ALU.add),
             ins=[("mdts",), ("c", "ones_col")], outs=[("mdts",)], tag=S + f"acum{tc}")

        p.op("tensor",
             lambda e, tc=tc: e.matmul(m["pc"][:, 0:128],
                                       p1f(K, MB["mbcs"] + tc * 128, 128)[0:16, :],
                                       p1f(K, MB["mbcs"] + 512 + tc * 128, 128)[0:16, :],
                                       start=True, stop=True),
             ins=[("mbcs",)], outs=[("pc",)], tag=S + f"gt{tc}")
        p.op("scalar",
             lambda e: e.activation(p2b(K, MGT, 128), m["pc"][:, 0:128], AF.Identity),
             ins=[("pc",)], outs=[("p2", 16)], tag=S + f"gte{tc}")
        p.op("tensor",
             lambda e, tc=tc: e.transpose(m["pd"][:, 0:16],
                                          p1f(K, MB["mbcs"] + tc * 128, 128)[0:16, :],
                                          m["ident"][0:16, 0:16]),
             ins=[("mbcs",), ("c", "ident")], outs=[("pd",)], tag=S + f"btm{tc}")
        p.op("scalar",
             lambda e: e.activation(p2b(K, MBTMB, 16), m["pd"][:, 0:16], AF.Identity),
             ins=[("pd",)], outs=[("p2", 16)], tag=S + f"btme{tc}")
        p.op("vector",
             lambda e: e.tensor_sub(p1f(K, dts + 384, 128)[64:128, :],
                                    p1f(K, dts, 128)[64:128, :],
                                    p1f(K, dts + 256, 128)[64:128, :]),
             ins=[("mdts",)], outs=[("mdts",)], tag=S + f"am{tc}")
        p.op("tensor",
             lambda e: e.transpose(m["pd"][:, 0:64], p1f(K, dts + 384, 128)[64:128, :],
                                   m["ident"][64:128, 64:128]),
             ins=[("mdts",), ("c", "ident")], outs=[("pd",)], tag=S + f"amt{tc}")
        p.op("scalar",
             lambda e: e.activation(m["small"][:, 1536:1600], m["pd"][:, 0:64],
                                    AF.Identity, scale=-1.0),
             ins=[("pd",)], outs=[("small", 6)], tag=S + f"amte{tc}")
        p.op("scalar",
             lambda e: e.activation(p1f(K, dts + 256, 128)[64:128, :],
                                    p1f(K, dts, 128)[64:128, :], AF.Exp),
             ins=[("mdts",)], outs=[("mdts",)], tag=S + f"expa{tc}")
        p.op("tensor",
             lambda e: e.transpose(m["pd"][:, 0:64], p1f(K, dts + 256, 128)[64:128, :],
                                   m["ident"][64:128, 64:128]),
             ins=[("mdts",), ("c", "ident")], outs=[("pd",)], tag=S + f"expat{tc}")
        p.op("scalar",
             lambda e, tc=tc: e.activation(m["small"][:, 1792 + tc * 64: 1856 + tc * 64],
                                           m["pd"][:, 0:64], AF.Identity),
             ins=[("pd",)], outs=[("small", 7)], tag=S + f"expate{tc}")
        p.op("scalar",
             lambda e: e.activation(m["small"][64:128, 1664:1665],
                                    p1f(K, dts, 128)[64:128, 127:128], AF.Exp),
             ins=[("mdts",)], outs=[("small", 6)], tag=S + f"dtot{tc}")
        def fdt(e, tc=tc):
            dst = bass.AP(d["agin_m"], tc * 65600 + 65536, [[1, 64], [1, 1]])
            return e.dma_start(out=dst, in_=m["small"][64:128, 1664:1665])
        p.dma("sync", fdt, ins=[("small", 6)], outs=[("aginm", tc, 8)], tag=S + f"dtst{tc}")
        p.op("scalar",
             lambda e: e.activation(p1f(K, dts + 384, 128)[64:128, :],
                                    p1f(K, dts, 128)[64:128, :], AF.Exp,
                                    scale=-1.0,
                                    bias=p1f(K, dts, 128)[64:128, 127:128]),
             ins=[("mdts",)], outs=[("mdts",)], tag=S + f"e2{tc}")
        p.op("vector",
             lambda e: e.tensor_mul(p1f(K, dts + 384, 128)[64:128, :],
                                    p1f(K, dts + 384, 128)[64:128, :],
                                    p1f(K, dts + 128, 128)[64:128, :]),
             ins=[("mdts",)], outs=[("mdts",)], tag=S + f"w2{tc}")
        p.op("tensor",
             lambda e: e.transpose(m["pd"][:, 0:64], p1f(K, dts + 384, 128)[64:128, :],
                                   m["ident"][64:128, 64:128]),
             ins=[("mdts",), ("c", "ident")], outs=[("pd",)], tag=S + f"w2t{tc}")
        p.op("scalar",
             lambda e: e.activation(m["small"][:, 2112:2176], m["pd"][:, 0:64], AF.Identity),
             ins=[("pd",)], outs=[("small", 8)], tag=S + f"w2te{tc}")
        def fxt1(e, tc=tc):
            if tc < 2:
                srcx = m["p1"][:, tc * 4096:(tc + 1) * 4096]
            else:
                srcx = m["p3"][:, (tc - 2) * 4096:(tc - 1) * 4096]
            dst = bass.AP(m.t["wb"], 0, [[8192, 128], [128, 32], [1, 128]])
            return e.dma_start_transpose(dst, srcx)
        p.dma("sync", fxt1,
              ins=([("p1", 2 * tc), ("p1", 2 * tc + 1)] if tc < 2 else
                   [("p3", i) for i in range((tc - 2) * 8, (tc - 1) * 8)]),
              outs=[("wb", 0), ("wb", 1)], tag=S + f"xt1{tc}")
        p.op("vector",
             lambda e: e.tensor_tensor(
                 m["wb"][:, 4096:8192], m["wb"][:, 0:4096],
                 bass.AP(m.t["small"], 2112, [[2304, 128], [1, 64], [0, 64]]),
                 ALU.mult),
             ins=[("wb", 0), ("wb", 1), ("small", 8)],
             outs=[("wb", 2), ("wb", 3)], tag=S + f"xe{tc}")
        grp_y = None
        grp_h = None
        for h in range(64):
            b8 = h % 8
            if b8 == 0:
                grp_y = PsumGroup()
                grp_h = PsumGroup()
            p.op("tensor",
                 lambda e, h=h: e.matmul(m["pb"][0:1, 0:128],
                                         m["ident"][64:128, 64 + h:65 + h],
                                         p1f(K, dts, 128)[64:128, :],
                                         start=True, stop=True),
                 ins=[("mdts",), ("c", "ident")], outs=[("pb",)], tag=S + f"ax{tc}_{h}")
            p.op("scalar",
                 lambda e: e.activation(m["small"][0:1, 2176:2304], m["pb"][0:1, 0:128],
                                        AF.Identity),
                 ins=[("pb",)], outs=[("small", 8)], tag=S + f"axe{tc}_{h}")
            p.op("tensor",
                 lambda e: e.matmul(m["pc"][:, 0:128], m["ones_row"][0:1, :],
                                    m["small"][0:1, 2176:2304], start=True, stop=False),
                 ins=[("small", 8), ("c", "ones_row")], outs=[("pc",)], tag=S + f"pb1{tc}_{h}")
            p.op("tensor",
                 lambda e: e.matmul(m["pc"][:, 0:128], m["ident"][:, :],
                                    m["trineg"][:, :], start=False, stop=True),
                 ins=[("c", "ident"), ("c", "trineg"), ("pc",)], outs=[("pc",)],
                 tag=S + f"pb2{tc}_{h}")
            p.op("scalar",
                 lambda e, h=h: e.activation(p2b(K, MPT, 128), m["pc"][:, 0:128],
                                             AF.Exp, bias=m["small"][:, 1536 + h:1537 + h]),
                 ins=[("pc",), ("small", 6)], outs=[("p2", 16)], tag=S + f"pex{tc}_{h}")
            p.op("vector",
                 lambda e: e.tensor_mul(p2b(K, MPT, 128), p2b(K, MPT, 128),
                                        p2b(K, MGT, 128)),
                 ins=[("p2", 16)], outs=[("p2", 16)], tag=S + f"pg{tc}_{h}")
            fy = grp_y.next_flags(0)
            p.op("tensor",
                 lambda e, h=h, b8=b8, fy=fy:
                 e.matmul(m["pa"][:, b8 * 64:(b8 + 1) * 64], p2b(K, MPT, 128),
                          m["wb"][:, h * 64:(h + 1) * 64],
                          start=fy["start"], stop=fy["stop"]),
                 ins=[("p2", 16), ("wb", 0), ("wb", 1)], outs=[("pa",)],
                 tag=S + f"yd{tc}_{h}")
            fh = grp_h.next_flags(0)
            p.op("tensor",
                 lambda e, h=h, b8=b8, fh=fh:
                 e.matmul(m["pd"][0:16, b8 * 64:(b8 + 1) * 64], p2b(K, MBTMB, 16),
                          m["wb"][:, 4096 + h * 64: 4096 + (h + 1) * 64],
                          start=fh["start"], stop=fh["stop"]),
                 ins=[("p2", 16), ("wb", 2), ("wb", 3)], outs=[("pd",)],
                 tag=S + f"hm{tc}_{h}")
            if b8 == 7:
                grp_y.close()
                grp_h.close()
                bi = h // 8
                p.op("scalar",
                     lambda e, bi=bi, tc=tc: e.activation(
                         m["p2"][:, 16384 + tc * 4096 + bi * 512: 16384 + tc * 4096 + (bi + 1) * 512],
                         m["pa"][:, :], AF.Identity),
                     ins=[("pa",)], outs=[("p2", 32 + tc * 8 + bi)],
                     tag=S + f"yev{tc}_{bi}")
                p.op("vector",
                     lambda e: e.tensor_copy(p1f(K, MB["msta"], 512)[0:16, :],
                                             m["pd"][0:16, :]),
                     ins=[("pd",)], outs=[("msta",)], tag=S + f"hev{tc}_{bi}")
                def fhst(e, tc=tc, bi=bi):
                    src = p1f(K, MB["msta"], 512)[0:16, :]
                    dst = bass.AP(d["agin_m"], tc * 65600 + bi * 512,
                                  [[4096, 16], [1, 512]])
                    return e.dma_start(out=dst, in_=src)
                p.dma("sync", fhst, ins=[("msta",)], outs=[("aginm", tc, bi)],
                      tag=S + f"hst{tc}_{bi}")

    if DBG and g == 0:
        p.dma("sync", lambda e: e.dma_start(out=d["dbgb"].ap(),
                                            in_=m["p2"][:, 16384:32768]),
              ins=[("p2", 32 + i) for i in range(32)], outs=[("dbgt1",)],
              tag=S + "tap1")
        p.dma("gpsimd", lambda e: e.dma_start(out=d["dbg2"].ap(),
                                              in_=m["small"][:, 1536:2048]),
              ins=[("small", 6), ("small", 7), ("small", 8)],
              outs=[("dbgt2",)], tag=S + "tap2")

    agout_m = "agout_mA" if g == 0 else "agout_mB"
    p.collective(
        lambda e, agout_m=agout_m: e.collective_compute(
            "AllGather", ALU.bypass,
            replica_groups=[[0, 1], [2, 3], [4, 5], [6, 7]],
            ins=[d["agin_m"].ap()], outs=[d[agout_m].ap()]),
        ins=[("aginm", tc, bi) for tc in range(4) for bi in list(range(8)) + [8]],
        outs=[(agout_m,)], tag=S + "agm")

    p2f = m.t["p2"].ap().bitcast(F32)
    p.dma("sync", lambda e: e.dma_start(out=p2f[0:16, 0:4096], in_=d["msdram"].ap()),
          ins=[("msdram",)], outs=[("p2", i) for i in range(16)], tag=S + "resT")
    for tc in range(4):
        j = 4 * g + tc
        _mamba_phase2(K, site, g, tc, j, agout_m)
    p.dma("sync", lambda e: e.dma_start(out=d["msdram"].ap(), in_=p2f[0:16, 0:4096]),
          ins=[("p2", i) for i in range(16)], outs=[("msdram",)], tag=S + "savT")

    _mamba_wout(K, site, g)


def _mamba_phase2(K, site, g, tc, j, agout_m):
    p, m, d = K.prog, K.mem, K.dram
    S = f"s{g}p{tc}_"
    p2f = m.t["p2"].ap().bitcast(F32)
    if j > 0:
        for a in (2 * j - 2, 2 * j - 1):
            slab, grp_i, idx = a % 2, a // 8, (a // 2) % 4
            agname = "agout_mA" if grp_i == 0 else "agout_mB"
            def fdk(e, agname=agname, slab=slab, idx=idx):
                src = bass.AP(d[agname], slab * 4 * 65600 + idx * 65600 + 65536,
                              [[64, 1], [1, 64]])
                return e.dma_start(out=m["small"][0:1, 2176:2240], in_=src)
            p.dma("scalar", fdk, ins=[(agname,)], outs=[("small", 8)], tag=S + f"dk{a}")
            p.op("tensor",
                 lambda e: e.matmul(m["pd"][:, 0:64], m["ones_row"][0:1, :],
                                    m["small"][0:1, 2176:2240], start=True, stop=True),
                 ins=[("small", 8), ("c", "ones_row")], outs=[("pd",)], tag=S + f"dkb{a}")
            p.op("vector",
                 lambda e: e.tensor_copy(m["small"][:, 1600:1664], m["pd"][:, 0:64]),
                 ins=[("pd",)], outs=[("small", 6)], tag=S + f"dke{a}")
            def fhl(e, agname=agname, slab=slab, idx=idx):
                src = bass.AP(d[agname], slab * 4 * 65600 + idx * 65600,
                              [[4096, 16], [1, 4096]])
                return e.dma_start(out=p2f[0:16, 4096:8192], in_=src)
            p.dma("scalar", fhl, ins=[(agname,)], outs=[("p2", 16 + i) for i in range(16)],
                  tag=S + f"hl{a}")
            p.op("vector",
                 lambda e: e.tensor_tensor(
                     p2f[0:16, 0:4096], p2f[0:16, 0:4096],
                     bass.AP(m.t["small"], 1600, [[2304, 128], [1, 64], [0, 64]])[0:16, :, :],
                     ALU.mult),
                 ins=[("small", 6)] + [("p2", i) for i in range(16)],
                 outs=[("p2", i) for i in range(16)], tag=S + f"tm{a}")
            p.op("vector",
                 lambda e: e.tensor_add(p2f[0:16, 0:4096], p2f[0:16, 0:4096],
                                        p2f[0:16, 4096:8192]),
                 ins=[("p2", 16 + i) for i in range(16)] + [("p2", i) for i in range(16)],
                 outs=[("p2", i) for i in range(16)], tag=S + f"ta{a}")
    a2 = 2 * j
    slab2, grp2, idx2 = a2 % 2, a2 // 8, (a2 // 2) % 4
    agn2 = "agout_mA" if grp2 == 0 else "agout_mB"
    def fdk2(e, agn2=agn2, slab2=slab2, idx2=idx2):
        src = bass.AP(d[agn2], slab2 * 4 * 65600 + idx2 * 65600 + 65536, [[64, 1], [1, 64]])
        return e.dma_start(out=m["small"][0:1, 2176:2240], in_=src)
    p.dma("scalar", fdk2, ins=[(agn2,)], outs=[("small", 8)], tag=S + "dk2")
    p.op("tensor",
         lambda e: e.matmul(m["pd"][:, 0:64], m["ones_row"][0:1, :],
                            m["small"][0:1, 2176:2240], start=True, stop=True),
         ins=[("small", 8), ("c", "ones_row")], outs=[("pd",)], tag=S + "dkb2")
    p.op("vector", lambda e: e.tensor_copy(m["small"][:, 1600:1664], m["pd"][:, 0:64]),
         ins=[("pd",)], outs=[("small", 6)], tag=S + "dke2")
    def fhl2(e, agn2=agn2, slab2=slab2, idx2=idx2):
        src = bass.AP(d[agn2], slab2 * 4 * 65600 + idx2 * 65600, [[4096, 16], [1, 4096]])
        return e.dma_start(out=p2f[0:16, 4096:8192], in_=src)
    p.dma("scalar", fhl2, ins=[(agn2,)], outs=[("p2", 16 + i) for i in range(16)],
          tag=S + "hl2")
    p.op("vector",
         lambda e, tc=tc: e.tensor_scalar(out=p1f(K, MB["mbcs"] + 1056, 128)[0:16, :],
                                          in0=p1f(K, MB["mbcs"] + 512 + tc * 128, 128)[0:16, :],
                                          scalar1=m["parw_bc"][0:16, 0:1], scalar2=None,
                                          op0=ALU.mult),
         ins=[("mbcs",), ("parw_bc",)], outs=[("mbcs",)], tag=S + "c0")
    p.op("vector",
         lambda e, tc=tc: e.tensor_scalar(out=p1f(K, MB["mbcs"] + 1184, 128)[0:16, :],
                                          in0=p1f(K, MB["mbcs"] + 512 + tc * 128, 128)[0:16, :],
                                          scalar1=m["parw_bc"][0:16, 1:2], scalar2=None,
                                          op0=ALU.mult),
         ins=[("mbcs",), ("parw_bc",)], outs=[("mbcs",)], tag=S + "c1")
    def fsg(e, j=j):
        return e.dma_start(out=m["wb"][:, 4096:8192],
                           in_=d["gdram"].ap()[j:j + 1].squeeze(0))
    p.dma("gpsimd", fsg, ins=[("gdram", j)], outs=[("wb", 2), ("wb", 3)], tag=S + "sg")
    def fxt1(e, tc=tc):
        if tc < 2:
            srcx = m["p1"][:, tc * 4096:(tc + 1) * 4096]
        else:
            srcx = m["p3"][:, (tc - 2) * 4096:(tc - 1) * 4096]
        dst = bass.AP(m.t["wb"], 0, [[8192, 128], [128, 32], [1, 128]])
        return e.dma_start_transpose(dst, srcx)
    p.dma("sync", fxt1,
          ins=([("p1", 2 * tc), ("p1", 2 * tc + 1)] if tc < 2 else
               [("p3", i) for i in range((tc - 2) * 8, (tc - 1) * 8)]),
          outs=[("wb", 0), ("wb", 1)], tag=S + "xt1")
    for sl in range(8):
        p.op("vector",
             lambda e, sl=sl: e.tensor_tensor(
                 p1f(K, MB["msta"], 512)[0:16, :],
                 p2f[0:16, sl * 512:(sl + 1) * 512],
                 bass.AP(m.t["small"], 1600 + sl * 8, [[2304, 128], [1, 8], [0, 64]])[0:16, :, :],
                 ALU.mult),
             ins=[("small", 6)] + [("p2", i) for i in range(16)], outs=[("msta",)],
             tag=S + f"td{sl}")
        grp = PsumGroup()
        f1 = grp.next_flags(0)
        p.op("tensor",
             lambda e, sl=sl, f1=f1: e.matmul(
                 m["pb"][:, :], p1f(K, MB["mbcs"] + 1056, 128)[0:16, :],
                 p2f[0:16, sl * 512:(sl + 1) * 512],
                 start=f1["start"], stop=f1["stop"]),
             ins=[("mbcs",)] + [("p2", i) for i in range(16)], outs=[("pb",)],
             tag=S + f"u0{sl}")
        f2 = grp.next_flags(0)
        p.op("tensor",
             lambda e, sl=sl, f2=f2: e.matmul(
                 m["pb"][:, :], p1f(K, MB["mbcs"] + 1184, 128)[0:16, :],
                 p1f(K, MB["msta"], 512)[0:16, :],
                 start=f2["start"], stop=f2["stop"]),
             ins=[("mbcs",), ("msta",)], outs=[("pb",)], tag=S + f"u1{sl}")
        f3 = grp.next_flags(0)
        p.op("tensor",
             lambda e, sl=sl, f3=f3: e.matmul(
                 m["pb"][:, :], p1f(K, MB["mbcs"] + 1184, 128)[0:16, :],
                 p2f[0:16, 4096 + sl * 512: 4096 + (sl + 1) * 512],
                 start=f3["start"], stop=f3["stop"]),
             ins=[("mbcs",)] + [("p2", 16 + i) for i in range(16)], outs=[("pb",)],
             tag=S + f"u2{sl}")
        grp.close()
        p.op("vector",
             lambda e, sl=sl, tc=tc: e.tensor_tensor(
                 p1f(K, MB["msta"], 512),
                 m["pb"][:, :],
                 bass.AP(m.t["small"], 1792 + tc * 64 + sl * 8,
                         [[2304, 128], [1, 8], [0, 64]]),
                 ALU.mult),
             ins=[("pb",), ("small", 7)], outs=[("msta",)], tag=S + f"ux{sl}")
        p.op("vector",
             lambda e, sl=sl, tc=tc: e.tensor_add(
                 p1f(K, MB["msta"], 512),
                 p1f(K, MB["msta"], 512),
                 m["p2"][:, 16384 + tc * 4096 + sl * 512: 16384 + tc * 4096 + (sl + 1) * 512]),
             ins=[("msta",), ("p2", 32 + tc * 8 + sl)],
             outs=[("msta",)], tag=S + f"ya{sl}")
        p.op("vector",
             lambda e, sl=sl: e.tensor_tensor(
                 p1f(K, MB["mdtr"], 512),
                 m["wb"][:, sl * 512:(sl + 1) * 512],
                 bass.AP(m.t["small"], 2048 + sl * 8, [[2304, 128], [1, 8], [0, 64]]),
                 ALU.mult),
             ins=[("wb", 0), ("wb", 1), ("small", 8)], outs=[("mdtr",)],
             tag=S + f"dx{sl}")
        p.op("vector",
             lambda e, sl=sl: e.tensor_add(p1f(K, MB["msta"], 512),
                                           p1f(K, MB["msta"], 512),
                                           p1f(K, MB["mdtr"], 512)),
             ins=[("msta",), ("mdtr",)], outs=[("msta",)], tag=S + f"yb{sl}")
        p.op("vector",
             lambda e, sl=sl, tc=tc: e.tensor_tensor(
                 m["p2"][:, 16384 + tc * 4096 + sl * 512: 16384 + tc * 4096 + (sl + 1) * 512],
                 p1f(K, MB["msta"], 512),
                 m["wb"][:, 4096 + sl * 512: 4096 + (sl + 1) * 512], ALU.mult),
             ins=[("msta",), ("wb", 2), ("wb", 3)],
             outs=[("p2", 32 + tc * 8 + sl)], tag=S + f"y2{sl}")
    for sl in range(8):
        p.op("vector",
             lambda e, sl=sl, tc=tc: e.bn_stats(
                 m["small"][:, sl * 6: (sl + 1) * 6],
                 m["p2"][:, 16384 + tc * 4096 + sl * 512: 16384 + tc * 4096 + (sl + 1) * 512]),
             ins=[("p2", 32 + tc * 8 + sl)], outs=[("small", 0)],
             tag=S + f"bn{sl}")
    p.op("vector",
         lambda e: e.bn_aggr(m["small"][:, 64:66], m["small"][:, 0:48]),
         ins=[("small", 0)], outs=[("small", 0)], tag=S + "ag")
    p.op("vector",
         lambda e: e.scalar_tensor_tensor(
             out=m["small"][:, 66:67], in0=m["small"][:, 64:65],
             scalar=m["small"][:, 64:65], in1=m["small"][:, 65:66],
             op0=ALU.mult, op1=ALU.add),
         ins=[("small", 0)], outs=[("small", 0)], tag=S + "msq")
    p.op("vector",
         lambda e: e.tensor_scalar(out=m["small"][:, 66:67], in0=m["small"][:, 66:67],
                                   scalar1=float(EPS), scalar2=None, op0=ALU.add),
         ins=[("small", 0)], outs=[("small", 0)], tag=S + "msqe")
    p.op("scalar",
         lambda e: e.activation(m["small"][:, 67:68], m["small"][:, 66:67], AF.Ln),
         ins=[("small", 0)], outs=[("small", 0)], tag=S + "lnm")
    p.op("scalar",
         lambda e: e.activation(m["small"][:, 66:67], m["small"][:, 67:68], AF.Exp,
                                scale=-0.5),
         ins=[("small", 0)], outs=[("small", 0)], tag=S + "rst")
    for sl in range(8):
        p.op("vector",
             lambda e, sl=sl, tc=tc: e.tensor_scalar(
                 out=m["p2"][:, 16384 + tc * 4096 + sl * 512: 16384 + tc * 4096 + (sl + 1) * 512],
                 in0=m["p2"][:, 16384 + tc * 4096 + sl * 512: 16384 + tc * 4096 + (sl + 1) * 512],
                 scalar1=m["small"][:, 66:67], scalar2=None, op0=ALU.mult),
             ins=[("small", 0), ("p2", 32 + tc * 8 + sl)],
             outs=[("p2", 32 + tc * 8 + sl)], tag=S + f"yn{sl}")
    def fyt1(e, tc=tc):
        src = m["p2"][:, 16384 + tc * 4096: 16384 + (tc + 1) * 4096]
        if tc < 2:
            dst = bass.AP(m.t["p1"], tc * 4096, [[16384, 128], [128, 32], [1, 128]])
        else:
            dst = bass.AP(m.t["p3"], (tc - 2) * 4096, [[8192, 128], [128, 32], [1, 128]])
        return e.dma_start_transpose(dst, src)
    p.dma("sync", fyt1, ins=[("p2", 32 + tc * 8 + i) for i in range(8)],
          outs=([("p1", 2 * tc), ("p1", 2 * tc + 1)] if tc < 2 else
                [("p3", i) for i in range((tc - 2) * 8, (tc - 1) * 8)]),
          tag=S + "yt1")


def _mamba_wout(K, site, g):
    p, m, d = K.prog, K.mem, K.dram
    S = f"s{g}w_"
    p2f = m.t["p2"].ap().bitcast(F32)
    for cc in range(16):
        slot = cc % 2
        def fwo(e, cc=cc, slot=slot):
            src = bass.AP(d["wout"], cc * 128, [[2048, 128], [128 * 2048, 32], [1, 128]])
            return e.dma_start(out=m["wb"][:, slot * 4096:(slot + 1) * 4096], in_=src)
        p.dma("gpsimd", fwo, outs=[("wb", 2 * slot), ("wb", 2 * slot + 1)],
              tag=S + f"lwo{cc}")
        p.op("vector",
             lambda e, slot=slot: e.tensor_tensor(
                 m["wb"][:, slot * 4096:(slot + 1) * 4096],
                 m["wb"][:, slot * 4096:(slot + 1) * 4096],
                 bass.AP(m.t["mnw"], 0, [[32, 128], [1, 32], [0, 128]]),
                 ALU.mult),
             ins=[("wb", 2 * slot), ("wb", 2 * slot + 1), ("mnw",)],
             outs=[("wb", 2 * slot), ("wb", 2 * slot + 1)], tag=S + f"nw{cc}")
        ps = "pa" if slot == 0 else "pb"
        grp = PsumGroup()
        for kc in range(32):
            for half in range(2):
                fl = grp.next_flags(0)
                def fmm(e, kc=kc, slot=slot, ps=ps, half=half, fl=fl):
                    if half == 0:
                        rhs = bass.AP(m.t["p1"], kc * 128,
                                      [[16384, 128], [4096, 2], [1, 128]])
                    else:
                        rhs = bass.AP(m.t["p3"], kc * 128,
                                      [[8192, 128], [4096, 2], [1, 128]])
                    return e.matmul(
                        m[ps][:, half * 256:(half + 1) * 256],
                        m["wb"][:, slot * 4096 + kc * 128: slot * 4096 + (kc + 1) * 128],
                        rhs, start=fl["start"], stop=fl["stop"])
                p.op("tensor", fmm,
                     ins=[("wb", 2 * slot), ("wb", 2 * slot + 1)] +
                         ([("p1", 0), ("p1", 1), ("p1", 2), ("p1", 3)] if half == 0 else
                          [("p3", i) for i in range(16)]),
                     outs=[(ps,)], tag=S + f"om{cc}_{kc}_{half}")
        grp.close()
        p.op("scalar",
             lambda e, ps=ps, cc=cc: e.activation(m["p2"][:, cc * 512:(cc + 1) * 512],
                                                  m[ps][:, :], AF.Identity),
             ins=[(ps,)], outs=[("p2", cc)], tag=S + f"oev{cc}")
    for jj in range(4):
        j = 4 * g + jj
        for cb in range(4):
            for ci4 in range(4):
                ci = cb * 4 + ci4
                def ftr(e, ci=ci, ci4=ci4, jj=jj):
                    pdb = m["pd"][:, :].bitcast(BF16)
                    return e.transpose(pdb[:, ci4 * 128:(ci4 + 1) * 128],
                                       m["p2"][:, ci * 512 + jj * 128: ci * 512 + (jj + 1) * 128],
                                       m["identb"][:, :])
                p.op("tensor", ftr, ins=[("p2", ci), ("c", "identb")],
                     outs=[("pd",)], tag=S + f"tr{j}_{cb}_{ci4}")
            p.op("vector",
                 lambda e, j=j, cb=cb: e.tensor_add(
                     xtile(K, j)[:, cb * 512:(cb + 1) * 512],
                     m["pd"][:, :].bitcast(BF16)[:, 0:512],
                     xtile(K, j)[:, cb * 512:(cb + 1) * 512]),
                 ins=[("pd",), ("x", j)], outs=[("x", j)], tag=S + f"res{j}_{cb}")


CHAIN = ('l0ffn', 'l0attn', 'l0ffn2', 'l1ffn', 'l1attn', 'l1ffn2')


EMIT_RANK_REGS = False


def build_body(K):
    load_consts(K)
    if EMIT_RANK_REGS:
        emit_rank_regs(K)
    if not K.chain:
        emit_ln(K)
        emit_hfm(K)
    for site, name in enumerate(K.chain):
        if name in ("l1ffn", "l1ffn2"):
            emit_l1ffn(K, site)
        elif name == "l0attn":
            emit_l0attn(K, site)
        elif name in ("l0ffn", "l0ffn2"):
            emit_l0ffn(K, site)
        elif name == "l1attn":
            emit_l1attn(K, site)
        else:
            raise ValueError(name)
    store_out(K)


def kernel(**inputs):
    in_maps = host_prepare(inputs)
    nc = build(CHAIN)
    res = run_bass_kernel_spmd(nc, in_maps, core_ids=list(range(NCORES)))
    out = np.zeros((NSEQ * T, D), np.float32)
    for c in range(NCORES):
        out[_interleave_index(c)] = res.results[c]["out"]
    return out.reshape(NSEQ, T, D)


if __name__ == "__main__":
    import reference as _r
    ins = _r.setup_inputs(0)
    o = kernel(**{k: np.asarray(v) for k, v in ins.items()})
    print("ran", o.shape, o.dtype)
